# version 10
# baseline (speedup 1.0000x reference)
import contextlib
import numpy as np
import ml_dtypes
import concourse.bass as bass
import concourse.mybir as mybir
from concourse.bass_utils import run_bass_kernel_spmd

F32 = mybir.dt.float32
BF16 = mybir.dt.bfloat16
AF = mybir.ActivationFunctionType
ALU = mybir.AluOpType
AX = mybir.AxisListType

NCORES = 8
LN_EPS = 1e-5
RMS_EPS = 1.1920929e-7


class Cfg:
    def __init__(self, seqs=2, T=2048, D=2048, HQ=32, HKV=8, QB=1024, TWIN=512):
        self.SEQS, self.T, self.D, self.HQ, self.HKV = seqs, T, D, HQ, HKV
        self.HD = 64
        self.G = HQ // HKV
        self.DKV = HKV * self.HD
        self.Hh = 4 * D
        self.H2 = 8 * D
        self.QB, self.TWIN = QB, TWIN
        assert D == HQ * self.HD and T % QB == 0 and QB % 512 == 0
        self.DC = D // 128
        self.TC = T // 128
        self.KT = T // 128
        self.KVC = self.DKV // 128
        self.NQB = T // QB
        self.NTOK = seqs * T
        self.NWIN = self.NTOK // TWIN
        self.HC = self.Hh // 128
        lo = [h for h in range(HQ) if (h // self.G) % 2 == 0]
        hi = [h for h in range(HQ) if (h // self.G) % 2 == 1]
        self.pairs = list(zip(lo, hi))


def host_consts(c: Cfg):
    t = np.arange(c.T, dtype=np.float32)
    inv_freq = (1.0 / 10000.0 ** (np.arange(0, c.HD, 2, dtype=np.float32) / c.HD))
    ang = t[:, None].astype(np.float64) * inv_freq[None, :].astype(np.float64)
    cos = np.cos(ang).astype(np.float32)
    sin = np.sin(ang).astype(np.float32)
    return {
        "cos_tab": np.ascontiguousarray(np.concatenate([cos, cos], 1)),
        "sins_tab": np.ascontiguousarray(
            np.concatenate([-sin, sin], 1)).astype(np.float32),
        "ident": np.eye(128, dtype=ml_dtypes.bfloat16),
        "identf": np.eye(128, dtype=np.float32),
        "ones_col": np.ones((128, 1), dtype=ml_dtypes.bfloat16),
        "gmat": np.stack([(np.arange(c.HQ) // c.G == kv).astype(np.float32)
                          for kv in range(c.HKV)]),
    }


class Builder:
    ENG = {"t": "tensor", "v": "vector", "s": "scalar", "g": "gpsimd", "q": "sync"}

    def __init__(self, c: Cfg):
        self.c = c
        self.nc = bass.Bass("TRN2", target_bir_lowering=False)
        self.cnt = {}
        self.waited = {}
        self.pend = {k: [] for k in self.ENG}
        self.sems = {}
        self.bar = ()

    def sem(self, name):
        s = self.es.enter_context(self.nc.semaphore(name))
        self.sems[id(s)] = name
        self.cnt[name] = 0
        return s

    def op(self, eng, fn, waits=(), inc=None, incn=1):
        tok = None
        flat = []
        for w in waits:
            if w is None:
                continue
            if isinstance(w, list):
                flat.extend(x for x in w if x is not None)
            else:
                flat.append(w)
        waits = tuple(flat)
        if inc is not None:
            nm = self.sems[id(inc)]
            self.cnt[nm] += incn
            tok = (inc, self.cnt[nm])
        self.pend[eng].append((fn, waits, inc, incn))
        return tok

    def dma(self, eng, dst, src, sem, waits=()):
        return self.op(eng, lambda h, d=dst, s=src: h.dma_start(d, s),
                       waits=waits, inc=sem, incn=16)

    def tok(self, semname):
        return (getattr(self, semname), self.cnt[semname])

    def barrier(self):
        self.bar = tuple(self.tok(nm) for nm in self.cnt)

    def flush(self):
        if not any(self.pend.values()):
            return
        with self.nc.Block() as blk:
            for ename, attr in self.ENG.items():
                items = self.pend[ename]
                if not items:
                    continue
                self.pend[ename] = []

                def body(h, items=items, ename=ename):
                    for fn, waits, inc, incn in items:
                        for (sm, v) in waits:
                            if v <= 0:
                                continue
                            key = (ename, self.sems[id(sm)])
                            if self.waited.get(key, 0) >= v:
                                continue
                            h.wait_ge(sm, v)
                            self.waited[key] = v
                        instr = fn(h)
                        if inc is not None:
                            instr.then_inc(inc, incn)
                getattr(blk, attr)(body)

    def build(self):
        c, nc = self.c, self.nc
        self.es = contextlib.ExitStack()
        d = lambda n, s, ty, k: nc.dram_tensor(n, s, ty, kind=k).ap()
        self.x_in = d("x", [c.SEQS, c.T, c.D], F32, "ExternalInput")
        self.Wq = d("Wq", [c.D, c.D], F32, "ExternalInput")
        self.bq = d("bq", [c.D], F32, "ExternalInput")
        self.Wk = d("Wk", [c.DKV, c.D], F32, "ExternalInput")
        self.bk = d("bk", [c.DKV], F32, "ExternalInput")
        self.Wv = d("Wv", [c.DKV, c.D], F32, "ExternalInput")
        self.bv = d("bv", [c.DKV], F32, "ExternalInput")
        self.Wo = d("Wo", [c.D, c.D], F32, "ExternalInput")
        self.bo = d("bo", [c.D], F32, "ExternalInput")
        self.Wgu = d("Wgu", [c.H2, c.D], F32, "ExternalInput")
        self.Wp = d("Wp", [c.D, c.Hh], F32, "ExternalInput")
        self.cos_in = d("cos_tab", [c.T, c.HD], F32, "ExternalInput")
        self.sin_in = d("sins_tab", [c.T, c.HD], F32, "ExternalInput")
        self.ident_in = d("identb_in", [128, 128], BF16, "ExternalInput")
        self.identf_in = d("identf_in", [128, 128], F32, "ExternalInput")
        self.onescol_in = d("onescol_in", [128, 1], BF16, "ExternalInput")
        self.gmat_in = d("gmat_in", [c.HKV, c.HQ], F32, "ExternalInput")
        self.out = d("out", [c.SEQS, c.T, c.D], F32, "ExternalOutput")
        self.scr_gu = d("scr_gu", [c.D, c.H2], BF16, "Internal")
        self.scr_p = d("scr_p", [c.Hh, c.D], BF16, "Internal")
        self.scr_x1tm = d("scr_x1tm", [c.NTOK, c.D], F32, "Internal")
        self.scr_x1fm = d("scr_x1fm", [c.D, c.NTOK], BF16, "Internal")
        self.scr_sum = d("scr_sum", [2, c.QB], F32, "Internal")
        self.scr_rcp = d("scr_rcp", [2, c.QB], F32, "Internal")
        with self.es:
            self._alloc()
            self._program()
        return nc

    def _alloc(self):
        c, nc, es = self.c, self.nc, self.es
        sb = lambda n, s, ty: es.enter_context(nc.sbuf_tensor(n, s, ty))
        pt = lambda n, s, ty: es.enter_context(nc.psum_tensor(n, s, ty))
        need_attn = (c.KVC * c.T // 2 + (c.TC * c.HKV * 65 + 1) // 2
                     + max(2048, c.DC * 128)
                     + max(c.DC * c.DKV // 2, c.DC * 256) + 1024)
        self.N1 = max(need_attn, 14336)
        self.ar1 = sb("ar1", [128, self.N1], F32)
        self.N2 = max(c.DC * c.QB, 12800)
        self.ar2 = sb("ar2", [128, self.N2], F32)
        self.ar4 = sb("ar4", [128, 4096], F32)
        self.ar5 = sb("ar5", [128, 2 * 2048], F32)
        self.ar6 = sb("ar6", [128, 2048], F32)
        self.ar7 = sb("ar7", [128, 3072], F32)
        self.ar8 = sb("ar8", [128, 2560], F32)
        self.ident = sb("ident", [128, 128], BF16)
        self.identf = sb("identf", [128, 128], F32)
        self.ones_col = sb("ones_col", [128, 1], BF16)
        self.gmat = sb("gmat", [c.HKV, c.HQ], F32)
        self.ones3 = sb("ones3", [65, 128], F32)
        self.brow = sb("brow", [65, max(2 * c.DKV, c.D)], F32)
        self.bn6 = sb("bn6", [128, 24], F32)
        self.colA = sb("colA", [128, 8], F32)
        self.colB = sb("colB", [128, 8], F32)
        self.normQ = sb("normQ", [128, (c.QB // 128) * c.HQ], F32)
        self.normK = sb("normK", [128, c.TC * c.HKV], F32)
        self.biasmat = sb("biasmat", [128, c.HQ], F32)
        self.kmaxc = sb("kmaxc", [c.HKV, 2], F32)
        self.hqc = sb("hqc", [c.HQ, 8], F32)
        self.rcol = sb("rcol", [128, 48], F32)
        self.rowz = self.ar8[0:1, 0:1024]
        self.rowz_f = self.ar7[0:1, 512:1024]
        self.PS1 = pt("PS1", [128, 2048], F32)
        self.PS2 = pt("PS2", [128, 2048], F32)
        for nm in ("ct", "cv", "cs", "cg", "d_cst", "d_st0", "d_st1", "d_ws",
                   "d_ws1", "d_x1", "d_x1b", "d_xr", "d_su0", "d_su1", "d_rb0", "d_rb1",
                   "d_xw", "d_sg0", "d_sg1", "d_sp0", "d_sp1", "d_sp2", "d_out"):
            setattr(self, nm, self.sem(nm))
        self.st_tok = [None, None]
        self.lc_g = [None, None]
        self.ps1q = [None, None, None, None]
        self.trg = [None, None]
        self._treg = 0
        self._stsl = 0

    def bf(self, ar, off, n, p0=0, p1=128):
        return ar[p0:p1, off:off + n].bitcast(BF16)

    def ps1(self, i):
        return self.PS1[:, i * 512:(i + 1) * 512]

    def transpose_group(self, tiles, dst, waits=(), f32=False, evac=None):
        r = self._treg = 1 - self._treg
        nt = len(tiles)
        if f32:
            assert nt <= 4
            ps = self.PS2[:, r * 512:r * 512 + nt * 128]
            ident = self.identf
        else:
            assert nt <= 8
            ps = self.PS2[:, r * 512:(r + 1) * 512].bitcast(BF16)[:, 0:nt * 128]
            ident = self.ident
        tk = None
        w0 = tuple(waits) + (self.trg[r],) + self.bar
        for j, s in enumerate(tiles):
            o = ps[:, j * 128:(j + 1) * 128]
            tk = self.op("t", lambda h, o=o, i=s, idn=ident:
                         h.transpose(o, i, idn[:]),
                         waits=w0 if j == 0 else (), inc=self.ct)
        if evac is None:
            pv = ps.rearrange("p (n f) -> p n f", f=128)
            ev = self.op("v", lambda h, d=dst, s=pv: h.tensor_copy(d, s),
                         waits=(tk,), inc=self.cv)
        else:
            ev = evac(ps, tk)
        self.trg[r] = ev
        return ev

    def load_cast(self, src, n, waits=(), eng="q", dst_off=None):
        sl = self._stsl = 1 - self._stsl
        st = [self.d_st0, self.d_st1][sl]
        stage = self.ar6[:, sl * 1024: sl * 1024 + n]
        ld = self.dma(eng, stage, src, st, waits=(self.st_tok[sl],) + self.bar)
        off = sl * 512 if dst_off is None else dst_off
        dst = self.bf(self.ar7, off, n // 2)
        gate = self.lc_g[sl] if dst_off is None else None
        cs = self.op("s", lambda h, d=dst, s=stage: h.copy(d, s),
                     waits=(ld, gate) + tuple(waits) + self.bar, inc=self.cs)
        self.st_tok[sl] = cs
        self._lc_sl = sl
        return dst, cs

    def build_wT(self, W, wT, row0, ncols):
        c = self.c
        ev = None
        for rch in range(ncols // 128):
            for hf in range((c.D + 1023) // 1024):
                w = min(1024, c.D - hf * 1024)
                src = W[row0 + rch * 128: row0 + rch * 128 + 128,
                        hf * 1024: hf * 1024 + w]
                wbf, cs = self.load_cast(src, w)
                sl = self._lc_sl
                tiles = [wbf[:, j * 128:(j + 1) * 128] for j in range(w // 128)]
                dst = wT[:, hf * 8: hf * 8 + w // 128, rch * 128: rch * 128 + 128]
                ev = self.transpose_group(tiles, dst, waits=(cs,))
                self.lc_g[sl] = ev
        return ev

    def _program(self):
        self._consts()
        self.flush()
        self.barrier()
        self._quant()
        self.ln_done = {}
        for seq in range(self.c.SEQS):
            self._attn_seq(seq)
        self.flush()
        self.barrier()
        self.quant_drain()
        self._ffn()
        self._finish()
        self.flush()

    def _consts(self):
        c = self.c
        t = lambda d_, s_: self.dma("q", d_, s_, self.d_cst)
        t(self.ident[:], self.ident_in)
        t(self.identf[:], self.identf_in)
        t(self.ones_col[:], self.onescol_in)
        t(self.gmat[:], self.gmat_in)
        self.o_cos, self.o_sin = 0, c.TC * c.HD
        t(self.ar4[:, self.o_cos:self.o_cos + c.TC * c.HD]
          .rearrange("p (tc f) -> p tc f", f=c.HD),
          self.cos_in.rearrange("(tc p) f -> p tc f", p=128))
        t(self.ar4[:, self.o_sin:self.o_sin + c.TC * c.HD]
          .rearrange("p (tc f) -> p tc f", f=c.HD),
          self.sin_in.rearrange("(tc p) f -> p tc f", p=128))
        t(self.brow[0:1, 0:c.D], self.bq.rearrange("(o d) -> o d", o=1))
        t(self.brow[32:33, 0:c.D], self.bo.rearrange("(o d) -> o d", o=1))
        t(self.brow[64:65, 0:c.DKV], self.bk.rearrange("(o d) -> o d", o=1))
        t(self.brow[64:65, c.DKV:2 * c.DKV], self.bv.rearrange("(o d) -> o d", o=1))
        self.tok_consts = self.tok("d_cst")
        self.tok_ones = self.op("g", lambda h: h.memset(self.ones3[:], 1.0),
                                inc=self.cg)

    def quant_step(self, n=1):
        g = getattr(self, "_qgen", None)
        if g is None:
            return
        for _ in range(n):
            try:
                next(g)
            except StopIteration:
                self._qgen = None
                self.tok_quant = self.tok("d_ws")
                self.tok_quant1 = self.tok("d_ws1")
                return

    def quant_drain(self):
        while getattr(self, "_qgen", None) is not None:
            self.quant_step(8)

    def _quant(self):
        self._qgen = self._quant_gen()
        self.quant_step(5)

    def _quant_gen(self):
        c = self.c
        WBO = [1024, 1536]
        RQ = [2048, 2560]
        STG = [2048, 2560]
        SC = 2048
        cons = [None, None]
        wsd = [None, None]

        def piece(idx, src, dst_fn, n):
            sl = idx % 2
            g = n // 64
            wb, tcast = self.load_cast(src, n, waits=(cons[sl],),
                                       dst_off=WBO[sl])
            wbv = wb.rearrange("p (g k) -> p g k", k=64)
            sums = self.ar8[:, SC + sl * 256: SC + sl * 256 + g]
            sclbf = self.bf(self.ar8, SC + sl * 256 + 32, 8)[:, 0:g]
            rcp = self.ar8[:, SC + sl * 256 + 48: SC + sl * 256 + 48 + g]
            t1 = self.op("v", lambda h, o=sums, i=wbv: h.reduce_sum(
                o.rearrange("p (g o2) -> p g o2", o2=1), i, axis=AX.X,
                apply_absolute_value=True), waits=(tcast,), inc=self.cv)
            t2 = self.op("g", lambda h, o=sclbf, i=sums: h.tensor_scalar(
                o, i, 1.0 / 64, 1e-8, ALU.mult, ALU.max), waits=(t1,), inc=self.cg)
            t3 = self.op("v", lambda h, o=rcp, i=sclbf: h.reciprocal(o, i),
                         waits=(t2,), inc=self.cv)
            r = self.bf(self.ar7, RQ[sl], n // 2)
            rv = r.rearrange("p (g k) -> p g k", k=64)
            rcpb = bass.AP(rcp.tensor, rcp.offset,
                           [list(rcp.ap[0]), [1, g], [0, 64]])
            t4 = self.op("g", lambda h, o=rv, a=wbv, b=rcpb:
                         h.tensor_tensor(o, a, b, ALU.mult),
                         waits=(t3, wsd[sl]) + self.bar, inc=self.cg)
            t5 = self.op("g", lambda h, o=r, i=r: h.tensor_scalar(
                o, i, 192.0, None, ALU.add), waits=(t4,), inc=self.cg)
            t6 = self.op("g", lambda h, o=r, i=r: h.tensor_scalar(
                o, i, -192.0, 1.0, ALU.add, ALU.min), waits=(t5,), inc=self.cg)
            t7 = self.op("g", lambda h, o=r, i=r: h.tensor_scalar(
                o, i, -1.0, None, ALU.max), waits=(t6,), inc=self.cg)
            sclbb = bass.AP(sclbf.tensor, sclbf.offset,
                            [list(sclbf.ap[0]), [1, g], [0, 64]])
            t8 = self.op("g", lambda h, o=wbv, a=rv, b=sclbb:
                         h.tensor_tensor(o, a, b, ALU.mult), waits=(t7,), inc=self.cg)

            def stage2(sl=sl, wb=wb, t8=t8, dst_fn=dst_fn, n=n):
                nt = n // 128
                stg = self.bf(self.ar7, STG[sl], nt * 64)
                stgv = stg.rearrange("p (t f) -> p t f", f=128)
                ev = self.transpose_group(
                    [wb[:, j * 128:(j + 1) * 128] for j in range(nt)], stgv,
                    waits=(t8, wsd[sl]))
                cons[sl] = ev
                self.lc_g[sl] = ev
                last = None
                wsem = [self.d_ws, self.d_ws1][sl]
                for j4 in range(0, nt, 4):
                    k = min(4, nt - j4)
                    last = self.dma("g" if sl else "q", dst_fn(j4, k),
                                    stgv[:, j4:j4 + k, :], wsem, waits=(ev,))
                wsd[sl] = last
            return stage2

        idx = 0
        wcol = min(1024, c.D)
        for rch in range(c.H2 // 128):
            for jc in range(c.D // wcol):
                def dgu(j4, k, rch=rch, jc=jc):
                    r0 = jc * wcol + j4 * 128
                    return self.scr_gu[r0:r0 + k * 128, rch * 128:(rch + 1) * 128] \
                        .rearrange("(n p) f -> p n f", p=128)
                st2 = piece(idx, self.Wgu[rch * 128:(rch + 1) * 128,
                                          jc * wcol:(jc + 1) * wcol], dgu, wcol)
                idx += 1
                yield
                st2()
                yield
        wcol2 = min(1024, c.Hh)
        for rch in range(c.D // 128):
            for jc in range(c.Hh // wcol2):
                def dp(j4, k, rch=rch, jc=jc):
                    r0 = jc * wcol2 + j4 * 128
                    return self.scr_p[r0:r0 + k * 128, rch * 128:(rch + 1) * 128] \
                        .rearrange("(n p) f -> p n f", p=128)
                st2 = piece(idx, self.Wp[rch * 128:(rch + 1) * 128,
                                         jc * wcol2:(jc + 1) * wcol2], dp, wcol2)
                idx += 1
                yield
                st2()
                yield

    def _attn_seq(self, seq):
        c = self.c
        self.o_kfm = 0
        self.o_vh = c.KVC * c.T // 2
        self.o_xw = self.o_vh + (c.TC * c.HKV * 65 + 1) // 2
        self.o_wt = self.o_xw + max(2048, c.DC * 128)
        self.o_tm = self.o_wt + max(c.DC * c.DKV // 2, c.DC * 256)
        kfm = self.bf(self.ar1, self.o_kfm, c.KVC * c.T // 2) \
            .rearrange("p (kc t) -> p kc t", t=c.T)
        nvh = c.TC * c.HKV * 65
        vh = self.bf(self.ar1, self.o_vh, (nvh + 1) // 2)[:, 0:nvh] \
            .rearrange("p (tc kv f) -> p tc kv f", kv=c.HKV, f=65)
        self.flush()
        self.barrier()
        self._s1kv(seq, kfm, vh)
        self.flush()
        for qb in range(c.NQB):
            qfm = self.bf(self.ar2, 0, c.DC * c.QB // 2) \
                .rearrange("p (dc t) -> p dc t", t=c.QB)
            ofm = self.bf(self.ar2, c.DC * c.QB // 2, c.DC * c.QB // 2) \
                .rearrange("p (dc t) -> p dc t", t=c.QB)
            self.barrier()
            self._s1q(seq, qb, qfm)
            self.flush()
            self.barrier()
            self._biasmat()
            self.flush()
            self.barrier()
            self._s2(seq, qb, qfm, kfm, vh, ofm)
            self.flush()
            self.barrier()
            self._s3(seq, qb, ofm)
            self.flush()

    def _xtile(self, seq, tc, off, gate=None):
        c = self.c
        dst = self.bf(self.ar1, off, c.DC * 64) \
            .rearrange("p (dc f) -> p dc f", f=128)
        ev = None
        for hf in range((c.D + 1023) // 1024):
            w = min(1024, c.D - hf * 1024)
            src = self.x_in[seq, tc * 128:(tc + 1) * 128, hf * 1024:hf * 1024 + w]
            xbf, cs = self.load_cast(src, w, waits=(gate,))
            sl = self._lc_sl
            tiles = [xbf[:, j * 128:(j + 1) * 128] for j in range(w // 128)]
            ev = self.transpose_group(tiles, dst[:, hf * 8: hf * 8 + w // 128, :],
                                      waits=(cs,))
            self.lc_g[sl] = ev
        return dst, ev

    def _rope(self, ps, dstview, tc, n, waits):
        par = list(self.ar4[:].ap[0])
        cosb = bass.AP(self.ar4[:].tensor, self.ar4[:, self.o_cos + tc * 64].offset,
                       [par, [0, n], [1, 64]])
        sinb = bass.AP(self.ar4[:].tensor, self.ar4[:, self.o_sin + tc * 64].offset,
                       [par, [0, n], [32, 2], [1, 32]])
        psv = ps.rearrange("p (h f) -> p h f", f=64)
        psw = bass.AP(ps.tensor, ps.offset + 32,
                      [list(ps.ap[0]), [64, n], [-32, 2], [1, 32]])
        ms = self._mrot = 1 - getattr(self, "_mrot", 0)
        m1 = self.ar8[:, ms * 1024:ms * 1024 + n * 64]
        m2 = self.ar8[:, ms * 1024 + 512:ms * 1024 + 512 + n * 64]
        mg = getattr(self, "_rope_gates", [None, None])
        a = self.op("v", lambda h: h.tensor_tensor(
            m1.rearrange("p (h f) -> p h f", f=64), psv, cosb, ALU.mult),
            waits=tuple(waits) + (mg[ms],) + self.bar, inc=self.cv)
        b = self.op("v", lambda h: h.tensor_tensor(
            m2.rearrange("p (h two f) -> p h two f", two=2, f=32), psw, sinb,
            ALU.mult), inc=self.cv)
        e = self.op("v", lambda h: h.tensor_tensor(
            dstview, m1.rearrange("p (h f) -> p h f", f=64),
            m2.rearrange("p (h f) -> p h f", f=64), ALU.add),
            waits=(a, b), inc=self.cv)
        mg[ms] = e
        self._rope_gates = mg
        return e

    def _s1kv(self, seq, kfm, vh):
        c = self.c
        vones = self.op("g", lambda h: h.memset(
            bass.AP(vh.tensor, vh[:, 0, 0, 64:65].offset,
                    [list(vh.ap[0]), [65, c.TC * c.HKV], [1, 1]]), 1.0),
            waits=(self.tok_consts,) + self.bar, inc=self.cg)
        wkT = self.bf(self.ar1, self.o_wt, c.DC * c.DKV // 2) \
            .rearrange("p (dc f) -> p dc f", f=c.DKV)
        wktok = self.build_wT(self.Wk, wkT, 0, c.DKV)
        wvT = self.bf(self.ar2, 0, c.DC * c.DKV // 2) \
            .rearrange("p (dc f) -> p dc f", f=c.DKV)
        wvtok = self.build_wT(self.Wv, wvT, 0, c.DKV)
        self._ktm_g = [None, None]
        for tc in range(c.TC):
            sl = tc % 2
            xt, xtok = self._xtile(seq, tc, self.o_xw + sl * 1024,
                                   gate=getattr(self, "_xt_g", [None, None])[sl])
            ps = self.PS1[:, sl * 1024: sl * 1024 + c.DKV]
            psv = self.PS1[:, sl * 1024 + c.DKV: sl * 1024 + 2 * c.DKV]
            for dc in range(c.DC):
                self.op("t", lambda h, o=ps, l=xt[:, dc, :], r=wkT[:, dc, :],
                        st=(dc == 0):
                        h.matmul(o, l, r, start=st, stop=False,
                                 skip_group_check=True),
                        waits=(xtok, wktok, wvtok, self.ps1q[2 * sl],
                               self.ps1q[2 * sl + 1])
                        + self.bar if dc == 0 else ())
            mm = self.op("t", lambda h, o=ps: h.matmul(
                o, self.ones3[64:65, :], self.brow[64:65, 0:c.DKV],
                start=False, stop=True, skip_group_check=True),
                waits=(self.tok_consts, self.tok_ones), inc=self.ct)
            for dc in range(c.DC):
                self.op("t", lambda h, o=psv, l=xt[:, dc, :], r=wvT[:, dc, :],
                        st=(dc == 0):
                        h.matmul(o, l, r, start=st, stop=False,
                                 skip_group_check=True))
            mmv = self.op("t", lambda h, o=psv: h.matmul(
                o, self.ones3[64:65, :], self.brow[64:65, c.DKV:2 * c.DKV],
                start=False, stop=True, skip_group_check=True), inc=self.ct)
            xg = getattr(self, "_xt_g", [None, None])
            xg[sl] = mmv
            self._xt_g = xg
            sq = self.ar5[:, 0:c.DKV]
            s1 = self.op("s", lambda h, o=sq, i=ps: h.activation(o, i, AF.Square),
                         waits=(mm, getattr(self, "_sq_g", None)), inc=self.cs)
            nk = self.normK[:, tc * c.HKV:(tc + 1) * c.HKV]
            self._sq_g = self.op(
                "v", lambda h, o=nk, i=sq.rearrange("p (kv f) -> p kv f", f=64):
                h.reduce_sum(o.rearrange("p (kv o2) -> p kv o2", o2=1), i,
                             axis=AX.X), waits=(s1,), inc=self.cv)
            ktm = self.bf(self.ar1, self.o_tm + sl * 512, c.DKV // 2)
            rt = self._rope(ps, ktm.rearrange("p (h f) -> p h f", f=64),
                            tc, c.HKV, (mm, s1, self._ktm_g[sl]))
            self.ps1q[2 * sl] = rt
            ev = self.transpose_group(
                [ktm[:, j * 128:(j + 1) * 128] for j in range(c.KVC)],
                bass.AP(kfm.tensor, kfm[:, 0, tc * 128].offset,
                        [list(kfm.ap[0]), [c.T, c.KVC], [1, 128]]),
                waits=(rt,))
            self._ktm_g[sl] = ev
            dstv = bass.AP(vh.tensor, vh[:, tc, 0, 0].offset,
                           [list(vh.ap[0]), [65, c.HKV], [1, 64]])
            evv = self.op("v", lambda h, o=dstv,
                          i=psv.rearrange("p (kv f) -> p kv f", f=64):
                          h.tensor_copy(o, i), waits=(mmv, vones), inc=self.cv)
            self.ps1q[2 * sl + 1] = evv
            self.quant_step(10)
        self.vh_tok = self.tok("cv")
        nkv = bass.AP(self.normK[:].tensor, self.normK[:].offset,
                      [list(self.normK[:].ap[0]), [1, c.HKV], [c.HKV, c.TC]])
        t1 = self.op("v", lambda h: h.reduce_max(
            self.colA[:, 0:c.HKV].rearrange("p (kv o2) -> p kv o2", o2=1), nkv,
            axis=AX.X), waits=(self._sq_g,), inc=self.cv)
        t2 = self.op("t", lambda h: h.transpose(
            self.PS2[0:c.HKV, 1024:1152], self.colA[:, 0:c.HKV], self.identf[:]),
            waits=(t1,), inc=self.ct)
        t3 = self.op("v", lambda h: h.reduce_max(
            self.kmaxc[:, 0:1].rearrange("p (o o2) -> p o o2", o2=1),
            self.PS2[0:c.HKV, 1024:1152].rearrange("p (o f) -> p o f", o=1),
            axis=AX.X), waits=(t2,), inc=self.cv)
        self.kmax_tok = t3

    def _s1q(self, seq, qb, qfm):
        c = self.c
        qf_tok = None
        self._qtm_g = [None, None, None, None]
        NQT = c.QB // 128
        xqb = self.bf(self.ar2, c.DC * c.QB // 2, c.DC * c.QB // 2) \
            .rearrange("p (dc t) -> p dc t", t=c.QB)
        xtoks = []
        for tq in range(NQT):
            tcg = qb * NQT + tq
            dst = xqb[:, :, tq * 128:(tq + 1) * 128]
            xtok = None
            for hf in range((c.D + 1023) // 1024):
                w = min(1024, c.D - hf * 1024)
                src = self.x_in[seq, tcg * 128:(tcg + 1) * 128,
                                hf * 1024:hf * 1024 + w]
                xbf, cs = self.load_cast(src, w)
                sl_ = self._lc_sl
                xtok = self.transpose_group(
                    [xbf[:, j * 128:(j + 1) * 128] for j in range(w // 128)],
                    dst[:, hf * 8:hf * 8 + w // 128, :], waits=(cs,))
                self.lc_g[sl_] = xtok
            xtoks.append(xtok)
            self.quant_step(3)
        for d8p in range(c.HQ // 8):
            wq8 = self.bf(self.ar1, self.o_wt, c.DC * 256) \
                .rearrange("p (dc f) -> p dc f", f=512)
            wtok = self.build_wT(self.Wq, wq8, d8p * 512, 512)
            for tq in range(NQT):
                tcg = qb * NQT + tq
                sl = tq % 4
                ps = self.ps1(sl)
                for dc in range(c.DC):
                    self.op("t", lambda h, o=ps,
                            l=xqb[:, dc, tq * 128:(tq + 1) * 128],
                            r=wq8[:, dc, :], st=(dc == 0):
                            h.matmul(o, l, r, start=st, stop=False,
                                     skip_group_check=True),
                            waits=(xtoks[tq], wtok, self.ps1q[sl])
                            if dc == 0 else ())
                mm = self.op("t", lambda h, o=ps, d0=d8p * 512: h.matmul(
                    o, self.ones3[0:1, :], self.brow[0:1, d0:d0 + 512],
                    start=False, stop=True, skip_group_check=True), inc=self.ct)
                sq = self.ar5[:, 0:512]
                s1 = self.op("s", lambda h, o=sq, i=ps: h.activation(
                    o, i, AF.Square),
                    waits=(mm, getattr(self, "_sq_g", None)), inc=self.cs)
                nq = self.normQ[:, tq * c.HQ + d8p * 8: tq * c.HQ + d8p * 8 + 8]
                self._sq_g = self.op(
                    "v", lambda h, o=nq,
                    i=sq.rearrange("p (g f) -> p g f", f=64):
                    h.reduce_sum(o.rearrange("p (g o2) -> p g o2", o2=1), i,
                                 axis=AX.X), waits=(s1,), inc=self.cv)
                qt_b = self.bf(self.ar1, self.o_tm + sl * 256, 256)
                dstv = bass.AP(qt_b.tensor, qt_b.offset,
                               [list(qt_b.ap[0]), [64, 2], [128, 4], [1, 64]])
                rt = self._rope(ps, dstv, tcg, 8,
                                (mm, s1, self._qtm_g[sl]))
                self.ps1q[sl] = rt
                ev = self.transpose_group(
                    [qt_b[:, j * 128:(j + 1) * 128] for j in range(4)],
                    qfm[:, 4 * d8p:4 * d8p + 4, tq * 128:(tq + 1) * 128],
                    waits=(rt,))
                self._qtm_g[sl] = ev
                qf_tok = ev
                self.quant_step(3)
        self.q_ready = qf_tok

    def _biasmat(self):
        c = self.c
        NQT = c.QB // 128
        nqv = bass.AP(self.normQ[:].tensor, self.normQ[:].offset,
                      [list(self.normQ[:].ap[0]), [1, c.HQ], [c.HQ, NQT]])
        qmax_s = self.ar8[:, 512:512 + c.HQ]
        t1 = self.op("v", lambda h: h.reduce_max(
            qmax_s.rearrange("p (hq o2) -> p hq o2", o2=1), nqv, axis=AX.X),
            waits=self.bar, inc=self.cv)
        t2 = self.op("t", lambda h: h.transpose(
            self.PS2[0:c.HQ, 1024:1152], qmax_s, self.identf[:]),
            waits=(t1,) + self.bar, inc=self.ct)
        t3 = self.op("v", lambda h: h.reduce_max(
            self.hqc[:, 0:1].rearrange("p (o o2) -> p o o2", o2=1),
            self.PS2[0:c.HQ, 1024:1152].rearrange("p (o f) -> p o f", o=1),
            axis=AX.X), waits=(t2,), inc=self.cv)
        t4 = self.op("t", lambda h: h.matmul(
            self.PS2[0:c.HQ, 1152:1153], self.gmat[:], self.kmaxc[:, 0:1],
            start=True, stop=True, skip_group_check=True),
            waits=(self.kmax_tok, t3), inc=self.ct)
        t5 = self.op("v", lambda h: h.tensor_copy(
            self.hqc[:, 1:2], self.PS2[0:c.HQ, 1152:1153]), waits=(t4,), inc=self.cv)
        t6 = self.op("v", lambda h: h.tensor_tensor(
            self.hqc[:, 2:3], self.hqc[:, 0:1], self.hqc[:, 1:2], ALU.mult),
            waits=(t5,), inc=self.cv)
        t6b = self.op("v", lambda h: h.tensor_scalar(
            self.hqc[:, 2:3], self.hqc[:, 2:3], 1e-36, None, ALU.max),
            waits=(t6,), inc=self.cv)
        t7 = self.op("s", lambda h: h.activation(
            self.hqc[:, 3:4], self.hqc[:, 2:3], AF.Ln),
            waits=(t6b,) + self.bar, inc=self.cs)
        t8 = self.op("s", lambda h: h.activation(
            self.hqc[:, 4:5], self.hqc[:, 3:4], AF.Exp, scale=0.5),
            waits=(t7,), inc=self.cs)
        t9 = self.op("v", lambda h: h.tensor_scalar(
            self.hqc[:, 5:6], self.hqc[:, 4:5], -0.125, None, ALU.mult),
            waits=(t8,), inc=self.cv)
        ta = self.op("t", lambda h: h.transpose(
            self.PS2[0:1, 1024:1024 + c.HQ], self.hqc[0:c.HQ, 5:6],
            self.identf[0:c.HQ, 0:c.HQ]), waits=(t9,), inc=self.ct)
        tb = self.op("v", lambda h: h.tensor_copy(
            self.rowz[0:1, 0:c.HQ], self.PS2[0:1, 1024:1024 + c.HQ]),
            waits=(ta, getattr(self, "_rowz_g", None)), inc=self.cv)
        tc_ = self.op("t", lambda h: h.matmul(
            self.PS2[:, 1024:1024 + c.HQ], self.ones3[0:1, :],
            self.rowz[0:1, 0:c.HQ], start=True, stop=True,
            skip_group_check=True), waits=(tb,), inc=self.ct)
        td = self.op("v", lambda h: h.tensor_copy(
            self.biasmat[:], self.PS2[:, 1024:1024 + c.HQ]),
            waits=(tc_,), inc=self.cv)
        self._rowz_g = tc_
        self.biasmat_tok = td

    def _s2(self, seq, qb, qfm, kfm, vh, ofm):
        c = self.c
        QB = c.QB
        NJ = QB // 512
        atb_off = [0, QB // 2, 2048, 2048 + QB // 2]
        atb = [[self.bf(self.ar7 if s == 0 else self.ar4,
                        atb_off[2 * s + r], QB // 2)
                for r in range(2)] for s in range(2)]
        exp_g = [None, None]
        av_g = [[None, None], [None, None]]
        o_g = [None, None]
        for ip, (hlo, hhi) in enumerate(c.pairs):
            kvl, kvh_ = hlo // c.G, hhi // c.G
            av_last = [None, None]

            def emit_av(kt, side, kv, first_extra=()):
                rot = kt % 2
                for jh in range(NJ):
                    w = ([exp_g[side]] + list(first_extra)) if jh == 0 else ()
                    self.op("t", lambda h,
                            o=self.PS2[0:65, side * 1024 + jh * 512:
                                       side * 1024 + (jh + 1) * 512],
                            l=vh[:, kt, kv, :],
                            r=atb[side][rot][:, jh * 512:(jh + 1) * 512],
                            st=(kt == 0), sp=(kt == c.KT - 1):
                            h.matmul(o, l, r, start=st, stop=sp,
                                     skip_group_check=True),
                            waits=tuple(w), inc=self.ct if jh == NJ - 1 else None)
                av_g[side][rot] = av_last[side] = self.tok("ct")

            pend_av = None
            for kt in range(c.KT):
                rot = kt % 2
                pslo = self.PS1[:, 0:QB]
                pshi = self.PS1[:, 1024:1024 + QB]
                for jh in range(NJ):
                    w = ((self.q_ready, self.biasmat_tok, self.vh_tok, exp_g[0])
                         + self.bar if jh == 0 else ())
                    self.op("t", lambda h, o=pslo[:, jh * 512:(jh + 1) * 512],
                            l=kfm[0:64, kvl // 2, kt * 128:(kt + 1) * 128],
                            r=qfm[0:64, ip, jh * 512:(jh + 1) * 512]:
                            h.matmul(o, l, r, start=True, stop=True,
                                     tile_position=(0, 0), skip_group_check=True),
                            waits=w, inc=self.ct if jh == NJ - 1 else None)
                mlo = self.tok("ct")
                for jh in range(NJ):
                    w = ((exp_g[1],) if jh == 0 else ())
                    self.op("t", lambda h, o=pshi[:, jh * 512:(jh + 1) * 512],
                            l=kfm[64:128, kvh_ // 2, kt * 128:(kt + 1) * 128],
                            r=qfm[64:128, ip, jh * 512:(jh + 1) * 512]:
                            h.matmul(o, l, r, start=True, stop=True,
                                     tile_position=(64, 0), skip_group_check=True),
                            waits=w, inc=self.ct if jh == NJ - 1 else None)
                mhi = self.tok("ct")
                if pend_av is not None:
                    pend_av()
                elo = self.op("s", lambda h, o=atb[0][rot], i=pslo,
                              b=self.biasmat[:, hlo:hlo + 1]:
                              h.activation(o, i, AF.Exp, bias=b, scale=0.125),
                              waits=(mlo, av_g[0][rot]) + self.bar, inc=self.cs)
                ehi = self.op("s", lambda h, o=atb[1][rot], i=pshi,
                              b=self.biasmat[:, hhi:hhi + 1]:
                              h.activation(o, i, AF.Exp, bias=b, scale=0.125),
                              waits=(mhi, av_g[1][rot]), inc=self.cs)
                exp_g = [elo, ehi]

                def pend(kt=kt, fe0=(o_g[0],) if kt == 0 else (),
                         fe1=(o_g[1],) if kt == 0 else ()):
                    emit_av(kt, 0, kvl, fe0)
                    emit_av(kt, 1, kvh_, fe1)
                pend_av = pend
            pend_av()
            for side, hd in ((0, hlo), (1, hhi)):
                psO = self.PS2[:, side * 1024: side * 1024 + QB]
                ost_g = getattr(self, "_ost_g", [None, None])
                ost = self.ar5[0:65, side * 2048: side * 2048 + QB]
                t0 = self.op("v", lambda h, o=ost, i=psO[0:65, :]:
                             h.tensor_copy(o, i),
                             waits=(av_last[side], ost_g[side]), inc=self.cv)
                t1 = self.op("v", lambda h, o=self.rowz[0:1, 0:QB], i=ost[64:65, :]:
                             h.tensor_copy(o, i),
                             waits=(t0, getattr(self, "_rowz_g", None)),
                             inc=self.cv)
                t4 = self.op("v", lambda h, o=self.rowz[0:1, 0:QB],
                             i=self.rowz[0:1, 0:QB]:
                             h.reciprocal(o, i), waits=(t1,), inc=self.cv)
                bm = None
                for jo in range(0, QB, 512):
                    bm = self.op("t", lambda h,
                                 o=psO[0:64, jo:jo + 512],
                                 r=self.rowz[0:1, jo:jo + 512]:
                                 h.matmul(o, self.ones3[0:1, 0:64], r,
                                          start=True, stop=True,
                                          skip_group_check=True),
                                 waits=(t4,) if jo == 0 else (), inc=self.ct)
                self._rowz_g = bm
                dchunk, roff = hd // 2, (hd % 2) * 64
                t7 = self.op("v", lambda h, o=ofm[roff:roff + 64, dchunk, :],
                             a=ost[0:64, :], b=psO[0:64, :]:
                             h.tensor_tensor(o, a, b, ALU.mult),
                             waits=(bm,), inc=self.cv)
                o_g[side] = t7
                ost_g[side] = t7
                self._ost_g = ost_g
        self._rope_gate = None
        self.ps1q = [self.tok("cs")] * 4
        self.ofm_tok = self.tok("cv")

    def _s3(self, seq, qb, ofm):
        c = self.c
        NQT = c.QB // 128
        DH = c.D // 2
        woT = self.bf(self.ar2, 0, c.DC * DH // 2) \
            .rearrange("p (dc f) -> p dc f", f=DH)
        attnout = self.ar5[:, 0:c.D]
        x_rd = self.ar5[:, c.D:2 * c.D]

        def stash_ap(qt):
            if qt < NQT - 1:
                return self.ar1[:, self.o_xw + qt * DH:
                                self.o_xw + (qt + 1) * DH]
            return self.ar8[:, 0:DH]
        stash_g = [None] * NQT
        ln_g = None
        for half in range(2):
            wtok = self.build_wT(self.Wo, woT, half * DH, DH)
            for qt in range(NQT):
                sl = qt % 2
                ps = self.PS1[:, sl * 1024: sl * 1024 + DH]
                for dc in range(c.DC):
                    for jo in range(0, DH, 512):
                        wo_ = min(512, DH - jo)
                        self.op("t", lambda h, o=ps[:, jo:jo + wo_],
                                l=ofm[:, dc, qt * 128:(qt + 1) * 128],
                                r=woT[:, dc, jo:jo + wo_], st=(dc == 0):
                                h.matmul(o, l, r, start=st, stop=False,
                                         skip_group_check=True),
                                waits=(wtok, self.ofm_tok, self.ps1q[2 * sl],
                                       self.ps1q[2 * sl + 1]) + self.bar
                                if (dc == 0 and jo == 0) else ())
                for jo in range(0, DH, 512):
                    wo_ = min(512, DH - jo)
                    self.op("t", lambda h, o=ps[:, jo:jo + wo_], h0=half * DH,
                            jo=jo, wo_=wo_: h.matmul(
                        o, self.ones3[32:33, :],
                        self.brow[32:33, h0 + jo:h0 + jo + wo_],
                        start=False, stop=True, skip_group_check=True),
                        inc=self.ct if jo + 512 >= DH else None)
                mm = self.tok("ct")
                if half == 0:
                    ev = self.op("v", lambda h, o=stash_ap(qt), i=ps:
                                 h.tensor_copy(o, i),
                                 waits=(mm, stash_g[qt]) + self.bar, inc=self.cv)
                    stash_g[qt] = ev
                    self.ps1q[2 * sl] = self.ps1q[2 * sl + 1] = ev
                else:
                    ev = self.op("v", lambda h, o=attnout[:, DH:2 * DH], i=ps:
                                 h.tensor_copy(o, i), waits=(mm, ln_g), inc=self.cv)
                    self.ps1q[2 * sl] = self.ps1q[2 * sl + 1] = ev
                    cp = self.op("v", lambda h, o=attnout[:, 0:DH], i=stash_ap(qt):
                                 h.tensor_copy(o, i), waits=(stash_g[qt], ev),
                                 inc=self.cv)
                    ln_g = self._ln1(seq, qb, qt, attnout, x_rd, (cp, ev))

    def _ln1(self, seq, qb, qt, attnout, x_rd, waits):
        c = self.c
        tcg = qb * (c.QB // 128) + qt
        gq = seq * (c.T // 128) + tcg
        xl = self.dma("q", x_rd, self.x_in[seq, tcg * 128:(tcg + 1) * 128, :],
                      self.d_xr, waits=(getattr(self, "_xr_g", None),))
        s_ = x_rd
        t1 = self.op("v", lambda h, o=s_, a=attnout, b=x_rd:
                     h.tensor_tensor(o, a, b, ALU.add),
                     waits=tuple(waits) + (xl,), inc=self.cv)
        nbk = (c.D + 511) // 512
        t2 = None
        for bi in range(nbk):
            w = min(512, c.D - bi * 512)
            t2 = self.op("v", lambda h, o=self.bn6[:, bi * 6:bi * 6 + 6],
                         i=s_[:, bi * 512:bi * 512 + w]: h.bn_stats(o, i),
                         waits=(t1,), inc=self.cv)
        t3 = self.op("v", lambda h: h.bn_aggr(self.colA[:, 0:2],
                                              self.bn6[:, 0:6 * nbk]),
                     waits=(t2,), inc=self.cv)
        t4 = self.op("v", lambda h: h.tensor_scalar(
            self.colA[:, 2:3], self.colA[:, 1:2], LN_EPS, None, ALU.add),
            waits=(t3,), inc=self.cv)
        t5 = self.op("v", lambda h: h.reciprocal(
            self.colA[:, 3:4], self.colA[:, 2:3]), waits=(t4,), inc=self.cv)
        t6 = self.op("s", lambda h: h.activation(
            self.colA[:, 4:5], self.colA[:, 3:4], AF.Ln), waits=(t5,), inc=self.cs)
        t7 = self.op("s", lambda h: h.activation(
            self.colA[:, 5:6], self.colA[:, 4:5], AF.Exp, scale=0.5),
            waits=(t6,), inc=self.cs)
        t8 = self.op("v", lambda h: h.tensor_scalar(
            self.colA[:, 6:7], self.colA[:, 0:1], self.colA[:, 5:6], -1.0,
            ALU.mult, ALU.mult), waits=(t7,), inc=self.cv)
        x1 = attnout
        t9 = self.op("s", lambda h, o=x1, i=s_: h.activation(
            o, i, AF.Identity, bias=self.colA[:, 6:7], scale=self.colA[:, 5:6]),
            waits=(t8,), inc=self.cs)
        w1 = self.dma("q", self.scr_x1tm[gq * 128:(gq + 1) * 128, :], x1,
                      self.d_x1, waits=(t9,))
        x1b = self.bf(self.ar7, 2048, c.D // 2)
        ta = self.op("v", lambda h, o=x1b, i=x1: h.tensor_copy(o, i),
                     waits=(t9, getattr(self, "_x1b_g", None)), inc=self.cv)
        evs = {}
        ngrp = (c.DC + 7) // 8
        for gj in range(ngrp):
            nt = min(8, c.DC - gj * 8)
            ev = self.transpose_group(
                [x1b[:, (gj * 8 + j) * 128:(gj * 8 + j + 1) * 128]
                 for j in range(nt)],
                self.bf(self.ar8, 1024 + gj * 512, nt * 64)
                .rearrange("p (n f) -> p n f", f=128),
                waits=(ta, getattr(self, "_x1s_g", None)))
            for j in range(nt):
                evs[gj * 8 + j] = ev
        last = None
        for d0 in range(0, c.DC, 4):
            k = min(4, c.DC - d0)
            src = self.bf(self.ar8, 1024 + d0 * 64, k * 64) \
                .rearrange("p (n f) -> p n f", f=128)
            dst = self.scr_x1fm[d0 * 128:(d0 + k) * 128,
                                gq * 128:(gq + 1) * 128] \
                .rearrange("(n p) f -> p n f", p=128)
            last = self.dma("q", dst, src, self.d_x1b, waits=(evs[d0],))
        self._x1s_g = last
        self._x1b_g = evs[(ngrp - 1) * 8]
        self._xr_g = t9
        self.ln_done[gq] = last
        return w1

    def _ffn(self):
        c = self.c
        TW = c.TWIN
        HC = c.HC
        NTC = TW // 128
        n1h = min(HC, self.N1 // (TW // 2))

        def hbuf(hc):
            if hc < n1h:
                return self.bf(self.ar1, hc * (TW // 2), TW // 2)
            return self.bf(self.ar2, 12288 + (hc - n1h) * (TW // 2), TW // 2)
        if HC > n1h:
            assert 12288 + (HC - n1h) * (TW // 2) <= self.N2
        x1w = self.bf(self.ar2, 0, c.DC * TW // 2) \
            .rearrange("p (dc t) -> p dc t", t=TW)
        ytm = [self.ar4[:, 0:c.D], self.ar4[:, 2048:2048 + c.D],
               self.ar5[:, 0:c.D], self.ar8[:, 0:c.D]][:NTC]
        h_g = [None] * HC
        ytm_g = [None] * NTC
        sg_g = [None, None]
        sg_cons = [None, None]
        sp_g = [None, None, None]
        sp_cons = [None, None, None]
        x1w_g = None
        zsb_g = [None, None]
        sq_g = [None, None]
        pend_sos = None
        sos_prev = None
        self.flush()
        self.barrier()
        x1_all = (self.tok("d_x1"), self.tok("d_x1b"))
        for win in range(c.NWIN):
            tok0 = win * TW
            lnw = x1_all
            xw = None
            for d4 in range(c.DC // 4):
                xw = self.dma("q", x1w[:, d4 * 4:(d4 + 1) * 4, :],
                              bass.AP(self.scr_x1fm.tensor,
                                      self.scr_x1fm[d4 * 512:d4 * 512 + 128,
                                                    tok0:tok0 + TW].offset,
                                      [[c.NTOK, 128], [128 * c.NTOK, 4], [1, TW]]),
                              self.d_xw,
                              waits=lnw + (x1w_g,) + self.bar if d4 == 0 else ())
            for hc in range(HC):
                ssl = (hc // 2) % 2
                if hc % 2 == 0:
                    sgsem = [self.d_sg0, self.d_sg1][ssl]
                    gslab = self.bf(self.ar2, 4096 + ssl * 2048, c.DC * 128) \
                        .rearrange("p (dc f) -> p dc f", f=256)
                    uslab = self.bf(self.ar2, 8192 + ssl * 2048, c.DC * 128) \
                        .rearrange("p (dc f) -> p dc f", f=256)
                    w = (self.tok_quant, self.tok_quant1, sg_g[ssl],
                         sg_cons[ssl]) + self.bar
                    last = None
                    for four, (slab, col0) in enumerate(
                            ((gslab, hc * 128), (uslab, c.Hh + hc * 128))):
                        for dq in range(c.DC // 4):
                            last = self.dma(
                                "q", slab[:, dq * 4:(dq + 1) * 4, :],
                                bass.AP(self.scr_gu.tensor,
                                        self.scr_gu[dq * 512:dq * 512 + 128,
                                                    col0:col0 + 256].offset,
                                        [[c.H2, 128], [128 * c.H2, 4], [1, 256]]),
                                sgsem, waits=w if (four == 0 and dq == 0) else ())
                    sg_g[ssl] = last
                    cur = (gslab, uslab, last)
                gslab, uslab, slab_tok = cur
                hoff = (hc % 2) * 128
                qg, qu = 2 * (hc % 2), 2 * (hc % 2) + 1
                psg = self.PS1[:, qg * 512: qg * 512 + TW]
                psu = self.PS1[:, qu * 512: qu * 512 + TW]
                for dc in range(c.DC):
                    self.op("t", lambda h, o=psg, l=gslab[:, dc, hoff:hoff + 128],
                            r=x1w[:, dc, :], st=(dc == 0), sp=(dc == c.DC - 1):
                            h.matmul(o, l, r, start=st, stop=sp,
                                     skip_group_check=True),
                            waits=(xw, slab_tok, self.ps1q[qg]) + self.bar
                            if dc == 0 else (),
                            inc=self.ct if dc == c.DC - 1 else None)
                mg = self.tok("ct")
                for dc in range(c.DC):
                    self.op("t", lambda h, o=psu, l=uslab[:, dc, hoff:hoff + 128],
                            r=x1w[:, dc, :], st=(dc == 0), sp=(dc == c.DC - 1):
                            h.matmul(o, l, r, start=st, stop=sp,
                                     skip_group_check=True),
                            waits=(self.ps1q[qu],) if dc == 0 else (),
                            inc=self.ct if dc == c.DC - 1 else None)
                mu = self.tok("ct")
                sg_cons[ssl] = mu
                xw_cons = mu
                gsil = self.ar8[:, 2048:2048 + TW]
                t1 = self.op("s", lambda h, o=gsil, i=psg:
                             h.activation(o, i, AF.Silu),
                             waits=(mg, getattr(self, "_gsil_g", None)) + self.bar,
                             inc=self.cs)
                self.ps1q[qg] = t1
                hv = hbuf(hc)
                t2 = self.op("v", lambda h, o=hv, a=gsil, b=psu:
                             h.tensor_tensor(o, a, b, ALU.mult),
                             waits=(t1, mu, h_g[hc]) + self.bar, inc=self.cv)
                self._gsil_g = t2
                self.ps1q[qu] = t2
                h_g[hc] = t2
                sqh = self.bf(self.ar7, (hc % 2) * 256, TW // 2)
                t3 = self.op("v", lambda h, o=sqh, a=hv, b=hv:
                             h.tensor_tensor(o, a, b, ALU.mult),
                             waits=(t2, sq_g[hc % 2]), inc=self.cv)

                def sos_mm(hc=hc, t3=t3, sqh=sqh):
                    self.op("t", lambda h, o=self.PS2[0:1, 1024:1024 + TW], i=sqh,
                            st=(hc == 0), sp=(hc == HC - 1):
                            h.matmul(o, self.ones_col[:], i, start=st,
                                     stop=sp, skip_group_check=True),
                            waits=(t3,) + ((sos_prev,) if hc == 0 else ()),
                            inc=self.ct if hc == HC - 1 else None)
                    sq_g[hc % 2] = self.tok("ct")
                if pend_sos is not None:
                    pend_sos()
                pend_sos = sos_mm
            pend_sos()
            pend_sos = None
            sos_tok = self.tok("ct")
            r0 = self.op("v", lambda h: h.tensor_scalar(
                self.rowz_f[0:1, 0:TW], self.PS2[0:1, 1024:1024 + TW],
                1.0 / c.Hh, RMS_EPS, ALU.mult, ALU.add),
                waits=(sos_tok, getattr(self, "_rowz_g", None)), inc=self.cv)
            r1 = self.op("s", lambda h: h.activation(
                self.rowz_f[0:1, 0:TW], self.rowz_f[0:1, 0:TW], AF.Ln),
                waits=(r0,), inc=self.cs)
            r2 = self.op("s", lambda h: h.activation(
                self.rowz_f[0:1, 0:TW], self.rowz_f[0:1, 0:TW], AF.Exp, scale=-0.5),
                waits=(r1,), inc=self.cs)
            sos_prev = r2
            self._rowz_g = r2
            rc_t = []
            rcm_prev = getattr(self, "_rcm_g", None)
            for tcw in range(NTC):
                m = self.op("t", lambda h, o=self.PS2[:, 1536 + tcw:1537 + tcw],
                            l=self.rowz_f[0:1, tcw * 128:(tcw + 1) * 128]:
                            h.matmul(o, l, self.ones3[0:1, 0:1], start=True,
                                     stop=True, skip_group_check=True),
                            waits=(r2, rcm_prev), inc=self.ct)
                e = self.op("v", lambda h, o=self.colB[:, tcw:tcw + 1],
                            i=self.PS2[:, 1536 + tcw:1537 + tcw]:
                            h.tensor_copy(o, i), waits=(m,), inc=self.cv)
                rc_t.append(e)
            self._rcm_g = rc_t[-1]
            h_last = h_g[HC - 1]
            xw_cons_f = xw_cons
            for dc in range(c.DC):
                psl = dc % 3
                spsem = [self.d_sp0, self.d_sp1, self.d_sp2][psl]
                pslab = self.bf(self.ar2, [4096, 8192, 0][psl], HC * 64) \
                    .rearrange("p (hc f) -> p hc f", f=128)
                w0 = ((sp_g[psl], sp_cons[psl], mg, mu, xw_cons)
                      if dc < 3 else (sp_g[psl], sp_cons[psl]))
                sp_last = None
                for h8 in range(HC // 8):
                    sp_last = self.dma(
                        "q", pslab[:, h8 * 8:(h8 + 1) * 8, :],
                        bass.AP(self.scr_p.tensor,
                                self.scr_p[h8 * 1024:h8 * 1024 + 128,
                                           dc * 128:(dc + 1) * 128].offset,
                                [[c.D, 128], [128 * c.D, 8], [1, 128]]),
                        spsem, waits=w0 if h8 == 0 else ())
                sp_g[psl] = sp_last
                qz = dc % 4
                psz = self.PS1[:, qz * 512: qz * 512 + TW]
                for hc in range(HC):
                    self.op("t", lambda h, o=psz, l=pslab[:, hc, :], r=hbuf(hc),
                            st=(hc == 0), sp=(hc == HC - 1):
                            h.matmul(o, l, r, start=st, stop=sp,
                                     skip_group_check=True),
                            waits=(sp_last, h_last, self.ps1q[qz]) if hc == 0 else (),
                            inc=self.ct if hc == HC - 1 else None)
                mz = self.tok("ct")
                sp_cons[psl] = mz
                zsb = self.ar7[:, 1024 + (dc % 2) * 512:1024 + (dc % 2) * 512 + TW]
                ez = self.op("v", lambda h, o=zsb, i=psz: h.tensor_copy(o, i),
                             waits=(mz, zsb_g[dc % 2]), inc=self.cv)
                self.ps1q[qz] = ez
                trk = None
                for tcw in range(NTC):
                    r = self._treg = 1 - self._treg
                    pst = self.PS2[:, r * 512:r * 512 + 128]
                    trk = self.op("t", lambda h, o=pst,
                                  i=zsb[:, tcw * 128:(tcw + 1) * 128]:
                                  h.transpose(o, i, self.identf[:]),
                                  waits=(ez, self.trg[r]), inc=self.ct)
                    ya = self.op("s", lambda h, o=ytm[tcw][:, dc * 128:(dc + 1) * 128],
                                 i=pst, sc=self.colB[:, tcw:tcw + 1]:
                                 h.activation(o, i, AF.Identity, scale=sc),
                                 waits=(trk, rc_t[tcw], ytm_g[tcw]), inc=self.cs)
                    self.trg[r] = ya
                    if dc == c.DC - 1:
                        ytm_g[tcw] = ya
                zsb_g[dc % 2] = trk
            for tcw in range(NTC):
                gq = (tok0 // 128) + tcw
                seq = gq // (c.T // 128)
                tloc = gq % (c.T // 128)
                x1r = self.ar5[:, 2048:2048 + c.D]
                xl = self.dma("q", x1r, self.scr_x1tm[gq * 128:(gq + 1) * 128, :],
                              self.d_xr, waits=(getattr(self, "_xr_g", None),))
                s_ = self.ar6[:, 0:c.D]
                t1 = self.op("v", lambda h, o=s_, a=ytm[tcw], b=x1r:
                             h.tensor_tensor(o, a, b, ALU.add),
                             waits=(xl, ytm_g[tcw], self.st_tok[0], self.st_tok[1]),
                             inc=self.cv)
                self.st_tok = [t1, t1]
                nbk = (c.D + 511) // 512
                t2 = None
                for bi in range(nbk):
                    w = min(512, c.D - bi * 512)
                    t2 = self.op("v", lambda h, o=self.bn6[:, bi * 6:bi * 6 + 6],
                                 i=s_[:, bi * 512:bi * 512 + w]: h.bn_stats(o, i),
                                 waits=(t1,), inc=self.cv)
                t3 = self.op("v", lambda h: h.bn_aggr(self.colA[:, 0:2],
                                                      self.bn6[:, 0:6 * nbk]),
                             waits=(t2,), inc=self.cv)
                t4 = self.op("v", lambda h: h.tensor_scalar(
                    self.colA[:, 2:3], self.colA[:, 1:2], LN_EPS, None, ALU.add),
                    waits=(t3,), inc=self.cv)
                t5 = self.op("v", lambda h: h.reciprocal(
                    self.colA[:, 3:4], self.colA[:, 2:3]), waits=(t4,), inc=self.cv)
                t6 = self.op("s", lambda h: h.activation(
                    self.colA[:, 4:5], self.colA[:, 3:4], AF.Ln),
                    waits=(t5,), inc=self.cs)
                t7 = self.op("s", lambda h: h.activation(
                    self.colA[:, 5:6], self.colA[:, 4:5], AF.Exp, scale=0.5),
                    waits=(t6,), inc=self.cs)
                t8 = self.op("v", lambda h: h.tensor_scalar(
                    self.colA[:, 6:7], self.colA[:, 0:1], self.colA[:, 5:6], -1.0,
                    ALU.mult, ALU.mult), waits=(t7,), inc=self.cv)
                t9 = self.op("s", lambda h, o=ytm[tcw], i=s_: h.activation(
                    o, i, AF.Identity, bias=self.colA[:, 6:7],
                    scale=self.colA[:, 5:6]), waits=(t8,), inc=self.cs)
                ytm_g[tcw] = self.dma(
                    "q", self.out[seq, tloc * 128:(tloc + 1) * 128, :],
                    ytm[tcw], self.d_out, waits=(t9,))
                self._xr_g = t1
            x1w_g = [self.tok("ct"), sp_g[2]]

    def _finish(self):
        waits = tuple(self.tok(nm) for nm in self.cnt if nm.startswith("d_"))
        self.op("q", lambda h: h.nop(), waits=waits)


_BUILT = {}


def _get_nc(cfg_key="full"):
    if cfg_key not in _BUILT:
        c = Cfg()
        b = Builder(c)
        nc = b.build()
        _BUILT[cfg_key] = (c, nc)
    return _BUILT[cfg_key]


def kernel(**inputs):
    c, nc = _get_nc()
    consts = host_consts(c)
    x = np.asarray(inputs["L0_attn_x"])
    base = {
        "Wq": np.ascontiguousarray(inputs["L0_attn_Wq"], np.float32),
        "bq": np.ascontiguousarray(inputs["L0_attn_bq"], np.float32),
        "Wk": np.ascontiguousarray(inputs["L0_attn_Wk"], np.float32),
        "bk": np.ascontiguousarray(inputs["L0_attn_bk"], np.float32),
        "Wv": np.ascontiguousarray(inputs["L0_attn_Wv"], np.float32),
        "bv": np.ascontiguousarray(inputs["L0_attn_bv"], np.float32),
        "Wo": np.ascontiguousarray(inputs["L0_attn_Wo"], np.float32),
        "bo": np.ascontiguousarray(inputs["L0_attn_bo"], np.float32),
        "Wgu": np.ascontiguousarray(inputs["L0_ffn_w_gate_up"], np.float32),
        "Wp": np.ascontiguousarray(inputs["L0_ffn_w_proj"], np.float32),
    }
    base.update({"identb_in": consts["ident"], "identf_in": consts["identf"],
                 "onescol_in": consts["ones_col"], "gmat_in": consts["gmat"],
                 "cos_tab": consts["cos_tab"], "sins_tab": consts["sins_tab"]})
    in_maps = []
    for core in range(NCORES):
        m = dict(base)
        m["x"] = np.ascontiguousarray(
            x[core * c.SEQS:(core + 1) * c.SEQS], np.float32)
        in_maps.append(m)
    res = run_bass_kernel_spmd(nc, in_maps, core_ids=list(range(NCORES)))
    return np.concatenate([r["out"] for r in res.results], axis=0)


# version 11
# speedup vs baseline: 1.8658x; 1.8658x over previous
import contextlib
import numpy as np
import ml_dtypes
import concourse.bass as bass
import concourse.mybir as mybir
from concourse.bass_utils import run_bass_kernel_spmd

F32 = mybir.dt.float32
BF16 = mybir.dt.bfloat16
AF = mybir.ActivationFunctionType
ALU = mybir.AluOpType
AX = mybir.AxisListType

NCORES = 8
LN_EPS = 1e-5
RMS_EPS = 1.1920929e-7


class Cfg:
    def __init__(self, seqs=2, T=2048, D=2048, HQ=32, HKV=8, QB=1024, TWIN=512):
        self.SEQS, self.T, self.D, self.HQ, self.HKV = seqs, T, D, HQ, HKV
        self.HD = 64
        self.G = HQ // HKV
        self.DKV = HKV * self.HD
        self.Hh = 4 * D
        self.H2 = 8 * D
        self.QB, self.TWIN = QB, TWIN
        assert D == HQ * self.HD and T % QB == 0 and QB % 512 == 0
        self.DC = D // 128
        self.TC = T // 128
        self.KT = T // 128
        self.KVC = self.DKV // 128
        self.NQB = T // QB
        self.NTOK = seqs * T
        self.NWIN = self.NTOK // TWIN
        self.HC = self.Hh // 128
        lo = [h for h in range(HQ) if (h // self.G) % 2 == 0]
        hi = [h for h in range(HQ) if (h // self.G) % 2 == 1]
        self.pairs = list(zip(lo, hi))


def host_consts(c: Cfg):
    t = np.arange(c.T, dtype=np.float32)
    inv_freq = (1.0 / 10000.0 ** (np.arange(0, c.HD, 2, dtype=np.float32) / c.HD))
    ang = t[:, None].astype(np.float64) * inv_freq[None, :].astype(np.float64)
    cos = np.cos(ang).astype(np.float32)
    sin = np.sin(ang).astype(np.float32)
    return {
        "cos_tab": np.ascontiguousarray(np.concatenate([cos, cos], 1)),
        "sins_tab": np.ascontiguousarray(
            np.concatenate([-sin, sin], 1)).astype(np.float32),
        "ident": np.eye(128, dtype=ml_dtypes.bfloat16),
        "identf": np.eye(128, dtype=np.float32),
        "ones_col": np.ones((128, 1), dtype=ml_dtypes.bfloat16),
        "gmat": np.stack([(np.arange(c.HQ) // c.G == kv).astype(np.float32)
                          for kv in range(c.HKV)]),
    }


class Builder:
    ENG = {"t": "tensor", "v": "vector", "s": "scalar", "g": "gpsimd", "q": "sync"}

    def __init__(self, c: Cfg):
        self.c = c
        self.nc = bass.Bass("TRN2", target_bir_lowering=False)
        self.cnt = {}
        self.waited = {}
        self.pend = {k: [] for k in self.ENG}
        self.sems = {}
        self.bar = ()

    def sem(self, name):
        s = self.es.enter_context(self.nc.semaphore(name))
        self.sems[id(s)] = name
        self.cnt[name] = 0
        return s

    def op(self, eng, fn, waits=(), inc=None, incn=1):
        tok = None
        flat = []
        for w in waits:
            if w is None:
                continue
            if isinstance(w, list):
                flat.extend(x for x in w if x is not None)
            else:
                flat.append(w)
        waits = tuple(flat)
        if inc is not None:
            nm = self.sems[id(inc)]
            self.cnt[nm] += incn
            tok = (inc, self.cnt[nm])
        self.pend[eng].append((fn, waits, inc, incn))
        return tok

    def dma(self, eng, dst, src, sem, waits=()):
        return self.op(eng, lambda h, d=dst, s=src: h.dma_start(d, s),
                       waits=waits, inc=sem, incn=16)

    def tok(self, semname):
        return (getattr(self, semname), self.cnt[semname])

    def barrier(self):
        self.bar = tuple(self.tok(nm) for nm in self.cnt)

    def flush(self):
        if not any(self.pend.values()):
            return
        with self.nc.Block() as blk:
            for ename, attr in self.ENG.items():
                items = self.pend[ename]
                if not items:
                    continue
                self.pend[ename] = []

                def body(h, items=items, ename=ename):
                    for fn, waits, inc, incn in items:
                        for (sm, v) in waits:
                            if v <= 0:
                                continue
                            key = (ename, self.sems[id(sm)])
                            if self.waited.get(key, 0) >= v:
                                continue
                            h.wait_ge(sm, v)
                            self.waited[key] = v
                        instr = fn(h)
                        if inc is not None:
                            instr.then_inc(inc, incn)
                getattr(blk, attr)(body)

    def build(self):
        c, nc = self.c, self.nc
        self.es = contextlib.ExitStack()
        d = lambda n, s, ty, k: nc.dram_tensor(n, s, ty, kind=k).ap()
        self.x_in = d("x", [c.SEQS, c.T, c.D], F32, "ExternalInput")
        self.Wq = d("Wq", [c.D, c.D], F32, "ExternalInput")
        self.bq = d("bq", [c.D], F32, "ExternalInput")
        self.Wk = d("Wk", [c.DKV, c.D], F32, "ExternalInput")
        self.bk = d("bk", [c.DKV], F32, "ExternalInput")
        self.Wv = d("Wv", [c.DKV, c.D], F32, "ExternalInput")
        self.bv = d("bv", [c.DKV], F32, "ExternalInput")
        self.Wo = d("Wo", [c.D, c.D], F32, "ExternalInput")
        self.bo = d("bo", [c.D], F32, "ExternalInput")
        self.Wgu = d("Wgu", [c.H2, c.D], F32, "ExternalInput")
        self.Wp = d("Wp", [c.D, c.Hh], F32, "ExternalInput")
        self.cos_in = d("cos_tab", [c.T, c.HD], F32, "ExternalInput")
        self.sin_in = d("sins_tab", [c.T, c.HD], F32, "ExternalInput")
        self.ident_in = d("identb_in", [128, 128], BF16, "ExternalInput")
        self.identf_in = d("identf_in", [128, 128], F32, "ExternalInput")
        self.onescol_in = d("onescol_in", [128, 1], BF16, "ExternalInput")
        self.gmat_in = d("gmat_in", [c.HKV, c.HQ], F32, "ExternalInput")
        self.out = d("out", [c.SEQS, c.T, c.D], F32, "ExternalOutput")
        self.scr_gu = d("scr_gu", [c.D, c.H2], BF16, "Internal")
        self.scr_p = d("scr_p", [c.Hh, c.D], BF16, "Internal")
        self.scr_x1tm = d("scr_x1tm", [c.NTOK, c.D], F32, "Internal")
        self.scr_x1fm = d("scr_x1fm", [c.D, c.NTOK], BF16, "Internal")
        self.scr_sum = d("scr_sum", [2, c.QB], F32, "Internal")
        self.scr_rcp = d("scr_rcp", [2, c.QB], F32, "Internal")
        with self.es:
            self._alloc()
            self._program()
        return nc

    def _alloc(self):
        c, nc, es = self.c, self.nc, self.es
        sb = lambda n, s, ty: es.enter_context(nc.sbuf_tensor(n, s, ty))
        pt = lambda n, s, ty: es.enter_context(nc.psum_tensor(n, s, ty))
        need_attn = (c.KVC * c.T // 2 + (c.TC * c.HKV * 65 + 1) // 2
                     + max(2048, c.DC * 128)
                     + max(c.DC * c.DKV // 2, c.DC * 256) + 1024)
        self.N1 = max(need_attn, 14336)
        self.ar1 = sb("ar1", [128, self.N1], F32)
        self.N2 = max(c.DC * c.QB, 12800)
        self.ar2 = sb("ar2", [128, self.N2], F32)
        self.ar4 = sb("ar4", [128, 4096], F32)
        self.ar5 = sb("ar5", [128, 2 * 2048], F32)
        self.ar6 = sb("ar6", [128, 2048], F32)
        self.ar7 = sb("ar7", [128, 3072], F32)
        self.ar8 = sb("ar8", [128, 2560], F32)
        self.ident = sb("ident", [128, 128], BF16)
        self.identf = sb("identf", [128, 128], F32)
        self.ones_col = sb("ones_col", [128, 1], BF16)
        self.gmat = sb("gmat", [c.HKV, c.HQ], F32)
        self.ones3 = sb("ones3", [65, 128], F32)
        self.brow = sb("brow", [65, max(2 * c.DKV, c.D)], F32)
        self.bn6 = sb("bn6", [128, 24], F32)
        self.colA = sb("colA", [128, 8], F32)
        self.colB = sb("colB", [128, 8], F32)
        self.normQ = sb("normQ", [128, (c.QB // 128) * c.HQ], F32)
        self.normK = sb("normK", [128, c.TC * c.HKV], F32)
        self.biasmat = sb("biasmat", [128, c.HQ], F32)
        self.kmaxc = sb("kmaxc", [c.HKV, 2], F32)
        self.hqc = sb("hqc", [c.HQ, 8], F32)
        self.rcol = sb("rcol", [128, 48], F32)
        self.rowz = self.ar8[0:1, 0:1024]
        self.rowz_f = self.ar7[0:1, 512:1024]
        self.PS1 = pt("PS1", [128, 2048], F32)
        self.PS2 = pt("PS2", [128, 2048], F32)
        for nm in ("ct", "cv", "cs", "cg", "d_cst", "d_st0", "d_st1", "d_ws",
                   "d_ws1", "d_x1", "d_x1b", "d_xr", "d_su0", "d_su1", "d_rb0", "d_rb1",
                   "d_xw", "d_sg0", "d_sg1", "d_sp0", "d_sp1", "d_sp2", "d_out"):
            setattr(self, nm, self.sem(nm))
        self.st_tok = [None, None]
        self.lc_g = [None, None]
        self.ps1q = [None, None, None, None]
        self.trg = [None, None]
        self._treg = 0
        self._stsl = 0

    def bf(self, ar, off, n, p0=0, p1=128):
        return ar[p0:p1, off:off + n].bitcast(BF16)

    def ps1(self, i):
        return self.PS1[:, i * 512:(i + 1) * 512]

    def transpose_group(self, tiles, dst, waits=(), f32=False, evac=None):
        r = self._treg = 1 - self._treg
        nt = len(tiles)
        if f32:
            assert nt <= 4
            ps = self.PS2[:, r * 512:r * 512 + nt * 128]
            ident = self.identf
        else:
            assert nt <= 8
            ps = self.PS2[:, r * 512:(r + 1) * 512].bitcast(BF16)[:, 0:nt * 128]
            ident = self.ident
        tk = None
        w0 = tuple(waits) + (self.trg[r],) + self.bar
        for j, s in enumerate(tiles):
            o = ps[:, j * 128:(j + 1) * 128]
            tk = self.op("t", lambda h, o=o, i=s, idn=ident:
                         h.transpose(o, i, idn[:]),
                         waits=w0 if j == 0 else (), inc=self.ct)
        if evac is None:
            pv = ps.rearrange("p (n f) -> p n f", f=128)
            ev = self.op("v", lambda h, d=dst, s=pv: h.tensor_copy(d, s),
                         waits=(tk,), inc=self.cv)
        else:
            ev = evac(ps, tk)
        self.trg[r] = ev
        return ev

    def load_cast(self, src, n, waits=(), eng="q", dst_off=None):
        sl = self._stsl = 1 - self._stsl
        st = [self.d_st0, self.d_st1][sl]
        stage = self.ar6[:, sl * 1024: sl * 1024 + n]
        ld = self.dma(eng, stage, src, st, waits=(self.st_tok[sl],) + self.bar)
        off = sl * 512 if dst_off is None else dst_off
        dst = self.bf(self.ar7, off, n // 2)
        gate = self.lc_g[sl] if dst_off is None else None
        cs = self.op("s", lambda h, d=dst, s=stage: h.copy(d, s),
                     waits=(ld, gate) + tuple(waits) + self.bar, inc=self.cs)
        self.st_tok[sl] = cs
        self._lc_sl = sl
        return dst, cs

    def build_wT(self, W, wT, row0, ncols):
        c = self.c
        ev = None
        for rch in range(ncols // 128):
            for hf in range((c.D + 1023) // 1024):
                w = min(1024, c.D - hf * 1024)
                src = W[row0 + rch * 128: row0 + rch * 128 + 128,
                        hf * 1024: hf * 1024 + w]
                wbf, cs = self.load_cast(src, w)
                sl = self._lc_sl
                tiles = [wbf[:, j * 128:(j + 1) * 128] for j in range(w // 128)]
                dst = wT[:, hf * 8: hf * 8 + w // 128, rch * 128: rch * 128 + 128]
                ev = self.transpose_group(tiles, dst, waits=(cs,))
                self.lc_g[sl] = ev
        return ev

    def _program(self):
        self._consts()
        self.flush()
        self.barrier()
        self._quant()
        self.ln_done = {}
        for seq in range(self.c.SEQS):
            self._attn_seq(seq)
        self.flush()
        self.barrier()
        self.quant_drain()
        self._ffn()
        self._finish()
        self.flush()

    def _consts(self):
        c = self.c
        t = lambda d_, s_: self.dma("q", d_, s_, self.d_cst)
        t(self.ident[:], self.ident_in)
        t(self.identf[:], self.identf_in)
        t(self.ones_col[:], self.onescol_in)
        t(self.gmat[:], self.gmat_in)
        self.o_cos, self.o_sin = 0, c.TC * c.HD
        t(self.ar4[:, self.o_cos:self.o_cos + c.TC * c.HD]
          .rearrange("p (tc f) -> p tc f", f=c.HD),
          self.cos_in.rearrange("(tc p) f -> p tc f", p=128))
        t(self.ar4[:, self.o_sin:self.o_sin + c.TC * c.HD]
          .rearrange("p (tc f) -> p tc f", f=c.HD),
          self.sin_in.rearrange("(tc p) f -> p tc f", p=128))
        t(self.brow[0:1, 0:c.D], self.bq.rearrange("(o d) -> o d", o=1))
        t(self.brow[32:33, 0:c.D], self.bo.rearrange("(o d) -> o d", o=1))
        t(self.brow[64:65, 0:c.DKV], self.bk.rearrange("(o d) -> o d", o=1))
        t(self.brow[64:65, c.DKV:2 * c.DKV], self.bv.rearrange("(o d) -> o d", o=1))
        self.tok_consts = self.tok("d_cst")
        self.tok_ones = self.op("g", lambda h: h.memset(self.ones3[:], 1.0),
                                inc=self.cg)

    def quant_step(self, n=1):
        g = getattr(self, "_qgen", None)
        if g is None:
            return
        for _ in range(n):
            try:
                next(g)
            except StopIteration:
                self._qgen = None
                self.tok_quant = self.tok("d_ws")
                self.tok_quant1 = self.tok("d_ws1")
                return

    def quant_drain(self):
        while getattr(self, "_qgen", None) is not None:
            self.quant_step(8)

    def _quant(self):
        self._qgen = self._quant_gen()
        self.quant_step(5)

    def _quant_gen(self):
        c = self.c
        WBO = [1024, 1536]
        RQ = [2048, 2560]
        STG = [2048, 2560]
        SC = 2048
        cons = [None, None]
        wsd = [None, None]

        def piece(idx, src, dst_fn, n):
            sl = idx % 2
            g = n // 64
            wb, tcast = self.load_cast(src, n, waits=(cons[sl],),
                                       dst_off=WBO[sl])
            wbv = wb.rearrange("p (g k) -> p g k", k=64)
            sums = self.ar8[:, SC + sl * 256: SC + sl * 256 + g]
            sclbf = self.bf(self.ar8, SC + sl * 256 + 32, 8)[:, 0:g]
            rcp = self.ar8[:, SC + sl * 256 + 48: SC + sl * 256 + 48 + g]
            t1 = self.op("v", lambda h, o=sums, i=wbv: h.reduce_sum(
                o.rearrange("p (g o2) -> p g o2", o2=1), i, axis=AX.X,
                apply_absolute_value=True), waits=(tcast,), inc=self.cv)
            t2 = self.op("v", lambda h, o=sclbf, i=sums: h.tensor_scalar(
                o, i, 1.0 / 64, 1e-8, ALU.mult, ALU.max), waits=(t1,), inc=self.cv)
            t3 = self.op("v", lambda h, o=rcp, i=sclbf: h.reciprocal(o, i),
                         waits=(t2,), inc=self.cv)
            r = self.bf(self.ar7, RQ[sl], n // 2)
            rv = r.rearrange("p (g k) -> p g k", k=64)
            rcpb = bass.AP(rcp.tensor, rcp.offset,
                           [list(rcp.ap[0]), [1, g], [0, 64]])
            t4 = self.op("v", lambda h, o=rv, a=wbv, b=rcpb:
                         h.tensor_tensor(o, a, b, ALU.mult),
                         waits=(t3, wsd[sl]) + self.bar, inc=self.cv)
            t5 = self.op("v", lambda h, o=r, i=r: h.tensor_scalar(
                o, i, 192.0, None, ALU.add), waits=(t4,), inc=self.cv)
            t6 = self.op("v", lambda h, o=r, i=r: h.tensor_scalar(
                o, i, -192.0, 1.0, ALU.add, ALU.min), waits=(t5,), inc=self.cv)
            t7 = self.op("v", lambda h, o=r, i=r: h.tensor_scalar(
                o, i, -1.0, None, ALU.max), waits=(t6,), inc=self.cv)
            sclbb = bass.AP(sclbf.tensor, sclbf.offset,
                            [list(sclbf.ap[0]), [1, g], [0, 64]])
            t8 = self.op("v", lambda h, o=wbv, a=rv, b=sclbb:
                         h.tensor_tensor(o, a, b, ALU.mult), waits=(t7,), inc=self.cv)

            def stage2(sl=sl, wb=wb, t8=t8, dst_fn=dst_fn, n=n):
                nt = n // 128
                stg = self.bf(self.ar7, STG[sl], nt * 64)
                stgv = stg.rearrange("p (t f) -> p t f", f=128)
                ev = self.transpose_group(
                    [wb[:, j * 128:(j + 1) * 128] for j in range(nt)], stgv,
                    waits=(t8, wsd[sl]))
                cons[sl] = ev
                self.lc_g[sl] = ev
                last = None
                wsem = [self.d_ws, self.d_ws1][sl]
                for j4 in range(0, nt, 4):
                    k = min(4, nt - j4)
                    last = self.dma("g" if sl else "q", dst_fn(j4, k),
                                    stgv[:, j4:j4 + k, :], wsem, waits=(ev,))
                wsd[sl] = last
            return stage2

        idx = 0
        wcol = min(1024, c.D)
        for rch in range(c.H2 // 128):
            for jc in range(c.D // wcol):
                def dgu(j4, k, rch=rch, jc=jc):
                    r0 = jc * wcol + j4 * 128
                    return self.scr_gu[r0:r0 + k * 128, rch * 128:(rch + 1) * 128] \
                        .rearrange("(n p) f -> p n f", p=128)
                st2 = piece(idx, self.Wgu[rch * 128:(rch + 1) * 128,
                                          jc * wcol:(jc + 1) * wcol], dgu, wcol)
                idx += 1
                yield
                st2()
                yield
        wcol2 = min(1024, c.Hh)
        for rch in range(c.D // 128):
            for jc in range(c.Hh // wcol2):
                def dp(j4, k, rch=rch, jc=jc):
                    r0 = jc * wcol2 + j4 * 128
                    return self.scr_p[r0:r0 + k * 128, rch * 128:(rch + 1) * 128] \
                        .rearrange("(n p) f -> p n f", p=128)
                st2 = piece(idx, self.Wp[rch * 128:(rch + 1) * 128,
                                         jc * wcol2:(jc + 1) * wcol2], dp, wcol2)
                idx += 1
                yield
                st2()
                yield

    def _attn_seq(self, seq):
        c = self.c
        self.o_kfm = 0
        self.o_vh = c.KVC * c.T // 2
        self.o_xw = self.o_vh + (c.TC * c.HKV * 65 + 1) // 2
        self.o_wt = self.o_xw + max(2048, c.DC * 128)
        self.o_tm = self.o_wt + max(c.DC * c.DKV // 2, c.DC * 256)
        kfm = self.bf(self.ar1, self.o_kfm, c.KVC * c.T // 2) \
            .rearrange("p (kc t) -> p kc t", t=c.T)
        nvh = c.TC * c.HKV * 65
        vh = self.bf(self.ar1, self.o_vh, (nvh + 1) // 2)[:, 0:nvh] \
            .rearrange("p (tc kv f) -> p tc kv f", kv=c.HKV, f=65)
        self.flush()
        self.barrier()
        self._s1kv(seq, kfm, vh)
        self.flush()
        for qb in range(c.NQB):
            qfm = self.bf(self.ar2, 0, c.DC * c.QB // 2) \
                .rearrange("p (dc t) -> p dc t", t=c.QB)
            ofm = self.bf(self.ar2, c.DC * c.QB // 2, c.DC * c.QB // 2) \
                .rearrange("p (dc t) -> p dc t", t=c.QB)
            self.barrier()
            self._s1q(seq, qb, qfm)
            self.flush()
            self.barrier()
            self._biasmat()
            self.flush()
            self.barrier()
            self._s2(seq, qb, qfm, kfm, vh, ofm)
            self.flush()
            self.barrier()
            self._s3(seq, qb, ofm)
            self.flush()

    def _xtile(self, seq, tc, off, gate=None):
        c = self.c
        dst = self.bf(self.ar1, off, c.DC * 64) \
            .rearrange("p (dc f) -> p dc f", f=128)
        ev = None
        for hf in range((c.D + 1023) // 1024):
            w = min(1024, c.D - hf * 1024)
            src = self.x_in[seq, tc * 128:(tc + 1) * 128, hf * 1024:hf * 1024 + w]
            xbf, cs = self.load_cast(src, w, waits=(gate,))
            sl = self._lc_sl
            tiles = [xbf[:, j * 128:(j + 1) * 128] for j in range(w // 128)]
            ev = self.transpose_group(tiles, dst[:, hf * 8: hf * 8 + w // 128, :],
                                      waits=(cs,))
            self.lc_g[sl] = ev
        return dst, ev

    def _rope(self, ps, dstview, tc, n, waits):
        par = list(self.ar4[:].ap[0])
        cosb = bass.AP(self.ar4[:].tensor, self.ar4[:, self.o_cos + tc * 64].offset,
                       [par, [0, n], [1, 64]])
        sinb = bass.AP(self.ar4[:].tensor, self.ar4[:, self.o_sin + tc * 64].offset,
                       [par, [0, n], [32, 2], [1, 32]])
        psv = ps.rearrange("p (h f) -> p h f", f=64)
        psw = bass.AP(ps.tensor, ps.offset + 32,
                      [list(ps.ap[0]), [64, n], [-32, 2], [1, 32]])
        ms = self._mrot = 1 - getattr(self, "_mrot", 0)
        m1 = self.ar8[:, ms * 1024:ms * 1024 + n * 64]
        m2 = self.ar8[:, ms * 1024 + 512:ms * 1024 + 512 + n * 64]
        mg = getattr(self, "_rope_gates", [None, None])
        a = self.op("v", lambda h: h.tensor_tensor(
            m1.rearrange("p (h f) -> p h f", f=64), psv, cosb, ALU.mult),
            waits=tuple(waits) + (mg[ms],) + self.bar, inc=self.cv)
        b = self.op("v", lambda h: h.tensor_tensor(
            m2.rearrange("p (h two f) -> p h two f", two=2, f=32), psw, sinb,
            ALU.mult), inc=self.cv)
        e = self.op("v", lambda h: h.tensor_tensor(
            dstview, m1.rearrange("p (h f) -> p h f", f=64),
            m2.rearrange("p (h f) -> p h f", f=64), ALU.add),
            waits=(a, b), inc=self.cv)
        mg[ms] = e
        self._rope_gates = mg
        return e

    def _s1kv(self, seq, kfm, vh):
        c = self.c
        vones = self.op("g", lambda h: h.memset(
            bass.AP(vh.tensor, vh[:, 0, 0, 64:65].offset,
                    [list(vh.ap[0]), [65, c.TC * c.HKV], [1, 1]]), 1.0),
            waits=(self.tok_consts,) + self.bar, inc=self.cg)
        wkT = self.bf(self.ar1, self.o_wt, c.DC * c.DKV // 2) \
            .rearrange("p (dc f) -> p dc f", f=c.DKV)
        wktok = self.build_wT(self.Wk, wkT, 0, c.DKV)
        wvT = self.bf(self.ar2, 0, c.DC * c.DKV // 2) \
            .rearrange("p (dc f) -> p dc f", f=c.DKV)
        wvtok = self.build_wT(self.Wv, wvT, 0, c.DKV)
        self._ktm_g = [None, None]
        for tc in range(c.TC):
            sl = tc % 2
            xt, xtok = self._xtile(seq, tc, self.o_xw + sl * 1024,
                                   gate=getattr(self, "_xt_g", [None, None])[sl])
            ps = self.PS1[:, sl * 1024: sl * 1024 + c.DKV]
            psv = self.PS1[:, sl * 1024 + c.DKV: sl * 1024 + 2 * c.DKV]
            for dc in range(c.DC):
                self.op("t", lambda h, o=ps, l=xt[:, dc, :], r=wkT[:, dc, :],
                        st=(dc == 0):
                        h.matmul(o, l, r, start=st, stop=False,
                                 skip_group_check=True),
                        waits=(xtok, wktok, wvtok, self.ps1q[2 * sl],
                               self.ps1q[2 * sl + 1])
                        + self.bar if dc == 0 else ())
            mm = self.op("t", lambda h, o=ps: h.matmul(
                o, self.ones3[64:65, :], self.brow[64:65, 0:c.DKV],
                start=False, stop=True, skip_group_check=True),
                waits=(self.tok_consts, self.tok_ones), inc=self.ct)
            for dc in range(c.DC):
                self.op("t", lambda h, o=psv, l=xt[:, dc, :], r=wvT[:, dc, :],
                        st=(dc == 0):
                        h.matmul(o, l, r, start=st, stop=False,
                                 skip_group_check=True))
            mmv = self.op("t", lambda h, o=psv: h.matmul(
                o, self.ones3[64:65, :], self.brow[64:65, c.DKV:2 * c.DKV],
                start=False, stop=True, skip_group_check=True), inc=self.ct)
            xg = getattr(self, "_xt_g", [None, None])
            xg[sl] = mmv
            self._xt_g = xg
            sq = self.ar5[:, 0:c.DKV]
            s1 = self.op("s", lambda h, o=sq, i=ps: h.activation(o, i, AF.Square),
                         waits=(mm, getattr(self, "_sq_g", None)), inc=self.cs)
            nk = self.normK[:, tc * c.HKV:(tc + 1) * c.HKV]
            self._sq_g = self.op(
                "v", lambda h, o=nk, i=sq.rearrange("p (kv f) -> p kv f", f=64):
                h.reduce_sum(o.rearrange("p (kv o2) -> p kv o2", o2=1), i,
                             axis=AX.X), waits=(s1,), inc=self.cv)
            ktm = self.bf(self.ar1, self.o_tm + sl * 512, c.DKV // 2)
            rt = self._rope(ps, ktm.rearrange("p (h f) -> p h f", f=64),
                            tc, c.HKV, (mm, s1, self._ktm_g[sl]))
            self.ps1q[2 * sl] = rt
            ev = self.transpose_group(
                [ktm[:, j * 128:(j + 1) * 128] for j in range(c.KVC)],
                bass.AP(kfm.tensor, kfm[:, 0, tc * 128].offset,
                        [list(kfm.ap[0]), [c.T, c.KVC], [1, 128]]),
                waits=(rt,))
            self._ktm_g[sl] = ev
            dstv = bass.AP(vh.tensor, vh[:, tc, 0, 0].offset,
                           [list(vh.ap[0]), [65, c.HKV], [1, 64]])
            evv = self.op("v", lambda h, o=dstv,
                          i=psv.rearrange("p (kv f) -> p kv f", f=64):
                          h.tensor_copy(o, i), waits=(mmv, vones), inc=self.cv)
            self.ps1q[2 * sl + 1] = evv
            self.quant_step(10)
        self.vh_tok = self.tok("cv")
        nkv = bass.AP(self.normK[:].tensor, self.normK[:].offset,
                      [list(self.normK[:].ap[0]), [1, c.HKV], [c.HKV, c.TC]])
        t1 = self.op("v", lambda h: h.reduce_max(
            self.colA[:, 0:c.HKV].rearrange("p (kv o2) -> p kv o2", o2=1), nkv,
            axis=AX.X), waits=(self._sq_g,), inc=self.cv)
        t2 = self.op("t", lambda h: h.transpose(
            self.PS2[0:c.HKV, 1024:1152], self.colA[:, 0:c.HKV], self.identf[:]),
            waits=(t1,), inc=self.ct)
        t3 = self.op("v", lambda h: h.reduce_max(
            self.kmaxc[:, 0:1].rearrange("p (o o2) -> p o o2", o2=1),
            self.PS2[0:c.HKV, 1024:1152].rearrange("p (o f) -> p o f", o=1),
            axis=AX.X), waits=(t2,), inc=self.cv)
        self.kmax_tok = t3

    def _s1q(self, seq, qb, qfm):
        c = self.c
        qf_tok = None
        self._qtm_g = [None, None, None, None]
        NQT = c.QB // 128
        xqb = self.bf(self.ar2, c.DC * c.QB // 2, c.DC * c.QB // 2) \
            .rearrange("p (dc t) -> p dc t", t=c.QB)
        xtoks = []
        for tq in range(NQT):
            tcg = qb * NQT + tq
            dst = xqb[:, :, tq * 128:(tq + 1) * 128]
            xtok = None
            for hf in range((c.D + 1023) // 1024):
                w = min(1024, c.D - hf * 1024)
                src = self.x_in[seq, tcg * 128:(tcg + 1) * 128,
                                hf * 1024:hf * 1024 + w]
                xbf, cs = self.load_cast(src, w)
                sl_ = self._lc_sl
                xtok = self.transpose_group(
                    [xbf[:, j * 128:(j + 1) * 128] for j in range(w // 128)],
                    dst[:, hf * 8:hf * 8 + w // 128, :], waits=(cs,))
                self.lc_g[sl_] = xtok
            xtoks.append(xtok)
            self.quant_step(3)
        for d8p in range(c.HQ // 8):
            wq8 = self.bf(self.ar1, self.o_wt, c.DC * 256) \
                .rearrange("p (dc f) -> p dc f", f=512)
            wtok = self.build_wT(self.Wq, wq8, d8p * 512, 512)
            for tq in range(NQT):
                tcg = qb * NQT + tq
                sl = tq % 4
                ps = self.ps1(sl)
                for dc in range(c.DC):
                    self.op("t", lambda h, o=ps,
                            l=xqb[:, dc, tq * 128:(tq + 1) * 128],
                            r=wq8[:, dc, :], st=(dc == 0):
                            h.matmul(o, l, r, start=st, stop=False,
                                     skip_group_check=True),
                            waits=(xtoks[tq], wtok, self.ps1q[sl])
                            if dc == 0 else ())
                mm = self.op("t", lambda h, o=ps, d0=d8p * 512: h.matmul(
                    o, self.ones3[0:1, :], self.brow[0:1, d0:d0 + 512],
                    start=False, stop=True, skip_group_check=True), inc=self.ct)
                sq = self.ar5[:, 0:512]
                s1 = self.op("s", lambda h, o=sq, i=ps: h.activation(
                    o, i, AF.Square),
                    waits=(mm, getattr(self, "_sq_g", None)), inc=self.cs)
                nq = self.normQ[:, tq * c.HQ + d8p * 8: tq * c.HQ + d8p * 8 + 8]
                self._sq_g = self.op(
                    "v", lambda h, o=nq,
                    i=sq.rearrange("p (g f) -> p g f", f=64):
                    h.reduce_sum(o.rearrange("p (g o2) -> p g o2", o2=1), i,
                                 axis=AX.X), waits=(s1,), inc=self.cv)
                qt_b = self.bf(self.ar1, self.o_tm + sl * 256, 256)
                dstv = bass.AP(qt_b.tensor, qt_b.offset,
                               [list(qt_b.ap[0]), [64, 2], [128, 4], [1, 64]])
                rt = self._rope(ps, dstv, tcg, 8,
                                (mm, s1, self._qtm_g[sl]))
                self.ps1q[sl] = rt
                ev = self.transpose_group(
                    [qt_b[:, j * 128:(j + 1) * 128] for j in range(4)],
                    qfm[:, 4 * d8p:4 * d8p + 4, tq * 128:(tq + 1) * 128],
                    waits=(rt,))
                self._qtm_g[sl] = ev
                qf_tok = ev
                self.quant_step(3)
        self.q_ready = qf_tok

    def _biasmat(self):
        c = self.c
        NQT = c.QB // 128
        nqv = bass.AP(self.normQ[:].tensor, self.normQ[:].offset,
                      [list(self.normQ[:].ap[0]), [1, c.HQ], [c.HQ, NQT]])
        qmax_s = self.ar8[:, 512:512 + c.HQ]
        t1 = self.op("v", lambda h: h.reduce_max(
            qmax_s.rearrange("p (hq o2) -> p hq o2", o2=1), nqv, axis=AX.X),
            waits=self.bar, inc=self.cv)
        t2 = self.op("t", lambda h: h.transpose(
            self.PS2[0:c.HQ, 1024:1152], qmax_s, self.identf[:]),
            waits=(t1,) + self.bar, inc=self.ct)
        t3 = self.op("v", lambda h: h.reduce_max(
            self.hqc[:, 0:1].rearrange("p (o o2) -> p o o2", o2=1),
            self.PS2[0:c.HQ, 1024:1152].rearrange("p (o f) -> p o f", o=1),
            axis=AX.X), waits=(t2,), inc=self.cv)
        t4 = self.op("t", lambda h: h.matmul(
            self.PS2[0:c.HQ, 1152:1153], self.gmat[:], self.kmaxc[:, 0:1],
            start=True, stop=True, skip_group_check=True),
            waits=(self.kmax_tok, t3), inc=self.ct)
        t5 = self.op("v", lambda h: h.tensor_copy(
            self.hqc[:, 1:2], self.PS2[0:c.HQ, 1152:1153]), waits=(t4,), inc=self.cv)
        t6 = self.op("v", lambda h: h.tensor_tensor(
            self.hqc[:, 2:3], self.hqc[:, 0:1], self.hqc[:, 1:2], ALU.mult),
            waits=(t5,), inc=self.cv)
        t6b = self.op("v", lambda h: h.tensor_scalar(
            self.hqc[:, 2:3], self.hqc[:, 2:3], 1e-36, None, ALU.max),
            waits=(t6,), inc=self.cv)
        t7 = self.op("s", lambda h: h.activation(
            self.hqc[:, 3:4], self.hqc[:, 2:3], AF.Ln),
            waits=(t6b,) + self.bar, inc=self.cs)
        t8 = self.op("s", lambda h: h.activation(
            self.hqc[:, 4:5], self.hqc[:, 3:4], AF.Exp, scale=0.5),
            waits=(t7,), inc=self.cs)
        t9 = self.op("v", lambda h: h.tensor_scalar(
            self.hqc[:, 5:6], self.hqc[:, 4:5], -0.125, None, ALU.mult),
            waits=(t8,), inc=self.cv)
        ta = self.op("t", lambda h: h.transpose(
            self.PS2[0:1, 1024:1024 + c.HQ], self.hqc[0:c.HQ, 5:6],
            self.identf[0:c.HQ, 0:c.HQ]), waits=(t9,), inc=self.ct)
        tb = self.op("v", lambda h: h.tensor_copy(
            self.rowz[0:1, 0:c.HQ], self.PS2[0:1, 1024:1024 + c.HQ]),
            waits=(ta, getattr(self, "_rowz_g", None)), inc=self.cv)
        tc_ = self.op("t", lambda h: h.matmul(
            self.PS2[:, 1024:1024 + c.HQ], self.ones3[0:1, :],
            self.rowz[0:1, 0:c.HQ], start=True, stop=True,
            skip_group_check=True), waits=(tb,), inc=self.ct)
        td = self.op("v", lambda h: h.tensor_copy(
            self.biasmat[:], self.PS2[:, 1024:1024 + c.HQ]),
            waits=(tc_,), inc=self.cv)
        self._rowz_g = tc_
        self.biasmat_tok = td

    def _s2(self, seq, qb, qfm, kfm, vh, ofm):
        c = self.c
        QB = c.QB
        NJ = QB // 512
        atb_off = [0, QB // 2, 2048, 2048 + QB // 2]
        atb = [[self.bf(self.ar7 if s == 0 else self.ar4,
                        atb_off[2 * s + r], QB // 2)
                for r in range(2)] for s in range(2)]
        exp_g = [None, None]
        av_g = [[None, None], [None, None]]
        o_g = [None, None]
        for ip, (hlo, hhi) in enumerate(c.pairs):
            kvl, kvh_ = hlo // c.G, hhi // c.G
            av_last = [None, None]

            def emit_av(kt, side, kv, first_extra=()):
                rot = kt % 2
                for jh in range(NJ):
                    w = ([exp_g[side]] + list(first_extra)) if jh == 0 else ()
                    self.op("t", lambda h,
                            o=self.PS2[0:65, side * 1024 + jh * 512:
                                       side * 1024 + (jh + 1) * 512],
                            l=vh[:, kt, kv, :],
                            r=atb[side][rot][:, jh * 512:(jh + 1) * 512],
                            st=(kt == 0), sp=(kt == c.KT - 1):
                            h.matmul(o, l, r, start=st, stop=sp,
                                     skip_group_check=True),
                            waits=tuple(w), inc=self.ct if jh == NJ - 1 else None)
                av_g[side][rot] = av_last[side] = self.tok("ct")

            pend_av = None
            for kt in range(c.KT):
                rot = kt % 2
                pslo = self.PS1[:, 0:QB]
                pshi = self.PS1[:, 1024:1024 + QB]
                for jh in range(NJ):
                    w = ((self.q_ready, self.biasmat_tok, self.vh_tok, exp_g[0])
                         + self.bar if jh == 0 else ())
                    self.op("t", lambda h, o=pslo[:, jh * 512:(jh + 1) * 512],
                            l=kfm[0:64, kvl // 2, kt * 128:(kt + 1) * 128],
                            r=qfm[0:64, ip, jh * 512:(jh + 1) * 512]:
                            h.matmul(o, l, r, start=True, stop=True,
                                     tile_position=(0, 0), skip_group_check=True),
                            waits=w, inc=self.ct if jh == NJ - 1 else None)
                mlo = self.tok("ct")
                for jh in range(NJ):
                    w = ((exp_g[1],) if jh == 0 else ())
                    self.op("t", lambda h, o=pshi[:, jh * 512:(jh + 1) * 512],
                            l=kfm[64:128, kvh_ // 2, kt * 128:(kt + 1) * 128],
                            r=qfm[64:128, ip, jh * 512:(jh + 1) * 512]:
                            h.matmul(o, l, r, start=True, stop=True,
                                     tile_position=(64, 0), skip_group_check=True),
                            waits=w, inc=self.ct if jh == NJ - 1 else None)
                mhi = self.tok("ct")
                if pend_av is not None:
                    pend_av()
                elo = self.op("s", lambda h, o=atb[0][rot], i=pslo,
                              b=self.biasmat[:, hlo:hlo + 1]:
                              h.activation(o, i, AF.Exp, bias=b, scale=0.125),
                              waits=(mlo, av_g[0][rot]) + self.bar, inc=self.cs)
                ehi = self.op("s", lambda h, o=atb[1][rot], i=pshi,
                              b=self.biasmat[:, hhi:hhi + 1]:
                              h.activation(o, i, AF.Exp, bias=b, scale=0.125),
                              waits=(mhi, av_g[1][rot]), inc=self.cs)
                exp_g = [elo, ehi]

                def pend(kt=kt, fe0=(o_g[0],) if kt == 0 else (),
                         fe1=(o_g[1],) if kt == 0 else ()):
                    emit_av(kt, 0, kvl, fe0)
                    emit_av(kt, 1, kvh_, fe1)
                pend_av = pend
            pend_av()
            for side, hd in ((0, hlo), (1, hhi)):
                psO = self.PS2[:, side * 1024: side * 1024 + QB]
                ost_g = getattr(self, "_ost_g", [None, None])
                ost = self.ar5[0:65, side * 2048: side * 2048 + QB]
                t0 = self.op("v", lambda h, o=ost, i=psO[0:65, :]:
                             h.tensor_copy(o, i),
                             waits=(av_last[side], ost_g[side]), inc=self.cv)
                t1 = self.op("v", lambda h, o=self.rowz[0:1, 0:QB], i=ost[64:65, :]:
                             h.tensor_copy(o, i),
                             waits=(t0, getattr(self, "_rowz_g", None)),
                             inc=self.cv)
                bm = None
                for jo in range(0, QB, 512):
                    bm = self.op("t", lambda h,
                                 o=psO[0:64, jo:jo + 512],
                                 r=self.rowz[0:1, jo:jo + 512]:
                                 h.matmul(o, self.ones3[0:1, 0:64], r,
                                          start=True, stop=True,
                                          skip_group_check=True),
                                 waits=(t1,) if jo == 0 else (), inc=self.ct)
                self._rowz_g = bm
                rb = (self.ar8[0:64, 1024:1024 + QB] if side == 0
                      else self.ar5[0:64, 3072:3072 + QB])
                rb_g = getattr(self, "_rb_g", [None, None])
                t4 = self.op("v", lambda h, o=rb, i=psO[0:64, :]:
                             h.reciprocal(o, i),
                             waits=(bm, rb_g[side]), inc=self.cv)
                dchunk, roff = hd // 2, (hd % 2) * 64
                t7 = self.op("v", lambda h, o=ofm[roff:roff + 64, dchunk, :],
                             a=ost[0:64, :], b=rb:
                             h.tensor_tensor(o, a, b, ALU.mult),
                             waits=(t4,), inc=self.cv)
                rb_g[side] = t7
                self._rb_g = rb_g
                o_g[side] = t7
                ost_g[side] = t7
                self._ost_g = ost_g
        self._rope_gate = None
        self.ps1q = [self.tok("cs")] * 4
        self.ofm_tok = self.tok("cv")

    def _s3(self, seq, qb, ofm):
        c = self.c
        NQT = c.QB // 128
        DH = c.D // 2
        woT = self.bf(self.ar2, 0, c.DC * DH // 2) \
            .rearrange("p (dc f) -> p dc f", f=DH)
        attnout = self.ar5[:, 0:c.D]
        x_rd = self.ar5[:, c.D:2 * c.D]

        def stash_ap(qt):
            if qt < NQT - 1:
                return self.ar1[:, self.o_xw + qt * DH:
                                self.o_xw + (qt + 1) * DH]
            return self.ar8[:, 0:DH]
        stash_g = [None] * NQT
        ln_g = None
        for half in range(2):
            wtok = self.build_wT(self.Wo, woT, half * DH, DH)
            for qt in range(NQT):
                sl = qt % 2
                ps = self.PS1[:, sl * 1024: sl * 1024 + DH]
                for dc in range(c.DC):
                    for jo in range(0, DH, 512):
                        wo_ = min(512, DH - jo)
                        self.op("t", lambda h, o=ps[:, jo:jo + wo_],
                                l=ofm[:, dc, qt * 128:(qt + 1) * 128],
                                r=woT[:, dc, jo:jo + wo_], st=(dc == 0):
                                h.matmul(o, l, r, start=st, stop=False,
                                         skip_group_check=True),
                                waits=(wtok, self.ofm_tok, self.ps1q[2 * sl],
                                       self.ps1q[2 * sl + 1]) + self.bar
                                if (dc == 0 and jo == 0) else ())
                for jo in range(0, DH, 512):
                    wo_ = min(512, DH - jo)
                    self.op("t", lambda h, o=ps[:, jo:jo + wo_], h0=half * DH,
                            jo=jo, wo_=wo_: h.matmul(
                        o, self.ones3[32:33, :],
                        self.brow[32:33, h0 + jo:h0 + jo + wo_],
                        start=False, stop=True, skip_group_check=True),
                        inc=self.ct if jo + 512 >= DH else None)
                mm = self.tok("ct")
                if half == 0:
                    ev = self.op("v", lambda h, o=stash_ap(qt), i=ps:
                                 h.tensor_copy(o, i),
                                 waits=(mm, stash_g[qt]) + self.bar, inc=self.cv)
                    stash_g[qt] = ev
                    self.ps1q[2 * sl] = self.ps1q[2 * sl + 1] = ev
                else:
                    ev = self.op("v", lambda h, o=attnout[:, DH:2 * DH], i=ps:
                                 h.tensor_copy(o, i), waits=(mm, ln_g), inc=self.cv)
                    self.ps1q[2 * sl] = self.ps1q[2 * sl + 1] = ev
                    cp = self.op("v", lambda h, o=attnout[:, 0:DH], i=stash_ap(qt):
                                 h.tensor_copy(o, i), waits=(stash_g[qt], ev),
                                 inc=self.cv)
                    ln_g = self._ln1(seq, qb, qt, attnout, x_rd, (cp, ev))

    def _ln1(self, seq, qb, qt, attnout, x_rd, waits):
        c = self.c
        tcg = qb * (c.QB // 128) + qt
        gq = seq * (c.T // 128) + tcg
        xl = self.dma("q", x_rd, self.x_in[seq, tcg * 128:(tcg + 1) * 128, :],
                      self.d_xr, waits=(getattr(self, "_xr_g", None),))
        s_ = x_rd
        t1 = self.op("v", lambda h, o=s_, a=attnout, b=x_rd:
                     h.tensor_tensor(o, a, b, ALU.add),
                     waits=tuple(waits) + (xl,), inc=self.cv)
        nbk = (c.D + 511) // 512
        t2 = None
        for bi in range(nbk):
            w = min(512, c.D - bi * 512)
            t2 = self.op("v", lambda h, o=self.bn6[:, bi * 6:bi * 6 + 6],
                         i=s_[:, bi * 512:bi * 512 + w]: h.bn_stats(o, i),
                         waits=(t1,), inc=self.cv)
        t3 = self.op("v", lambda h: h.bn_aggr(self.colA[:, 0:2],
                                              self.bn6[:, 0:6 * nbk]),
                     waits=(t2,), inc=self.cv)
        t4 = self.op("v", lambda h: h.tensor_scalar(
            self.colA[:, 2:3], self.colA[:, 1:2], LN_EPS, None, ALU.add),
            waits=(t3,), inc=self.cv)
        t5 = self.op("v", lambda h: h.reciprocal(
            self.colA[:, 3:4], self.colA[:, 2:3]), waits=(t4,), inc=self.cv)
        t6 = self.op("s", lambda h: h.activation(
            self.colA[:, 4:5], self.colA[:, 3:4], AF.Ln), waits=(t5,), inc=self.cs)
        t7 = self.op("s", lambda h: h.activation(
            self.colA[:, 5:6], self.colA[:, 4:5], AF.Exp, scale=0.5),
            waits=(t6,), inc=self.cs)
        t8 = self.op("v", lambda h: h.tensor_scalar(
            self.colA[:, 6:7], self.colA[:, 0:1], self.colA[:, 5:6], -1.0,
            ALU.mult, ALU.mult), waits=(t7,), inc=self.cv)
        x1 = attnout
        t9 = self.op("s", lambda h, o=x1, i=s_: h.activation(
            o, i, AF.Identity, bias=self.colA[:, 6:7], scale=self.colA[:, 5:6]),
            waits=(t8,), inc=self.cs)
        w1 = self.dma("q", self.scr_x1tm[gq * 128:(gq + 1) * 128, :], x1,
                      self.d_x1, waits=(t9,))
        x1b = self.bf(self.ar7, 2048, c.D // 2)
        ta = self.op("v", lambda h, o=x1b, i=x1: h.tensor_copy(o, i),
                     waits=(t9, getattr(self, "_x1b_g", None)), inc=self.cv)
        evs = {}
        ngrp = (c.DC + 7) // 8
        for gj in range(ngrp):
            nt = min(8, c.DC - gj * 8)
            ev = self.transpose_group(
                [x1b[:, (gj * 8 + j) * 128:(gj * 8 + j + 1) * 128]
                 for j in range(nt)],
                self.bf(self.ar8, 1024 + gj * 512, nt * 64)
                .rearrange("p (n f) -> p n f", f=128),
                waits=(ta, getattr(self, "_x1s_g", None)))
            for j in range(nt):
                evs[gj * 8 + j] = ev
        last = None
        for d0 in range(0, c.DC, 4):
            k = min(4, c.DC - d0)
            src = self.bf(self.ar8, 1024 + d0 * 64, k * 64) \
                .rearrange("p (n f) -> p n f", f=128)
            dst = self.scr_x1fm[d0 * 128:(d0 + k) * 128,
                                gq * 128:(gq + 1) * 128] \
                .rearrange("(n p) f -> p n f", p=128)
            last = self.dma("q", dst, src, self.d_x1b, waits=(evs[d0],))
        self._x1s_g = last
        self._x1b_g = evs[(ngrp - 1) * 8]
        self._xr_g = t9
        self.ln_done[gq] = last
        return w1

    def _ffn(self):
        c = self.c
        TW = c.TWIN
        HC = c.HC
        NTC = TW // 128
        n1h = min(HC, self.N1 // (TW // 2))

        def hbuf(hc):
            if hc < n1h:
                return self.bf(self.ar1, hc * (TW // 2), TW // 2)
            return self.bf(self.ar2, 12288 + (hc - n1h) * (TW // 2), TW // 2)
        if HC > n1h:
            assert 12288 + (HC - n1h) * (TW // 2) <= self.N2
        x1w = self.bf(self.ar2, 0, c.DC * TW // 2) \
            .rearrange("p (dc t) -> p dc t", t=TW)
        ytm = [self.ar4[:, 0:c.D], self.ar4[:, 2048:2048 + c.D],
               self.ar5[:, 0:c.D], self.ar8[:, 0:c.D]][:NTC]
        h_g = [None] * HC
        ytm_g = [None] * NTC
        sg_g = [None, None]
        sg_cons = [None, None]
        sp_g = [None, None, None]
        sp_cons = [None, None, None]
        x1w_g = None
        zsb_g = [None, None]
        sq_g = [None, None]
        pend_sos = None
        sos_prev = None
        self.flush()
        self.barrier()
        x1_all = (self.tok("d_x1"), self.tok("d_x1b"))
        for win in range(c.NWIN):
            tok0 = win * TW
            lnw = x1_all
            xw = None
            for d4 in range(c.DC // 4):
                xw = self.dma("q", x1w[:, d4 * 4:(d4 + 1) * 4, :],
                              bass.AP(self.scr_x1fm.tensor,
                                      self.scr_x1fm[d4 * 512:d4 * 512 + 128,
                                                    tok0:tok0 + TW].offset,
                                      [[c.NTOK, 128], [128 * c.NTOK, 4], [1, TW]]),
                              self.d_xw,
                              waits=lnw + (x1w_g,) + self.bar if d4 == 0 else ())
            for hc in range(HC):
                ssl = (hc // 2) % 2
                if hc % 2 == 0:
                    sgsem = [self.d_sg0, self.d_sg1][ssl]
                    gslab = self.bf(self.ar2, 4096 + ssl * 2048, c.DC * 128) \
                        .rearrange("p (dc f) -> p dc f", f=256)
                    uslab = self.bf(self.ar2, 8192 + ssl * 2048, c.DC * 128) \
                        .rearrange("p (dc f) -> p dc f", f=256)
                    w = (self.tok_quant, self.tok_quant1, sg_g[ssl],
                         sg_cons[ssl]) + self.bar
                    last = None
                    for four, (slab, col0) in enumerate(
                            ((gslab, hc * 128), (uslab, c.Hh + hc * 128))):
                        for dq in range(c.DC // 4):
                            last = self.dma(
                                "q", slab[:, dq * 4:(dq + 1) * 4, :],
                                bass.AP(self.scr_gu.tensor,
                                        self.scr_gu[dq * 512:dq * 512 + 128,
                                                    col0:col0 + 256].offset,
                                        [[c.H2, 128], [128 * c.H2, 4], [1, 256]]),
                                sgsem, waits=w if (four == 0 and dq == 0) else ())
                    sg_g[ssl] = last
                    cur = (gslab, uslab, last)
                gslab, uslab, slab_tok = cur
                hoff = (hc % 2) * 128
                qg, qu = 2 * (hc % 2), 2 * (hc % 2) + 1
                psg = self.PS1[:, qg * 512: qg * 512 + TW]
                psu = self.PS1[:, qu * 512: qu * 512 + TW]
                for dc in range(c.DC):
                    self.op("t", lambda h, o=psg, l=gslab[:, dc, hoff:hoff + 128],
                            r=x1w[:, dc, :], st=(dc == 0), sp=(dc == c.DC - 1):
                            h.matmul(o, l, r, start=st, stop=sp,
                                     skip_group_check=True),
                            waits=(xw, slab_tok, self.ps1q[qg]) + self.bar
                            if dc == 0 else (),
                            inc=self.ct if dc == c.DC - 1 else None)
                mg = self.tok("ct")
                for dc in range(c.DC):
                    self.op("t", lambda h, o=psu, l=uslab[:, dc, hoff:hoff + 128],
                            r=x1w[:, dc, :], st=(dc == 0), sp=(dc == c.DC - 1):
                            h.matmul(o, l, r, start=st, stop=sp,
                                     skip_group_check=True),
                            waits=(self.ps1q[qu],) if dc == 0 else (),
                            inc=self.ct if dc == c.DC - 1 else None)
                mu = self.tok("ct")
                sg_cons[ssl] = mu
                xw_cons = mu
                gsil = self.ar8[:, 2048:2048 + TW]
                t1 = self.op("s", lambda h, o=gsil, i=psg:
                             h.activation(o, i, AF.Silu),
                             waits=(mg, getattr(self, "_gsil_g", None)) + self.bar,
                             inc=self.cs)
                self.ps1q[qg] = t1
                hv = hbuf(hc)
                t2 = self.op("v", lambda h, o=hv, a=gsil, b=psu:
                             h.tensor_tensor(o, a, b, ALU.mult),
                             waits=(t1, mu, h_g[hc]) + self.bar, inc=self.cv)
                self._gsil_g = t2
                self.ps1q[qu] = t2
                h_g[hc] = t2
                sqh = self.bf(self.ar7, (hc % 2) * 256, TW // 2)
                t3 = self.op("v", lambda h, o=sqh, a=hv, b=hv:
                             h.tensor_tensor(o, a, b, ALU.mult),
                             waits=(t2, sq_g[hc % 2]), inc=self.cv)

                def sos_mm(hc=hc, t3=t3, sqh=sqh):
                    self.op("t", lambda h, o=self.PS2[0:1, 1024:1024 + TW], i=sqh,
                            st=(hc == 0), sp=(hc == HC - 1):
                            h.matmul(o, self.ones_col[:], i, start=st,
                                     stop=sp, skip_group_check=True),
                            waits=(t3,) + ((sos_prev,) if hc == 0 else ()),
                            inc=self.ct if hc == HC - 1 else None)
                    sq_g[hc % 2] = self.tok("ct")
                if pend_sos is not None:
                    pend_sos()
                pend_sos = sos_mm
            pend_sos()
            pend_sos = None
            sos_tok = self.tok("ct")
            r0 = self.op("v", lambda h: h.tensor_scalar(
                self.rowz_f[0:1, 0:TW], self.PS2[0:1, 1024:1024 + TW],
                1.0 / c.Hh, RMS_EPS, ALU.mult, ALU.add),
                waits=(sos_tok, getattr(self, "_rowz_g", None)), inc=self.cv)
            r1 = self.op("s", lambda h: h.activation(
                self.rowz_f[0:1, 0:TW], self.rowz_f[0:1, 0:TW], AF.Ln),
                waits=(r0,), inc=self.cs)
            r2 = self.op("s", lambda h: h.activation(
                self.rowz_f[0:1, 0:TW], self.rowz_f[0:1, 0:TW], AF.Exp, scale=-0.5),
                waits=(r1,), inc=self.cs)
            sos_prev = r2
            self._rowz_g = r2
            rc_t = []
            rcm_prev = getattr(self, "_rcm_g", None)
            for tcw in range(NTC):
                m = self.op("t", lambda h, o=self.PS2[:, 1536 + tcw:1537 + tcw],
                            l=self.rowz_f[0:1, tcw * 128:(tcw + 1) * 128]:
                            h.matmul(o, l, self.ones3[0:1, 0:1], start=True,
                                     stop=True, skip_group_check=True),
                            waits=(r2, rcm_prev), inc=self.ct)
                e = self.op("v", lambda h, o=self.colB[:, tcw:tcw + 1],
                            i=self.PS2[:, 1536 + tcw:1537 + tcw]:
                            h.tensor_copy(o, i), waits=(m,), inc=self.cv)
                rc_t.append(e)
            self._rcm_g = rc_t[-1]
            h_last = h_g[HC - 1]
            xw_cons_f = xw_cons
            for dc in range(c.DC):
                psl = dc % 3
                spsem = [self.d_sp0, self.d_sp1, self.d_sp2][psl]
                pslab = self.bf(self.ar2, [4096, 8192, 0][psl], HC * 64) \
                    .rearrange("p (hc f) -> p hc f", f=128)
                w0 = ((sp_g[psl], sp_cons[psl], mg, mu, xw_cons)
                      if dc < 3 else (sp_g[psl], sp_cons[psl]))
                sp_last = None
                for h8 in range(HC // 8):
                    sp_last = self.dma(
                        "q", pslab[:, h8 * 8:(h8 + 1) * 8, :],
                        bass.AP(self.scr_p.tensor,
                                self.scr_p[h8 * 1024:h8 * 1024 + 128,
                                           dc * 128:(dc + 1) * 128].offset,
                                [[c.D, 128], [128 * c.D, 8], [1, 128]]),
                        spsem, waits=w0 if h8 == 0 else ())
                sp_g[psl] = sp_last
                qz = dc % 4
                psz = self.PS1[:, qz * 512: qz * 512 + TW]
                for hc in range(HC):
                    self.op("t", lambda h, o=psz, l=pslab[:, hc, :], r=hbuf(hc),
                            st=(hc == 0), sp=(hc == HC - 1):
                            h.matmul(o, l, r, start=st, stop=sp,
                                     skip_group_check=True),
                            waits=(sp_last, h_last, self.ps1q[qz]) if hc == 0 else (),
                            inc=self.ct if hc == HC - 1 else None)
                mz = self.tok("ct")
                sp_cons[psl] = mz
                zsb = self.ar7[:, 1024 + (dc % 2) * 512:1024 + (dc % 2) * 512 + TW]
                ez = self.op("v", lambda h, o=zsb, i=psz: h.tensor_copy(o, i),
                             waits=(mz, zsb_g[dc % 2]), inc=self.cv)
                self.ps1q[qz] = ez
                trk = None
                for tcw in range(NTC):
                    r = self._treg = 1 - self._treg
                    pst = self.PS2[:, r * 512:r * 512 + 128]
                    trk = self.op("t", lambda h, o=pst,
                                  i=zsb[:, tcw * 128:(tcw + 1) * 128]:
                                  h.transpose(o, i, self.identf[:]),
                                  waits=(ez, self.trg[r]), inc=self.ct)
                    ya = self.op("s", lambda h, o=ytm[tcw][:, dc * 128:(dc + 1) * 128],
                                 i=pst, sc=self.colB[:, tcw:tcw + 1]:
                                 h.activation(o, i, AF.Identity, scale=sc),
                                 waits=(trk, rc_t[tcw], ytm_g[tcw]), inc=self.cs)
                    self.trg[r] = ya
                    if dc == c.DC - 1:
                        ytm_g[tcw] = ya
                zsb_g[dc % 2] = trk
            for tcw in range(NTC):
                gq = (tok0 // 128) + tcw
                seq = gq // (c.T // 128)
                tloc = gq % (c.T // 128)
                x1r = self.ar5[:, 2048:2048 + c.D]
                xl = self.dma("q", x1r, self.scr_x1tm[gq * 128:(gq + 1) * 128, :],
                              self.d_xr, waits=(getattr(self, "_xr_g", None),))
                s_ = self.ar6[:, 0:c.D]
                t1 = self.op("v", lambda h, o=s_, a=ytm[tcw], b=x1r:
                             h.tensor_tensor(o, a, b, ALU.add),
                             waits=(xl, ytm_g[tcw], self.st_tok[0], self.st_tok[1]),
                             inc=self.cv)
                self.st_tok = [t1, t1]
                nbk = (c.D + 511) // 512
                t2 = None
                for bi in range(nbk):
                    w = min(512, c.D - bi * 512)
                    t2 = self.op("v", lambda h, o=self.bn6[:, bi * 6:bi * 6 + 6],
                                 i=s_[:, bi * 512:bi * 512 + w]: h.bn_stats(o, i),
                                 waits=(t1,), inc=self.cv)
                t3 = self.op("v", lambda h: h.bn_aggr(self.colA[:, 0:2],
                                                      self.bn6[:, 0:6 * nbk]),
                             waits=(t2,), inc=self.cv)
                t4 = self.op("v", lambda h: h.tensor_scalar(
                    self.colA[:, 2:3], self.colA[:, 1:2], LN_EPS, None, ALU.add),
                    waits=(t3,), inc=self.cv)
                t5 = self.op("v", lambda h: h.reciprocal(
                    self.colA[:, 3:4], self.colA[:, 2:3]), waits=(t4,), inc=self.cv)
                t6 = self.op("s", lambda h: h.activation(
                    self.colA[:, 4:5], self.colA[:, 3:4], AF.Ln),
                    waits=(t5,), inc=self.cs)
                t7 = self.op("s", lambda h: h.activation(
                    self.colA[:, 5:6], self.colA[:, 4:5], AF.Exp, scale=0.5),
                    waits=(t6,), inc=self.cs)
                t8 = self.op("v", lambda h: h.tensor_scalar(
                    self.colA[:, 6:7], self.colA[:, 0:1], self.colA[:, 5:6], -1.0,
                    ALU.mult, ALU.mult), waits=(t7,), inc=self.cv)
                t9 = self.op("s", lambda h, o=ytm[tcw], i=s_: h.activation(
                    o, i, AF.Identity, bias=self.colA[:, 6:7],
                    scale=self.colA[:, 5:6]), waits=(t8,), inc=self.cs)
                ytm_g[tcw] = self.dma(
                    "q", self.out[seq, tloc * 128:(tloc + 1) * 128, :],
                    ytm[tcw], self.d_out, waits=(t9,))
                self._xr_g = t1
            x1w_g = [self.tok("ct"), sp_g[2]]

    def _finish(self):
        waits = tuple(self.tok(nm) for nm in self.cnt if nm.startswith("d_"))
        self.op("q", lambda h: h.nop(), waits=waits)


_BUILT = {}


def _get_nc(cfg_key="full"):
    if cfg_key not in _BUILT:
        c = Cfg()
        b = Builder(c)
        nc = b.build()
        _BUILT[cfg_key] = (c, nc)
    return _BUILT[cfg_key]


def kernel(**inputs):
    c, nc = _get_nc()
    consts = host_consts(c)
    x = np.asarray(inputs["L0_attn_x"])
    base = {
        "Wq": np.ascontiguousarray(inputs["L0_attn_Wq"], np.float32),
        "bq": np.ascontiguousarray(inputs["L0_attn_bq"], np.float32),
        "Wk": np.ascontiguousarray(inputs["L0_attn_Wk"], np.float32),
        "bk": np.ascontiguousarray(inputs["L0_attn_bk"], np.float32),
        "Wv": np.ascontiguousarray(inputs["L0_attn_Wv"], np.float32),
        "bv": np.ascontiguousarray(inputs["L0_attn_bv"], np.float32),
        "Wo": np.ascontiguousarray(inputs["L0_attn_Wo"], np.float32),
        "bo": np.ascontiguousarray(inputs["L0_attn_bo"], np.float32),
        "Wgu": np.ascontiguousarray(inputs["L0_ffn_w_gate_up"], np.float32),
        "Wp": np.ascontiguousarray(inputs["L0_ffn_w_proj"], np.float32),
    }
    base.update({"identb_in": consts["ident"], "identf_in": consts["identf"],
                 "onescol_in": consts["ones_col"], "gmat_in": consts["gmat"],
                 "cos_tab": consts["cos_tab"], "sins_tab": consts["sins_tab"]})
    in_maps = []
    for core in range(NCORES):
        m = dict(base)
        m["x"] = np.ascontiguousarray(
            x[core * c.SEQS:(core + 1) * c.SEQS], np.float32)
        in_maps.append(m)
    res = run_bass_kernel_spmd(nc, in_maps, core_ids=list(range(NCORES)))
    return np.concatenate([r["out"] for r in res.results], axis=0)


# version 13
# speedup vs baseline: 1.9573x; 1.0490x over previous
import contextlib
import numpy as np
import ml_dtypes
import concourse.bass as bass
import concourse.mybir as mybir
from concourse.bass_utils import run_bass_kernel_spmd

F32 = mybir.dt.float32
BF16 = mybir.dt.bfloat16
AF = mybir.ActivationFunctionType
ALU = mybir.AluOpType
AX = mybir.AxisListType

NCORES = 8
LN_EPS = 1e-5
RMS_EPS = 1.1920929e-7


class Cfg:
    def __init__(self, seqs=2, T=2048, D=2048, HQ=32, HKV=8, QB=1024, TWIN=512):
        self.SEQS, self.T, self.D, self.HQ, self.HKV = seqs, T, D, HQ, HKV
        self.HD = 64
        self.G = HQ // HKV
        self.DKV = HKV * self.HD
        self.Hh = 4 * D
        self.H2 = 8 * D
        self.QB, self.TWIN = QB, TWIN
        assert D == HQ * self.HD and T % QB == 0 and QB % 512 == 0
        self.DC = D // 128
        self.TC = T // 128
        self.KT = T // 128
        self.KVC = self.DKV // 128
        self.NQB = T // QB
        self.NTOK = seqs * T
        self.NWIN = self.NTOK // TWIN
        self.HC = self.Hh // 128
        lo = [h for h in range(HQ) if (h // self.G) % 2 == 0]
        hi = [h for h in range(HQ) if (h // self.G) % 2 == 1]
        self.pairs = list(zip(lo, hi))


def host_consts(c: Cfg):
    t = np.arange(c.T, dtype=np.float32)
    inv_freq = (1.0 / 10000.0 ** (np.arange(0, c.HD, 2, dtype=np.float32) / c.HD))
    ang = t[:, None].astype(np.float64) * inv_freq[None, :].astype(np.float64)
    cos = np.cos(ang).astype(np.float32)
    sin = np.sin(ang).astype(np.float32)
    return {
        "cos_tab": np.ascontiguousarray(np.concatenate([cos, cos], 1)),
        "sins_tab": np.ascontiguousarray(
            np.concatenate([-sin, sin], 1)).astype(np.float32),
        "ident": np.eye(128, dtype=ml_dtypes.bfloat16),
        "identf": np.eye(128, dtype=np.float32),
        "ones_col": np.ones((128, 1), dtype=ml_dtypes.bfloat16),
        "gmat": np.stack([(np.arange(c.HQ) // c.G == kv).astype(np.float32)
                          for kv in range(c.HKV)]),
    }


class Builder:
    ENG = {"t": "tensor", "v": "vector", "s": "scalar", "g": "gpsimd", "q": "sync"}

    def __init__(self, c: Cfg):
        self.c = c
        self.nc = bass.Bass("TRN2", target_bir_lowering=False)
        self.cnt = {}
        self.waited = {}
        self.pend = {k: [] for k in self.ENG}
        self.sems = {}
        self.bar = ()

    def sem(self, name):
        s = self.es.enter_context(self.nc.semaphore(name))
        self.sems[id(s)] = name
        self.cnt[name] = 0
        return s

    def op(self, eng, fn, waits=(), inc=None, incn=1):
        tok = None
        flat = []
        for w in waits:
            if w is None:
                continue
            if isinstance(w, list):
                flat.extend(x for x in w if x is not None)
            else:
                flat.append(w)
        waits = tuple(flat)
        if inc is not None:
            nm = self.sems[id(inc)]
            self.cnt[nm] += incn
            tok = (inc, self.cnt[nm])
        self.pend[eng].append((fn, waits, inc, incn))
        return tok

    def dma(self, eng, dst, src, sem, waits=()):
        return self.op(eng, lambda h, d=dst, s=src: h.dma_start(d, s),
                       waits=waits, inc=sem, incn=16)

    def tok(self, semname):
        return (getattr(self, semname), self.cnt[semname])

    def barrier(self):
        self.bar = tuple(self.tok(nm) for nm in self.cnt)

    def flush(self):
        if not any(self.pend.values()):
            return
        with self.nc.Block() as blk:
            for ename, attr in self.ENG.items():
                items = self.pend[ename]
                if not items:
                    continue
                self.pend[ename] = []

                def body(h, items=items, ename=ename):
                    for fn, waits, inc, incn in items:
                        for (sm, v) in waits:
                            if v <= 0:
                                continue
                            key = (ename, self.sems[id(sm)])
                            if self.waited.get(key, 0) >= v:
                                continue
                            h.wait_ge(sm, v)
                            self.waited[key] = v
                        instr = fn(h)
                        if inc is not None:
                            instr.then_inc(inc, incn)
                getattr(blk, attr)(body)

    def build(self):
        c, nc = self.c, self.nc
        self.es = contextlib.ExitStack()
        d = lambda n, s, ty, k: nc.dram_tensor(n, s, ty, kind=k).ap()
        self.x_in = d("x", [c.SEQS, c.T, c.D], F32, "ExternalInput")
        self.Wq = d("Wq", [c.D, c.D], F32, "ExternalInput")
        self.bq = d("bq", [c.D], F32, "ExternalInput")
        self.Wk = d("Wk", [c.DKV, c.D], F32, "ExternalInput")
        self.bk = d("bk", [c.DKV], F32, "ExternalInput")
        self.Wv = d("Wv", [c.DKV, c.D], F32, "ExternalInput")
        self.bv = d("bv", [c.DKV], F32, "ExternalInput")
        self.Wo = d("Wo", [c.D, c.D], F32, "ExternalInput")
        self.bo = d("bo", [c.D], F32, "ExternalInput")
        self.Wgu = d("Wgu", [c.H2, c.D], F32, "ExternalInput")
        self.Wp = d("Wp", [c.D, c.Hh], F32, "ExternalInput")
        self.cos_in = d("cos_tab", [c.T, c.HD], F32, "ExternalInput")
        self.sin_in = d("sins_tab", [c.T, c.HD], F32, "ExternalInput")
        self.ident_in = d("identb_in", [128, 128], BF16, "ExternalInput")
        self.identf_in = d("identf_in", [128, 128], F32, "ExternalInput")
        self.onescol_in = d("onescol_in", [128, 1], BF16, "ExternalInput")
        self.gmat_in = d("gmat_in", [c.HKV, c.HQ], F32, "ExternalInput")
        self.out = d("out", [c.SEQS, c.T, c.D], F32, "ExternalOutput")
        self.scr_gu = d("scr_gu", [c.D, c.H2], BF16, "Internal")
        self.scr_p = d("scr_p", [c.Hh, c.D], BF16, "Internal")
        self.scr_x1tm = d("scr_x1tm", [c.NTOK, c.D], F32, "Internal")
        self.scr_x1fm = d("scr_x1fm", [c.D, c.NTOK], BF16, "Internal")
        self.scr_sum = d("scr_sum", [2, c.QB], F32, "Internal")
        self.scr_rcp = d("scr_rcp", [2, c.QB], F32, "Internal")
        with self.es:
            self._alloc()
            self._program()
        return nc

    def _alloc(self):
        c, nc, es = self.c, self.nc, self.es
        sb = lambda n, s, ty: es.enter_context(nc.sbuf_tensor(n, s, ty))
        pt = lambda n, s, ty: es.enter_context(nc.psum_tensor(n, s, ty))
        need_attn = (c.KVC * c.T // 2 + (c.TC * c.HKV * 65 + 1) // 2
                     + max(2048, c.DC * 128)
                     + max(c.DC * c.DKV // 2, c.DC * 256) + 1024)
        self.N1 = max(need_attn, 14336)
        self.ar1 = sb("ar1", [128, self.N1], F32)
        self.N2 = max(c.DC * c.QB, 12800)
        self.ar2 = sb("ar2", [128, self.N2], F32)
        self.ar4 = sb("ar4", [128, 4096], F32)
        self.ar5 = sb("ar5", [128, 2 * 2048], F32)
        self.ar6 = sb("ar6", [128, 2048], F32)
        self.ar7 = sb("ar7", [128, 3072], F32)
        self.ar8 = sb("ar8", [128, 2560], F32)
        self.ident = sb("ident", [128, 128], BF16)
        self.identf = sb("identf", [128, 128], F32)
        self.ones_col = sb("ones_col", [128, 1], BF16)
        self.gmat = sb("gmat", [c.HKV, c.HQ], F32)
        self.ones3 = sb("ones3", [65, 128], F32)
        self.brow = sb("brow", [65, max(2 * c.DKV, c.D)], F32)
        self.bn6 = sb("bn6", [128, 24], F32)
        self.colA = sb("colA", [128, 8], F32)
        self.colB = sb("colB", [128, 8], F32)
        self.normQ = sb("normQ", [128, (c.QB // 128) * c.HQ], F32)
        self.normK = sb("normK", [128, c.TC * c.HKV], F32)
        self.biasmat = sb("biasmat", [128, c.HQ], F32)
        self.kmaxc = sb("kmaxc", [c.HKV, 2], F32)
        self.hqc = sb("hqc", [c.HQ, 8], F32)
        self.rcol = sb("rcol", [128, 48], F32)
        self.rowz = self.ar8[0:1, 0:1024]
        self.rowz_f = self.ar7[0:1, 512:1024]
        self.PS1 = pt("PS1", [128, 2048], F32)
        self.PS2 = pt("PS2", [128, 2048], F32)
        for nm in ("ct", "cv", "cs", "cg", "d_cst", "d_st0", "d_st1", "d_ws",
                   "d_ws1", "d_x1", "d_x1b", "d_xr", "d_su0", "d_su1", "d_rb0", "d_rb1",
                   "d_xw", "d_sg0", "d_sg1", "d_sp0", "d_sp1", "d_sp2", "d_out"):
            setattr(self, nm, self.sem(nm))
        self.st_tok = [None, None]
        self.lc_g = [None, None]
        self.ps1q = [None, None, None, None]
        self.trg = [None, None, None]
        self._nrot = 3
        self._treg = 0
        self._stsl = 0

    def bf(self, ar, off, n, p0=0, p1=128):
        return ar[p0:p1, off:off + n].bitcast(BF16)

    def ps1(self, i):
        return self.PS1[:, i * 512:(i + 1) * 512]

    def transpose_group(self, tiles, dst, waits=(), f32=False, evac=None):
        r = self._treg = (self._treg + 1) % self._nrot
        nt = len(tiles)
        if f32:
            assert nt <= 4
            ps = self.PS2[:, r * 512:r * 512 + nt * 128]
            ident = self.identf
        else:
            assert nt <= 8
            ps = self.PS2[:, r * 512:(r + 1) * 512].bitcast(BF16)[:, 0:nt * 128]
            ident = self.ident
        tk = None
        w0 = tuple(waits) + (self.trg[r],) + self.bar
        for j, s in enumerate(tiles):
            o = ps[:, j * 128:(j + 1) * 128]
            tk = self.op("t", lambda h, o=o, i=s, idn=ident:
                         h.transpose(o, i, idn[:]),
                         waits=w0 if j == 0 else (), inc=self.ct)
        if evac is None:
            pv = ps.rearrange("p (n f) -> p n f", f=128)
            ev = self.op("v", lambda h, d=dst, s=pv: h.tensor_copy(d, s),
                         waits=(tk,), inc=self.cv)
        else:
            ev = evac(ps, tk)
        self.trg[r] = ev
        return ev

    def load_cast(self, src, n, waits=(), eng="q", dst_off=None):
        sl = self._stsl = 1 - self._stsl
        st = [self.d_st0, self.d_st1][sl]
        stage = self.ar6[:, sl * 1024: sl * 1024 + n]
        ld = self.dma(eng, stage, src, st, waits=(self.st_tok[sl],) + self.bar)
        off = sl * 512 if dst_off is None else dst_off
        dst = self.bf(self.ar7, off, n // 2)
        gate = self.lc_g[sl] if dst_off is None else None
        cs = self.op("s", lambda h, d=dst, s=stage: h.copy(d, s),
                     waits=(ld, gate) + tuple(waits) + self.bar, inc=self.cs)
        self.st_tok[sl] = cs
        self._lc_sl = sl
        return dst, cs

    def build_wT(self, W, wT, row0, ncols):
        c = self.c
        ev = None
        for rch in range(ncols // 128):
            for hf in range((c.D + 1023) // 1024):
                w = min(1024, c.D - hf * 1024)
                src = W[row0 + rch * 128: row0 + rch * 128 + 128,
                        hf * 1024: hf * 1024 + w]
                wbf, cs = self.load_cast(src, w)
                sl = self._lc_sl
                tiles = [wbf[:, j * 128:(j + 1) * 128] for j in range(w // 128)]
                dst = wT[:, hf * 8: hf * 8 + w // 128, rch * 128: rch * 128 + 128]
                ev = self.transpose_group(tiles, dst, waits=(cs,))
                self.lc_g[sl] = ev
        return ev

    def _program(self):
        self._consts()
        self.flush()
        self.barrier()
        self._quant()
        self.ln_done = {}
        for seq in range(self.c.SEQS):
            self._attn_seq(seq)
        self.flush()
        self.barrier()
        self.quant_drain()
        self._ffn()
        self._finish()
        self.flush()

    def _consts(self):
        c = self.c
        t = lambda d_, s_: self.dma("q", d_, s_, self.d_cst)
        t(self.ident[:], self.ident_in)
        t(self.identf[:], self.identf_in)
        t(self.ones_col[:], self.onescol_in)
        t(self.gmat[:], self.gmat_in)
        self.o_cos, self.o_sin = 0, c.TC * c.HD
        t(self.ar4[:, self.o_cos:self.o_cos + c.TC * c.HD]
          .rearrange("p (tc f) -> p tc f", f=c.HD),
          self.cos_in.rearrange("(tc p) f -> p tc f", p=128))
        t(self.ar4[:, self.o_sin:self.o_sin + c.TC * c.HD]
          .rearrange("p (tc f) -> p tc f", f=c.HD),
          self.sin_in.rearrange("(tc p) f -> p tc f", p=128))
        t(self.brow[0:1, 0:c.D], self.bq.rearrange("(o d) -> o d", o=1))
        t(self.brow[32:33, 0:c.D], self.bo.rearrange("(o d) -> o d", o=1))
        t(self.brow[64:65, 0:c.DKV], self.bk.rearrange("(o d) -> o d", o=1))
        t(self.brow[64:65, c.DKV:2 * c.DKV], self.bv.rearrange("(o d) -> o d", o=1))
        self.tok_consts = self.tok("d_cst")
        self.tok_ones = self.op("g", lambda h: h.memset(self.ones3[:], 1.0),
                                inc=self.cg)

    def quant_step(self, n=1):
        g = getattr(self, "_qgen", None)
        if g is None:
            return
        for _ in range(n):
            try:
                next(g)
            except StopIteration:
                self._qgen = None
                self.tok_quant = self.tok("d_ws")
                self.tok_quant1 = self.tok("d_ws1")
                return

    def quant_drain(self):
        while getattr(self, "_qgen", None) is not None:
            self.quant_step(8)

    def _quant(self):
        self._qgen = self._quant_gen()
        self.quant_step(5)

    def _quant_gen(self):
        c = self.c
        WBO = [1024, 1536]
        RQ = [2048, 2560]
        STG = [2048, 2560]
        SC = 2048
        cons = [None, None]
        wsd = [None, None]

        def piece(idx, src, dst_fn, n):
            sl = idx % 2
            g = n // 64
            wb, tcast = self.load_cast(src, n, waits=(cons[sl],),
                                       dst_off=WBO[sl])
            wbv = wb.rearrange("p (g k) -> p g k", k=64)
            sums = self.ar8[:, SC + sl * 256: SC + sl * 256 + g]
            sclbf = self.bf(self.ar8, SC + sl * 256 + 32, 8)[:, 0:g]
            rcp = self.ar8[:, SC + sl * 256 + 48: SC + sl * 256 + 48 + g]
            t1 = self.op("v", lambda h, o=sums, i=wbv: h.reduce_sum(
                o.rearrange("p (g o2) -> p g o2", o2=1), i, axis=AX.X,
                apply_absolute_value=True), waits=(tcast,), inc=self.cv)
            t2 = self.op("v", lambda h, o=sclbf, i=sums: h.tensor_scalar(
                o, i, 1.0 / 64, 1e-8, ALU.mult, ALU.max), waits=(t1,), inc=self.cv)
            t3 = self.op("v", lambda h, o=rcp, i=sclbf: h.reciprocal(o, i),
                         waits=(t2,), inc=self.cv)
            r = self.bf(self.ar7, RQ[sl], n // 2)
            rv = r.rearrange("p (g k) -> p g k", k=64)
            rcpb = bass.AP(rcp.tensor, rcp.offset,
                           [list(rcp.ap[0]), [1, g], [0, 64]])
            t4 = self.op("g", lambda h, o=rv, a=wbv, b=rcpb:
                         h.tensor_tensor(o, a, b, ALU.mult),
                         waits=(t3, wsd[sl]) + self.bar, inc=self.cg)
            t5 = self.op("v", lambda h, o=r, i=r: h.tensor_scalar(
                o, i, 192.0, None, ALU.add), waits=(t4,), inc=self.cv)
            t6 = self.op("v", lambda h, o=r, i=r: h.tensor_scalar(
                o, i, -192.0, 1.0, ALU.add, ALU.min), waits=(t5,), inc=self.cv)
            t7 = self.op("v", lambda h, o=r, i=r: h.tensor_scalar(
                o, i, -1.0, None, ALU.max), waits=(t6,), inc=self.cv)
            sclbb = bass.AP(sclbf.tensor, sclbf.offset,
                            [list(sclbf.ap[0]), [1, g], [0, 64]])
            t8 = self.op("g", lambda h, o=wbv, a=rv, b=sclbb:
                         h.tensor_tensor(o, a, b, ALU.mult), waits=(t7,), inc=self.cg)

            def stage2(sl=sl, wb=wb, t8=t8, dst_fn=dst_fn, n=n):
                nt = n // 128
                stg = self.bf(self.ar7, STG[sl], nt * 64)
                stgv = stg.rearrange("p (t f) -> p t f", f=128)
                ev = self.transpose_group(
                    [wb[:, j * 128:(j + 1) * 128] for j in range(nt)], stgv,
                    waits=(t8, wsd[sl]))
                cons[sl] = ev
                self.lc_g[sl] = ev
                last = None
                wsem = [self.d_ws, self.d_ws1][sl]
                for j4 in range(0, nt, 4):
                    k = min(4, nt - j4)
                    last = self.dma("g" if sl else "q", dst_fn(j4, k),
                                    stgv[:, j4:j4 + k, :], wsem, waits=(ev,))
                wsd[sl] = last
            return stage2

        idx = 0
        wcol = min(1024, c.D)
        for rch in range(c.H2 // 128):
            for jc in range(c.D // wcol):
                def dgu(j4, k, rch=rch, jc=jc):
                    r0 = jc * wcol + j4 * 128
                    return self.scr_gu[r0:r0 + k * 128, rch * 128:(rch + 1) * 128] \
                        .rearrange("(n p) f -> p n f", p=128)
                st2 = piece(idx, self.Wgu[rch * 128:(rch + 1) * 128,
                                          jc * wcol:(jc + 1) * wcol], dgu, wcol)
                idx += 1
                yield
                st2()
                yield
        wcol2 = min(1024, c.Hh)
        for rch in range(c.D // 128):
            for jc in range(c.Hh // wcol2):
                def dp(j4, k, rch=rch, jc=jc):
                    r0 = jc * wcol2 + j4 * 128
                    return self.scr_p[r0:r0 + k * 128, rch * 128:(rch + 1) * 128] \
                        .rearrange("(n p) f -> p n f", p=128)
                st2 = piece(idx, self.Wp[rch * 128:(rch + 1) * 128,
                                         jc * wcol2:(jc + 1) * wcol2], dp, wcol2)
                idx += 1
                yield
                st2()
                yield

    def _attn_seq(self, seq):
        c = self.c
        self.o_kfm = 0
        self.o_vh = c.KVC * c.T // 2
        self.o_xw = self.o_vh + (c.TC * c.HKV * 65 + 1) // 2
        self.o_wt = self.o_xw + max(2048, c.DC * 128)
        self.o_tm = self.o_wt + max(c.DC * c.DKV // 2, c.DC * 256)
        kfm = self.bf(self.ar1, self.o_kfm, c.KVC * c.T // 2) \
            .rearrange("p (kc t) -> p kc t", t=c.T)
        nvh = c.TC * c.HKV * 65
        vh = self.bf(self.ar1, self.o_vh, (nvh + 1) // 2)[:, 0:nvh] \
            .rearrange("p (tc kv f) -> p tc kv f", kv=c.HKV, f=65)
        self.flush()
        self.barrier()
        self._s1kv(seq, kfm, vh)
        self.flush()
        for qb in range(c.NQB):
            qfm = self.bf(self.ar2, 0, c.DC * c.QB // 2) \
                .rearrange("p (dc t) -> p dc t", t=c.QB)
            ofm = self.bf(self.ar2, c.DC * c.QB // 2, c.DC * c.QB // 2) \
                .rearrange("p (dc t) -> p dc t", t=c.QB)
            self.barrier()
            self._s1q(seq, qb, qfm)
            self.flush()
            self.barrier()
            self._biasmat()
            self.flush()
            self.barrier()
            self._s2(seq, qb, qfm, kfm, vh, ofm)
            self.flush()
            self.barrier()
            self._s3(seq, qb, ofm)
            self.flush()

    def _xtile(self, seq, tc, off, gate=None):
        c = self.c
        dst = self.bf(self.ar1, off, c.DC * 64) \
            .rearrange("p (dc f) -> p dc f", f=128)
        ev = None
        for hf in range((c.D + 1023) // 1024):
            w = min(1024, c.D - hf * 1024)
            src = self.x_in[seq, tc * 128:(tc + 1) * 128, hf * 1024:hf * 1024 + w]
            xbf, cs = self.load_cast(src, w, waits=(gate,))
            sl = self._lc_sl
            tiles = [xbf[:, j * 128:(j + 1) * 128] for j in range(w // 128)]
            ev = self.transpose_group(tiles, dst[:, hf * 8: hf * 8 + w // 128, :],
                                      waits=(cs,))
            self.lc_g[sl] = ev
        return dst, ev

    def _rope(self, ps, dstview, tc, n, waits):
        par = list(self.ar4[:].ap[0])
        cosb = bass.AP(self.ar4[:].tensor, self.ar4[:, self.o_cos + tc * 64].offset,
                       [par, [0, n], [1, 64]])
        sinb = bass.AP(self.ar4[:].tensor, self.ar4[:, self.o_sin + tc * 64].offset,
                       [par, [0, n], [32, 2], [1, 32]])
        psv = ps.rearrange("p (h f) -> p h f", f=64)
        psw = bass.AP(ps.tensor, ps.offset + 32,
                      [list(ps.ap[0]), [64, n], [-32, 2], [1, 32]])
        ms = self._mrot = 1 - getattr(self, "_mrot", 0)
        m1 = self.ar8[:, ms * 1024:ms * 1024 + n * 64]
        m2 = self.ar8[:, ms * 1024 + 512:ms * 1024 + 512 + n * 64]
        mg = getattr(self, "_rope_gates", [None, None])
        a = self.op("v", lambda h: h.tensor_tensor(
            m1.rearrange("p (h f) -> p h f", f=64), psv, cosb, ALU.mult),
            waits=tuple(waits) + (mg[ms],) + self.bar, inc=self.cv)
        b = self.op("v", lambda h: h.tensor_tensor(
            m2.rearrange("p (h two f) -> p h two f", two=2, f=32), psw, sinb,
            ALU.mult), inc=self.cv)
        e = self.op("v", lambda h: h.tensor_tensor(
            dstview, m1.rearrange("p (h f) -> p h f", f=64),
            m2.rearrange("p (h f) -> p h f", f=64), ALU.add),
            waits=(a, b), inc=self.cv)
        mg[ms] = e
        self._rope_gates = mg
        return e

    def _s1kv(self, seq, kfm, vh):
        c = self.c
        vones = self.op("g", lambda h: h.memset(
            bass.AP(vh.tensor, vh[:, 0, 0, 64:65].offset,
                    [list(vh.ap[0]), [65, c.TC * c.HKV], [1, 1]]), 1.0),
            waits=(self.tok_consts,) + self.bar, inc=self.cg)
        wkT = self.bf(self.ar1, self.o_wt, c.DC * c.DKV // 2) \
            .rearrange("p (dc f) -> p dc f", f=c.DKV)
        wktok = self.build_wT(self.Wk, wkT, 0, c.DKV)
        wvT = self.bf(self.ar2, 0, c.DC * c.DKV // 2) \
            .rearrange("p (dc f) -> p dc f", f=c.DKV)
        wvtok = self.build_wT(self.Wv, wvT, 0, c.DKV)
        self._ktm_g = [None, None]
        for tc in range(c.TC):
            sl = tc % 2
            xt, xtok = self._xtile(seq, tc, self.o_xw + sl * 1024,
                                   gate=getattr(self, "_xt_g", [None, None])[sl])
            ps = self.PS1[:, sl * 1024: sl * 1024 + c.DKV]
            psv = self.PS1[:, sl * 1024 + c.DKV: sl * 1024 + 2 * c.DKV]
            for dc in range(c.DC):
                self.op("t", lambda h, o=ps, l=xt[:, dc, :], r=wkT[:, dc, :],
                        st=(dc == 0):
                        h.matmul(o, l, r, start=st, stop=False,
                                 skip_group_check=True),
                        waits=(xtok, wktok, wvtok, self.ps1q[2 * sl],
                               self.ps1q[2 * sl + 1])
                        + self.bar if dc == 0 else ())
            mm = self.op("t", lambda h, o=ps: h.matmul(
                o, self.ones3[64:65, :], self.brow[64:65, 0:c.DKV],
                start=False, stop=True, skip_group_check=True),
                waits=(self.tok_consts, self.tok_ones), inc=self.ct)
            self.quant_step(2)
            for dc in range(c.DC):
                self.op("t", lambda h, o=psv, l=xt[:, dc, :], r=wvT[:, dc, :],
                        st=(dc == 0):
                        h.matmul(o, l, r, start=st, stop=False,
                                 skip_group_check=True))
            mmv = self.op("t", lambda h, o=psv: h.matmul(
                o, self.ones3[64:65, :], self.brow[64:65, c.DKV:2 * c.DKV],
                start=False, stop=True, skip_group_check=True), inc=self.ct)
            self.quant_step(2)
            xg = getattr(self, "_xt_g", [None, None])
            xg[sl] = mmv
            self._xt_g = xg
            sq = self.ar5[:, 0:c.DKV]
            s1 = self.op("s", lambda h, o=sq, i=ps: h.activation(o, i, AF.Square),
                         waits=(mm, getattr(self, "_sq_g", None)), inc=self.cs)
            nk = self.normK[:, tc * c.HKV:(tc + 1) * c.HKV]
            self._sq_g = self.op(
                "v", lambda h, o=nk, i=sq.rearrange("p (kv f) -> p kv f", f=64):
                h.reduce_sum(o.rearrange("p (kv o2) -> p kv o2", o2=1), i,
                             axis=AX.X), waits=(s1,), inc=self.cv)
            self.quant_step(2)
            ktm = self.bf(self.ar1, self.o_tm + sl * 512, c.DKV // 2)
            rt = self._rope(ps, ktm.rearrange("p (h f) -> p h f", f=64),
                            tc, c.HKV, (mm, s1, self._ktm_g[sl]))
            self.ps1q[2 * sl] = rt
            ev = self.transpose_group(
                [ktm[:, j * 128:(j + 1) * 128] for j in range(c.KVC)],
                bass.AP(kfm.tensor, kfm[:, 0, tc * 128].offset,
                        [list(kfm.ap[0]), [c.T, c.KVC], [1, 128]]),
                waits=(rt,))
            self._ktm_g[sl] = ev
            self.quant_step(2)
            dstv = bass.AP(vh.tensor, vh[:, tc, 0, 0].offset,
                           [list(vh.ap[0]), [65, c.HKV], [1, 64]])
            evv = self.op("v", lambda h, o=dstv,
                          i=psv.rearrange("p (kv f) -> p kv f", f=64):
                          h.tensor_copy(o, i), waits=(mmv, vones), inc=self.cv)
            self.ps1q[2 * sl + 1] = evv
            self.quant_step(2)
        self.vh_tok = self.tok("cv")
        nkv = bass.AP(self.normK[:].tensor, self.normK[:].offset,
                      [list(self.normK[:].ap[0]), [1, c.HKV], [c.HKV, c.TC]])
        t1 = self.op("v", lambda h: h.reduce_max(
            self.colA[:, 0:c.HKV].rearrange("p (kv o2) -> p kv o2", o2=1), nkv,
            axis=AX.X), waits=(self._sq_g,), inc=self.cv)
        t2 = self.op("t", lambda h: h.transpose(
            self.PS2[0:c.HKV, 1536:1664], self.colA[:, 0:c.HKV], self.identf[:]),
            waits=(t1,), inc=self.ct)
        t3 = self.op("v", lambda h: h.reduce_max(
            self.kmaxc[:, 0:1].rearrange("p (o o2) -> p o o2", o2=1),
            self.PS2[0:c.HKV, 1536:1664].rearrange("p (o f) -> p o f", o=1),
            axis=AX.X), waits=(t2,), inc=self.cv)
        self.kmax_tok = t3

    def _s1q(self, seq, qb, qfm):
        c = self.c
        qf_tok = None
        self._qtm_g = [None, None, None, None]
        NQT = c.QB // 128
        xqb = self.bf(self.ar2, c.DC * c.QB // 2, c.DC * c.QB // 2) \
            .rearrange("p (dc t) -> p dc t", t=c.QB)
        xtoks = []
        for tq in range(NQT):
            tcg = qb * NQT + tq
            dst = xqb[:, :, tq * 128:(tq + 1) * 128]
            xtok = None
            for hf in range((c.D + 1023) // 1024):
                w = min(1024, c.D - hf * 1024)
                src = self.x_in[seq, tcg * 128:(tcg + 1) * 128,
                                hf * 1024:hf * 1024 + w]
                xbf, cs = self.load_cast(src, w)
                sl_ = self._lc_sl
                xtok = self.transpose_group(
                    [xbf[:, j * 128:(j + 1) * 128] for j in range(w // 128)],
                    dst[:, hf * 8:hf * 8 + w // 128, :], waits=(cs,))
                self.lc_g[sl_] = xtok
                self.quant_step(1)
            xtoks.append(xtok)
            self.quant_step(1)
        for d8p in range(c.HQ // 8):
            wq8 = self.bf(self.ar1, self.o_wt, c.DC * 256) \
                .rearrange("p (dc f) -> p dc f", f=512)
            wtok = self.build_wT(self.Wq, wq8, d8p * 512, 512)
            for tq in range(NQT):
                tcg = qb * NQT + tq
                sl = tq % 4
                ps = self.ps1(sl)
                for dc in range(c.DC):
                    self.op("t", lambda h, o=ps,
                            l=xqb[:, dc, tq * 128:(tq + 1) * 128],
                            r=wq8[:, dc, :], st=(dc == 0):
                            h.matmul(o, l, r, start=st, stop=False,
                                     skip_group_check=True),
                            waits=(xtoks[tq], wtok, self.ps1q[sl])
                            if dc == 0 else ())
                mm = self.op("t", lambda h, o=ps, d0=d8p * 512: h.matmul(
                    o, self.ones3[0:1, :], self.brow[0:1, d0:d0 + 512],
                    start=False, stop=True, skip_group_check=True), inc=self.ct)
                self.quant_step(1)
                sq = self.ar5[:, 0:512]
                s1 = self.op("s", lambda h, o=sq, i=ps: h.activation(
                    o, i, AF.Square),
                    waits=(mm, getattr(self, "_sq_g", None)), inc=self.cs)
                nq = self.normQ[:, tq * c.HQ + d8p * 8: tq * c.HQ + d8p * 8 + 8]
                self._sq_g = self.op(
                    "v", lambda h, o=nq,
                    i=sq.rearrange("p (g f) -> p g f", f=64):
                    h.reduce_sum(o.rearrange("p (g o2) -> p g o2", o2=1), i,
                                 axis=AX.X), waits=(s1,), inc=self.cv)
                qt_b = self.bf(self.ar1, self.o_tm + sl * 256, 256)
                dstv = bass.AP(qt_b.tensor, qt_b.offset,
                               [list(qt_b.ap[0]), [64, 2], [128, 4], [1, 64]])
                rt = self._rope(ps, dstv, tcg, 8,
                                (mm, s1, self._qtm_g[sl]))
                self.quant_step(1)
                self.ps1q[sl] = rt
                ev = self.transpose_group(
                    [qt_b[:, j * 128:(j + 1) * 128] for j in range(4)],
                    qfm[:, 4 * d8p:4 * d8p + 4, tq * 128:(tq + 1) * 128],
                    waits=(rt,))
                self._qtm_g[sl] = ev
                qf_tok = ev
                self.quant_step(1)
        self.q_ready = qf_tok

    def _biasmat(self):
        c = self.c
        NQT = c.QB // 128
        nqv = bass.AP(self.normQ[:].tensor, self.normQ[:].offset,
                      [list(self.normQ[:].ap[0]), [1, c.HQ], [c.HQ, NQT]])
        qmax_s = self.ar8[:, 512:512 + c.HQ]
        t1 = self.op("v", lambda h: h.reduce_max(
            qmax_s.rearrange("p (hq o2) -> p hq o2", o2=1), nqv, axis=AX.X),
            waits=self.bar, inc=self.cv)
        t2 = self.op("t", lambda h: h.transpose(
            self.PS2[0:c.HQ, 1536:1664], qmax_s, self.identf[:]),
            waits=(t1,) + self.bar, inc=self.ct)
        t3 = self.op("v", lambda h: h.reduce_max(
            self.hqc[:, 0:1].rearrange("p (o o2) -> p o o2", o2=1),
            self.PS2[0:c.HQ, 1536:1664].rearrange("p (o f) -> p o f", o=1),
            axis=AX.X), waits=(t2,), inc=self.cv)
        t4 = self.op("t", lambda h: h.matmul(
            self.PS2[0:c.HQ, 1664:1665], self.gmat[:], self.kmaxc[:, 0:1],
            start=True, stop=True, skip_group_check=True),
            waits=(self.kmax_tok, t3), inc=self.ct)
        t5 = self.op("v", lambda h: h.tensor_copy(
            self.hqc[:, 1:2], self.PS2[0:c.HQ, 1664:1665]), waits=(t4,), inc=self.cv)
        t6 = self.op("v", lambda h: h.tensor_tensor(
            self.hqc[:, 2:3], self.hqc[:, 0:1], self.hqc[:, 1:2], ALU.mult),
            waits=(t5,), inc=self.cv)
        t6b = self.op("v", lambda h: h.tensor_scalar(
            self.hqc[:, 2:3], self.hqc[:, 2:3], 1e-36, None, ALU.max),
            waits=(t6,), inc=self.cv)
        t7 = self.op("s", lambda h: h.activation(
            self.hqc[:, 3:4], self.hqc[:, 2:3], AF.Ln),
            waits=(t6b,) + self.bar, inc=self.cs)
        t8 = self.op("s", lambda h: h.activation(
            self.hqc[:, 4:5], self.hqc[:, 3:4], AF.Exp, scale=0.5),
            waits=(t7,), inc=self.cs)
        t9 = self.op("v", lambda h: h.tensor_scalar(
            self.hqc[:, 5:6], self.hqc[:, 4:5], -0.125, None, ALU.mult),
            waits=(t8,), inc=self.cv)
        ta = self.op("t", lambda h: h.transpose(
            self.PS2[0:1, 1536:1536 + c.HQ], self.hqc[0:c.HQ, 5:6],
            self.identf[0:c.HQ, 0:c.HQ]), waits=(t9,), inc=self.ct)
        tb = self.op("v", lambda h: h.tensor_copy(
            self.rowz[0:1, 0:c.HQ], self.PS2[0:1, 1536:1536 + c.HQ]),
            waits=(ta, getattr(self, "_rowz_g", None)), inc=self.cv)
        tc_ = self.op("t", lambda h: h.matmul(
            self.PS2[:, 1536:1536 + c.HQ], self.ones3[0:1, :],
            self.rowz[0:1, 0:c.HQ], start=True, stop=True,
            skip_group_check=True), waits=(tb,), inc=self.ct)
        td = self.op("v", lambda h: h.tensor_copy(
            self.biasmat[:], self.PS2[:, 1536:1536 + c.HQ]),
            waits=(tc_,), inc=self.cv)
        self._rowz_g = tc_
        self.biasmat_tok = td

    def _s2(self, seq, qb, qfm, kfm, vh, ofm):
        c = self.c
        QB = c.QB
        NJ = QB // 512
        atb_off = [0, QB // 2, 2048, 2048 + QB // 2]
        atb = [[self.bf(self.ar7 if s == 0 else self.ar4,
                        atb_off[2 * s + r], QB // 2)
                for r in range(2)] for s in range(2)]
        exp_g = [None, None]
        av_g = [[None, None], [None, None]]
        o_g = [None, None]
        for ip, (hlo, hhi) in enumerate(c.pairs):
            kvl, kvh_ = hlo // c.G, hhi // c.G
            av_last = [None, None]

            def emit_av(kt, side, kv, first_extra=()):
                rot = kt % 2
                for jh in range(NJ):
                    w = ([exp_g[side]] + list(first_extra)) if jh == 0 else ()
                    self.op("t", lambda h,
                            o=self.PS2[0:65, side * 1024 + jh * 512:
                                       side * 1024 + (jh + 1) * 512],
                            l=vh[:, kt, kv, :],
                            r=atb[side][rot][:, jh * 512:(jh + 1) * 512],
                            st=(kt == 0), sp=(kt == c.KT - 1):
                            h.matmul(o, l, r, start=st, stop=sp,
                                     skip_group_check=True),
                            waits=tuple(w), inc=self.ct if jh == NJ - 1 else None)
                av_g[side][rot] = av_last[side] = self.tok("ct")

            pend_av = None
            for kt in range(c.KT):
                rot = kt % 2
                pslo = self.PS1[:, 0:QB]
                pshi = self.PS1[:, 1024:1024 + QB]
                for jh in range(NJ):
                    w = ((self.q_ready, self.biasmat_tok, self.vh_tok, exp_g[0])
                         + self.bar if jh == 0 else ())
                    self.op("t", lambda h, o=pslo[:, jh * 512:(jh + 1) * 512],
                            l=kfm[0:64, kvl // 2, kt * 128:(kt + 1) * 128],
                            r=qfm[0:64, ip, jh * 512:(jh + 1) * 512]:
                            h.matmul(o, l, r, start=True, stop=True,
                                     tile_position=(0, 0), skip_group_check=True),
                            waits=w, inc=self.ct if jh == NJ - 1 else None)
                mlo = self.tok("ct")
                for jh in range(NJ):
                    w = ((exp_g[1],) if jh == 0 else ())
                    self.op("t", lambda h, o=pshi[:, jh * 512:(jh + 1) * 512],
                            l=kfm[64:128, kvh_ // 2, kt * 128:(kt + 1) * 128],
                            r=qfm[64:128, ip, jh * 512:(jh + 1) * 512]:
                            h.matmul(o, l, r, start=True, stop=True,
                                     tile_position=(64, 0), skip_group_check=True),
                            waits=w, inc=self.ct if jh == NJ - 1 else None)
                mhi = self.tok("ct")
                if pend_av is not None:
                    pend_av()
                elo = self.op("s", lambda h, o=atb[0][rot], i=pslo,
                              b=self.biasmat[:, hlo:hlo + 1]:
                              h.activation(o, i, AF.Exp, bias=b, scale=0.125),
                              waits=(mlo, av_g[0][rot]) + self.bar, inc=self.cs)
                ehi = self.op("s", lambda h, o=atb[1][rot], i=pshi,
                              b=self.biasmat[:, hhi:hhi + 1]:
                              h.activation(o, i, AF.Exp, bias=b, scale=0.125),
                              waits=(mhi, av_g[1][rot]), inc=self.cs)
                exp_g = [elo, ehi]

                def pend(kt=kt, fe0=(o_g[0],) if kt == 0 else (),
                         fe1=(o_g[1],) if kt == 0 else ()):
                    emit_av(kt, 0, kvl, fe0)
                    emit_av(kt, 1, kvh_, fe1)
                pend_av = pend
            pend_av()
            for side, hd in ((0, hlo), (1, hhi)):
                psO = self.PS2[:, side * 1024: side * 1024 + QB]
                ssu = [self.d_su0, self.d_su1][side]
                srb = [self.d_rb0, self.d_rb1][side]
                su_g = getattr(self, "_su_g", [None, None])
                rb_g = getattr(self, "_rb_g", [None, None])
                ost_g = getattr(self, "_ost_g", [None, None])
                ost = self.ar5[0:65, side * 2048: side * 2048 + QB]
                t0 = self.op("v", lambda h, o=ost, i=psO[0:65, :]:
                             h.tensor_copy(o, i),
                             waits=(av_last[side], ost_g[side]), inc=self.cv)
                o_g[side] = t0
                psO = ost
                t1 = self.op("v", lambda h, o=self.rowz[0:1, 0:QB], i=psO[64:65, :]:
                             h.tensor_copy(o, i),
                             waits=(t0, getattr(self, "_rowz_g", None)),
                             inc=self.cv)
                t2 = self.dma("q", self.scr_sum[side], self.rowz[0:1, 0:QB],
                              ssu, waits=(t1, su_g[side]))
                self._rowz_g = t2
                t3 = self.dma("q", self.rcol[:, side * 16:side * 16 + QB // 128],
                              self.scr_sum[side].rearrange("(p f) -> p f", p=128),
                              ssu, waits=(t2,))
                t4 = self.op("v", lambda h,
                             o=self.rcol[:, side * 16 + 8:side * 16 + 8 + QB // 128],
                             i=self.rcol[:, side * 16:side * 16 + QB // 128]:
                             h.reciprocal(o, i), waits=(t3,), inc=self.cv)
                su_g[side] = t4
                self._su_g = su_g
                t5 = self.dma("q", self.scr_rcp[side]
                              .rearrange("(p f) -> p f", p=128),
                              self.rcol[:, side * 16 + 8:side * 16 + 8 + QB // 128],
                              srb, waits=(t4, rb_g[side]))
                rb = (self.ar8[0:64, 1024:1024 + QB] if side == 0
                      else self.ar5[0:64, 3072:3072 + QB])
                t6 = self.dma("q", rb, bass.AP(
                    self.scr_rcp.tensor, self.scr_rcp[side].offset,
                    [[0, 64], [1, QB]]), srb, waits=(t5,))
                dchunk, roff = hd // 2, (hd % 2) * 64
                t7 = self.op("v", lambda h, o=ofm[roff:roff + 64, dchunk, :],
                             a=psO[0:64, :], b=rb:
                             h.tensor_tensor(o, a, b, ALU.mult),
                             waits=(t6,), inc=self.cv)
                rb_g[side] = t7
                self._rb_g = rb_g
                ost_g[side] = t7
                self._ost_g = ost_g
        self._rope_gate = None
        self.ps1q = [self.tok("cs")] * 4
        self.ofm_tok = self.tok("cv")

    def _s3(self, seq, qb, ofm):
        c = self.c
        NQT = c.QB // 128
        DH = c.D // 2
        woT = self.bf(self.ar2, 0, c.DC * DH // 2) \
            .rearrange("p (dc f) -> p dc f", f=DH)
        attnout = self.ar5[:, 0:c.D]
        x_rd = self.ar5[:, c.D:2 * c.D]

        def stash_ap(qt):
            if qt < NQT - 1:
                return self.ar1[:, self.o_xw + qt * DH:
                                self.o_xw + (qt + 1) * DH]
            return self.ar8[:, 0:DH]
        stash_g = [None] * NQT
        ln_g = None
        for half in range(2):
            wtok = self.build_wT(self.Wo, woT, half * DH, DH)
            for qt in range(NQT):
                sl = qt % 2
                ps = self.PS1[:, sl * 1024: sl * 1024 + DH]
                for dc in range(c.DC):
                    for jo in range(0, DH, 512):
                        wo_ = min(512, DH - jo)
                        self.op("t", lambda h, o=ps[:, jo:jo + wo_],
                                l=ofm[:, dc, qt * 128:(qt + 1) * 128],
                                r=woT[:, dc, jo:jo + wo_], st=(dc == 0):
                                h.matmul(o, l, r, start=st, stop=False,
                                         skip_group_check=True),
                                waits=(wtok, self.ofm_tok, self.ps1q[2 * sl],
                                       self.ps1q[2 * sl + 1]) + self.bar
                                if (dc == 0 and jo == 0) else ())
                for jo in range(0, DH, 512):
                    wo_ = min(512, DH - jo)
                    self.op("t", lambda h, o=ps[:, jo:jo + wo_], h0=half * DH,
                            jo=jo, wo_=wo_: h.matmul(
                        o, self.ones3[32:33, :],
                        self.brow[32:33, h0 + jo:h0 + jo + wo_],
                        start=False, stop=True, skip_group_check=True),
                        inc=self.ct if jo + 512 >= DH else None)
                mm = self.tok("ct")
                if half == 0:
                    ev = self.op("v", lambda h, o=stash_ap(qt), i=ps:
                                 h.tensor_copy(o, i),
                                 waits=(mm, stash_g[qt]) + self.bar, inc=self.cv)
                    stash_g[qt] = ev
                    self.ps1q[2 * sl] = self.ps1q[2 * sl + 1] = ev
                else:
                    ev = self.op("v", lambda h, o=attnout[:, DH:2 * DH], i=ps:
                                 h.tensor_copy(o, i), waits=(mm, ln_g), inc=self.cv)
                    self.ps1q[2 * sl] = self.ps1q[2 * sl + 1] = ev
                    cp = self.op("v", lambda h, o=attnout[:, 0:DH], i=stash_ap(qt):
                                 h.tensor_copy(o, i), waits=(stash_g[qt], ev),
                                 inc=self.cv)
                    ln_g = self._ln1(seq, qb, qt, attnout, x_rd, (cp, ev))

    def _ln1(self, seq, qb, qt, attnout, x_rd, waits):
        c = self.c
        tcg = qb * (c.QB // 128) + qt
        gq = seq * (c.T // 128) + tcg
        xl = self.dma("q", x_rd, self.x_in[seq, tcg * 128:(tcg + 1) * 128, :],
                      self.d_xr, waits=(getattr(self, "_xr_g", None),))
        s_ = x_rd
        t1 = self.op("v", lambda h, o=s_, a=attnout, b=x_rd:
                     h.tensor_tensor(o, a, b, ALU.add),
                     waits=tuple(waits) + (xl,), inc=self.cv)
        nbk = (c.D + 511) // 512
        t2 = None
        for bi in range(nbk):
            w = min(512, c.D - bi * 512)
            t2 = self.op("v", lambda h, o=self.bn6[:, bi * 6:bi * 6 + 6],
                         i=s_[:, bi * 512:bi * 512 + w]: h.bn_stats(o, i),
                         waits=(t1,), inc=self.cv)
        t3 = self.op("v", lambda h: h.bn_aggr(self.colA[:, 0:2],
                                              self.bn6[:, 0:6 * nbk]),
                     waits=(t2,), inc=self.cv)
        t4 = self.op("v", lambda h: h.tensor_scalar(
            self.colA[:, 2:3], self.colA[:, 1:2], LN_EPS, None, ALU.add),
            waits=(t3,), inc=self.cv)
        t5 = self.op("v", lambda h: h.reciprocal(
            self.colA[:, 3:4], self.colA[:, 2:3]), waits=(t4,), inc=self.cv)
        t6 = self.op("s", lambda h: h.activation(
            self.colA[:, 4:5], self.colA[:, 3:4], AF.Ln), waits=(t5,), inc=self.cs)
        t7 = self.op("s", lambda h: h.activation(
            self.colA[:, 5:6], self.colA[:, 4:5], AF.Exp, scale=0.5),
            waits=(t6,), inc=self.cs)
        t8 = self.op("v", lambda h: h.tensor_scalar(
            self.colA[:, 6:7], self.colA[:, 0:1], self.colA[:, 5:6], -1.0,
            ALU.mult, ALU.mult), waits=(t7,), inc=self.cv)
        x1 = attnout
        t9 = self.op("s", lambda h, o=x1, i=s_: h.activation(
            o, i, AF.Identity, bias=self.colA[:, 6:7], scale=self.colA[:, 5:6]),
            waits=(t8,), inc=self.cs)
        w1 = self.dma("q", self.scr_x1tm[gq * 128:(gq + 1) * 128, :], x1,
                      self.d_x1, waits=(t9,))
        x1b = self.bf(self.ar7, 2048, c.D // 2)
        ta = self.op("v", lambda h, o=x1b, i=x1: h.tensor_copy(o, i),
                     waits=(t9, getattr(self, "_x1b_g", None)), inc=self.cv)
        evs = {}
        ngrp = (c.DC + 7) // 8
        for gj in range(ngrp):
            nt = min(8, c.DC - gj * 8)
            ev = self.transpose_group(
                [x1b[:, (gj * 8 + j) * 128:(gj * 8 + j + 1) * 128]
                 for j in range(nt)],
                self.bf(self.ar8, 1024 + gj * 512, nt * 64)
                .rearrange("p (n f) -> p n f", f=128),
                waits=(ta, getattr(self, "_x1s_g", None)))
            for j in range(nt):
                evs[gj * 8 + j] = ev
        last = None
        for d0 in range(0, c.DC, 4):
            k = min(4, c.DC - d0)
            src = self.bf(self.ar8, 1024 + d0 * 64, k * 64) \
                .rearrange("p (n f) -> p n f", f=128)
            dst = self.scr_x1fm[d0 * 128:(d0 + k) * 128,
                                gq * 128:(gq + 1) * 128] \
                .rearrange("(n p) f -> p n f", p=128)
            last = self.dma("q", dst, src, self.d_x1b, waits=(evs[d0],))
        self._x1s_g = last
        self._x1b_g = evs[(ngrp - 1) * 8]
        self._xr_g = t9
        self.ln_done[gq] = last
        return w1

    def _ffn(self):
        c = self.c
        TW = c.TWIN
        HC = c.HC
        NTC = TW // 128
        n1h = min(HC, self.N1 // (TW // 2))

        def hbuf(hc):
            if hc < n1h:
                return self.bf(self.ar1, hc * (TW // 2), TW // 2)
            return self.bf(self.ar2, 12288 + (hc - n1h) * (TW // 2), TW // 2)
        if HC > n1h:
            assert 12288 + (HC - n1h) * (TW // 2) <= self.N2
        x1w = self.bf(self.ar2, 0, c.DC * TW // 2) \
            .rearrange("p (dc t) -> p dc t", t=TW)
        ytm = [self.ar4[:, 0:c.D], self.ar4[:, 2048:2048 + c.D],
               self.ar5[:, 0:c.D], self.ar8[:, 0:c.D]][:NTC]
        h_g = [None] * HC
        ytm_g = [None] * NTC
        sg_g = [None, None]
        sg_cons = [None, None]
        sp_g = [None, None, None]
        sp_cons = [None, None, None]
        x1w_g = None
        zsb_g = [None, None]
        sq_g = [None, None]
        pend_sos = None
        sos_prev = None
        self._nrot = 2
        self._treg = 0
        self.flush()
        self.barrier()
        x1_all = (self.tok("d_x1"), self.tok("d_x1b"))
        for win in range(c.NWIN):
            tok0 = win * TW
            lnw = x1_all
            xw = None
            for d4 in range(c.DC // 4):
                xw = self.dma("q", x1w[:, d4 * 4:(d4 + 1) * 4, :],
                              bass.AP(self.scr_x1fm.tensor,
                                      self.scr_x1fm[d4 * 512:d4 * 512 + 128,
                                                    tok0:tok0 + TW].offset,
                                      [[c.NTOK, 128], [128 * c.NTOK, 4], [1, TW]]),
                              self.d_xw,
                              waits=lnw + (x1w_g,) + self.bar if d4 == 0 else ())
            for hc in range(HC):
                ssl = (hc // 2) % 2
                if hc % 2 == 0:
                    sgsem = [self.d_sg0, self.d_sg1][ssl]
                    gslab = self.bf(self.ar2, 4096 + ssl * 2048, c.DC * 128) \
                        .rearrange("p (dc f) -> p dc f", f=256)
                    uslab = self.bf(self.ar2, 8192 + ssl * 2048, c.DC * 128) \
                        .rearrange("p (dc f) -> p dc f", f=256)
                    w = (self.tok_quant, self.tok_quant1, sg_g[ssl],
                         sg_cons[ssl]) + self.bar
                    last = None
                    for four, (slab, col0) in enumerate(
                            ((gslab, hc * 128), (uslab, c.Hh + hc * 128))):
                        for dq in range(c.DC // 4):
                            last = self.dma(
                                "q", slab[:, dq * 4:(dq + 1) * 4, :],
                                bass.AP(self.scr_gu.tensor,
                                        self.scr_gu[dq * 512:dq * 512 + 128,
                                                    col0:col0 + 256].offset,
                                        [[c.H2, 128], [128 * c.H2, 4], [1, 256]]),
                                sgsem, waits=w if (four == 0 and dq == 0) else ())
                    sg_g[ssl] = last
                    cur = (gslab, uslab, last)
                gslab, uslab, slab_tok = cur
                hoff = (hc % 2) * 128
                qg, qu = 2 * (hc % 2), 2 * (hc % 2) + 1
                psg = self.PS1[:, qg * 512: qg * 512 + TW]
                psu = self.PS1[:, qu * 512: qu * 512 + TW]
                for dc in range(c.DC):
                    self.op("t", lambda h, o=psg, l=gslab[:, dc, hoff:hoff + 128],
                            r=x1w[:, dc, :], st=(dc == 0), sp=(dc == c.DC - 1):
                            h.matmul(o, l, r, start=st, stop=sp,
                                     skip_group_check=True),
                            waits=(xw, slab_tok, self.ps1q[qg]) + self.bar
                            if dc == 0 else (),
                            inc=self.ct if dc == c.DC - 1 else None)
                mg = self.tok("ct")
                for dc in range(c.DC):
                    self.op("t", lambda h, o=psu, l=uslab[:, dc, hoff:hoff + 128],
                            r=x1w[:, dc, :], st=(dc == 0), sp=(dc == c.DC - 1):
                            h.matmul(o, l, r, start=st, stop=sp,
                                     skip_group_check=True),
                            waits=(self.ps1q[qu],) if dc == 0 else (),
                            inc=self.ct if dc == c.DC - 1 else None)
                mu = self.tok("ct")
                sg_cons[ssl] = mu
                xw_cons = mu
                gsil = self.ar8[:, 2048:2048 + TW]
                t1 = self.op("s", lambda h, o=gsil, i=psg:
                             h.activation(o, i, AF.Silu),
                             waits=(mg, getattr(self, "_gsil_g", None)) + self.bar,
                             inc=self.cs)
                self.ps1q[qg] = t1
                hv = hbuf(hc)
                t2 = self.op("v", lambda h, o=hv, a=gsil, b=psu:
                             h.tensor_tensor(o, a, b, ALU.mult),
                             waits=(t1, mu, h_g[hc]) + self.bar, inc=self.cv)
                self._gsil_g = t2
                self.ps1q[qu] = t2
                h_g[hc] = t2
                sqh = self.bf(self.ar7, (hc % 2) * 256, TW // 2)
                t3 = self.op("v", lambda h, o=sqh, a=hv, b=hv:
                             h.tensor_tensor(o, a, b, ALU.mult),
                             waits=(t2, sq_g[hc % 2]), inc=self.cv)

                def sos_mm(hc=hc, t3=t3, sqh=sqh):
                    self.op("t", lambda h, o=self.PS2[0:1, 1024:1024 + TW], i=sqh,
                            st=(hc == 0), sp=(hc == HC - 1):
                            h.matmul(o, self.ones_col[:], i, start=st,
                                     stop=sp, skip_group_check=True),
                            waits=(t3,) + ((sos_prev,) if hc == 0 else ()),
                            inc=self.ct if hc == HC - 1 else None)
                    sq_g[hc % 2] = self.tok("ct")
                if pend_sos is not None:
                    pend_sos()
                pend_sos = sos_mm
            pend_sos()
            pend_sos = None
            sos_tok = self.tok("ct")
            r0 = self.op("v", lambda h: h.tensor_scalar(
                self.rowz_f[0:1, 0:TW], self.PS2[0:1, 1024:1024 + TW],
                1.0 / c.Hh, RMS_EPS, ALU.mult, ALU.add),
                waits=(sos_tok, getattr(self, "_rowz_g", None)), inc=self.cv)
            r1 = self.op("s", lambda h: h.activation(
                self.rowz_f[0:1, 0:TW], self.rowz_f[0:1, 0:TW], AF.Ln),
                waits=(r0,), inc=self.cs)
            r2 = self.op("s", lambda h: h.activation(
                self.rowz_f[0:1, 0:TW], self.rowz_f[0:1, 0:TW], AF.Exp, scale=-0.5),
                waits=(r1,), inc=self.cs)
            sos_prev = r2
            self._rowz_g = r2
            rc_t = []
            rcm_prev = getattr(self, "_rcm_g", None)
            for tcw in range(NTC):
                m = self.op("t", lambda h, o=self.PS2[:, 1536 + tcw:1537 + tcw],
                            l=self.rowz_f[0:1, tcw * 128:(tcw + 1) * 128]:
                            h.matmul(o, l, self.ones3[0:1, 0:1], start=True,
                                     stop=True, skip_group_check=True),
                            waits=(r2, rcm_prev), inc=self.ct)
                e = self.op("v", lambda h, o=self.colB[:, tcw:tcw + 1],
                            i=self.PS2[:, 1536 + tcw:1537 + tcw]:
                            h.tensor_copy(o, i), waits=(m,), inc=self.cv)
                rc_t.append(e)
            self._rcm_g = rc_t[-1]
            h_last = h_g[HC - 1]
            xw_cons_f = xw_cons
            for dc in range(c.DC):
                psl = dc % 3
                spsem = [self.d_sp0, self.d_sp1, self.d_sp2][psl]
                pslab = self.bf(self.ar2, [4096, 8192, 0][psl], HC * 64) \
                    .rearrange("p (hc f) -> p hc f", f=128)
                w0 = ((sp_g[psl], sp_cons[psl], mg, mu, xw_cons)
                      if dc < 3 else (sp_g[psl], sp_cons[psl]))
                sp_last = None
                for h8 in range(HC // 8):
                    sp_last = self.dma(
                        "q", pslab[:, h8 * 8:(h8 + 1) * 8, :],
                        bass.AP(self.scr_p.tensor,
                                self.scr_p[h8 * 1024:h8 * 1024 + 128,
                                           dc * 128:(dc + 1) * 128].offset,
                                [[c.D, 128], [128 * c.D, 8], [1, 128]]),
                        spsem, waits=w0 if h8 == 0 else ())
                sp_g[psl] = sp_last
                qz = dc % 4
                psz = self.PS1[:, qz * 512: qz * 512 + TW]
                for hc in range(HC):
                    self.op("t", lambda h, o=psz, l=pslab[:, hc, :], r=hbuf(hc),
                            st=(hc == 0), sp=(hc == HC - 1):
                            h.matmul(o, l, r, start=st, stop=sp,
                                     skip_group_check=True),
                            waits=(sp_last, h_last, self.ps1q[qz]) if hc == 0 else (),
                            inc=self.ct if hc == HC - 1 else None)
                mz = self.tok("ct")
                sp_cons[psl] = mz
                zsb = self.ar7[:, 1024 + (dc % 2) * 512:1024 + (dc % 2) * 512 + TW]
                ez = self.op("v", lambda h, o=zsb, i=psz: h.tensor_copy(o, i),
                             waits=(mz, zsb_g[dc % 2]), inc=self.cv)
                self.ps1q[qz] = ez
                trk = None
                for tcw in range(NTC):
                    r = self._treg = 1 - self._treg
                    pst = self.PS2[:, r * 512:r * 512 + 128]
                    trk = self.op("t", lambda h, o=pst,
                                  i=zsb[:, tcw * 128:(tcw + 1) * 128]:
                                  h.transpose(o, i, self.identf[:]),
                                  waits=(ez, self.trg[r]), inc=self.ct)
                    ya = self.op("s", lambda h, o=ytm[tcw][:, dc * 128:(dc + 1) * 128],
                                 i=pst, sc=self.colB[:, tcw:tcw + 1]:
                                 h.activation(o, i, AF.Identity, scale=sc),
                                 waits=(trk, rc_t[tcw], ytm_g[tcw]), inc=self.cs)
                    self.trg[r] = ya
                    if dc == c.DC - 1:
                        ytm_g[tcw] = ya
                zsb_g[dc % 2] = trk
            for tcw in range(NTC):
                gq = (tok0 // 128) + tcw
                seq = gq // (c.T // 128)
                tloc = gq % (c.T // 128)
                x1r = self.ar5[:, 2048:2048 + c.D]
                xl = self.dma("q", x1r, self.scr_x1tm[gq * 128:(gq + 1) * 128, :],
                              self.d_xr, waits=(getattr(self, "_xr_g", None),))
                s_ = self.ar6[:, 0:c.D]
                t1 = self.op("v", lambda h, o=s_, a=ytm[tcw], b=x1r:
                             h.tensor_tensor(o, a, b, ALU.add),
                             waits=(xl, ytm_g[tcw], self.st_tok[0], self.st_tok[1]),
                             inc=self.cv)
                self.st_tok = [t1, t1]
                nbk = (c.D + 511) // 512
                t2 = None
                for bi in range(nbk):
                    w = min(512, c.D - bi * 512)
                    t2 = self.op("v", lambda h, o=self.bn6[:, bi * 6:bi * 6 + 6],
                                 i=s_[:, bi * 512:bi * 512 + w]: h.bn_stats(o, i),
                                 waits=(t1,), inc=self.cv)
                t3 = self.op("v", lambda h: h.bn_aggr(self.colA[:, 0:2],
                                                      self.bn6[:, 0:6 * nbk]),
                             waits=(t2,), inc=self.cv)
                t4 = self.op("v", lambda h: h.tensor_scalar(
                    self.colA[:, 2:3], self.colA[:, 1:2], LN_EPS, None, ALU.add),
                    waits=(t3,), inc=self.cv)
                t5 = self.op("v", lambda h: h.reciprocal(
                    self.colA[:, 3:4], self.colA[:, 2:3]), waits=(t4,), inc=self.cv)
                t6 = self.op("s", lambda h: h.activation(
                    self.colA[:, 4:5], self.colA[:, 3:4], AF.Ln),
                    waits=(t5,), inc=self.cs)
                t7 = self.op("s", lambda h: h.activation(
                    self.colA[:, 5:6], self.colA[:, 4:5], AF.Exp, scale=0.5),
                    waits=(t6,), inc=self.cs)
                t8 = self.op("v", lambda h: h.tensor_scalar(
                    self.colA[:, 6:7], self.colA[:, 0:1], self.colA[:, 5:6], -1.0,
                    ALU.mult, ALU.mult), waits=(t7,), inc=self.cv)
                t9 = self.op("s", lambda h, o=ytm[tcw], i=s_: h.activation(
                    o, i, AF.Identity, bias=self.colA[:, 6:7],
                    scale=self.colA[:, 5:6]), waits=(t8,), inc=self.cs)
                ytm_g[tcw] = self.dma(
                    "q", self.out[seq, tloc * 128:(tloc + 1) * 128, :],
                    ytm[tcw], self.d_out, waits=(t9,))
                self._xr_g = t1
            x1w_g = [self.tok("ct"), sp_g[2]]

    def _finish(self):
        waits = tuple(self.tok(nm) for nm in self.cnt if nm.startswith("d_"))
        self.op("q", lambda h: h.nop(), waits=waits)


_BUILT = {}


def _get_nc(cfg_key="full"):
    if cfg_key not in _BUILT:
        c = Cfg()
        b = Builder(c)
        nc = b.build()
        _BUILT[cfg_key] = (c, nc)
    return _BUILT[cfg_key]


def kernel(**inputs):
    c, nc = _get_nc()
    consts = host_consts(c)
    x = np.asarray(inputs["L0_attn_x"])
    base = {
        "Wq": np.ascontiguousarray(inputs["L0_attn_Wq"], np.float32),
        "bq": np.ascontiguousarray(inputs["L0_attn_bq"], np.float32),
        "Wk": np.ascontiguousarray(inputs["L0_attn_Wk"], np.float32),
        "bk": np.ascontiguousarray(inputs["L0_attn_bk"], np.float32),
        "Wv": np.ascontiguousarray(inputs["L0_attn_Wv"], np.float32),
        "bv": np.ascontiguousarray(inputs["L0_attn_bv"], np.float32),
        "Wo": np.ascontiguousarray(inputs["L0_attn_Wo"], np.float32),
        "bo": np.ascontiguousarray(inputs["L0_attn_bo"], np.float32),
        "Wgu": np.ascontiguousarray(inputs["L0_ffn_w_gate_up"], np.float32),
        "Wp": np.ascontiguousarray(inputs["L0_ffn_w_proj"], np.float32),
    }
    base.update({"identb_in": consts["ident"], "identf_in": consts["identf"],
                 "onescol_in": consts["ones_col"], "gmat_in": consts["gmat"],
                 "cos_tab": consts["cos_tab"], "sins_tab": consts["sins_tab"]})
    in_maps = []
    for core in range(NCORES):
        m = dict(base)
        m["x"] = np.ascontiguousarray(
            x[core * c.SEQS:(core + 1) * c.SEQS], np.float32)
        in_maps.append(m)
    res = run_bass_kernel_spmd(nc, in_maps, core_ids=list(range(NCORES)))
    return np.concatenate([r["out"] for r in res.results], axis=0)


# version 15
# speedup vs baseline: 1.9792x; 1.0112x over previous
import contextlib
import numpy as np
import ml_dtypes
import concourse.bass as bass
import concourse.mybir as mybir
from concourse.bass_utils import run_bass_kernel_spmd

F32 = mybir.dt.float32
BF16 = mybir.dt.bfloat16
AF = mybir.ActivationFunctionType
ALU = mybir.AluOpType
AX = mybir.AxisListType

NCORES = 8
LN_EPS = 1e-5
RMS_EPS = 1.1920929e-7


class Cfg:
    def __init__(self, seqs=2, T=2048, D=2048, HQ=32, HKV=8, QB=1024, TWIN=512):
        self.SEQS, self.T, self.D, self.HQ, self.HKV = seqs, T, D, HQ, HKV
        self.HD = 64
        self.G = HQ // HKV
        self.DKV = HKV * self.HD
        self.Hh = 4 * D
        self.H2 = 8 * D
        self.QB, self.TWIN = QB, TWIN
        assert D == HQ * self.HD and T % QB == 0 and QB % 512 == 0
        self.DC = D // 128
        self.TC = T // 128
        self.KT = T // 128
        self.KVC = self.DKV // 128
        self.NQB = T // QB
        self.NTOK = seqs * T
        self.NWIN = self.NTOK // TWIN
        self.HC = self.Hh // 128
        lo = [h for h in range(HQ) if (h // self.G) % 2 == 0]
        hi = [h for h in range(HQ) if (h // self.G) % 2 == 1]
        self.pairs = list(zip(lo, hi))


def host_consts(c: Cfg):
    t = np.arange(c.T, dtype=np.float32)
    inv_freq = (1.0 / 10000.0 ** (np.arange(0, c.HD, 2, dtype=np.float32) / c.HD))
    ang = t[:, None].astype(np.float64) * inv_freq[None, :].astype(np.float64)
    cos = np.cos(ang).astype(np.float32)
    sin = np.sin(ang).astype(np.float32)
    return {
        "cos_tab": np.ascontiguousarray(np.concatenate([cos, cos], 1)),
        "sins_tab": np.ascontiguousarray(
            np.concatenate([-sin, sin], 1)).astype(np.float32),
        "ident": np.eye(128, dtype=ml_dtypes.bfloat16),
        "identf": np.eye(128, dtype=np.float32),
        "ones_col": np.ones((128, 1), dtype=ml_dtypes.bfloat16),
        "gmat": np.stack([(np.arange(c.HQ) // c.G == kv).astype(np.float32)
                          for kv in range(c.HKV)]),
    }


class Builder:
    ENG = {"t": "tensor", "v": "vector", "s": "scalar", "g": "gpsimd", "q": "sync"}

    def __init__(self, c: Cfg):
        self.c = c
        self.nc = bass.Bass("TRN2", target_bir_lowering=False)
        self.cnt = {}
        self.waited = {}
        self.pend = {k: [] for k in self.ENG}
        self.sems = {}
        self.bar = ()

    def sem(self, name):
        s = self.es.enter_context(self.nc.semaphore(name))
        self.sems[id(s)] = name
        self.cnt[name] = 0
        return s

    def op(self, eng, fn, waits=(), inc=None, incn=1):
        tok = None
        flat = []
        for w in waits:
            if w is None:
                continue
            if isinstance(w, list):
                flat.extend(x for x in w if x is not None)
            else:
                flat.append(w)
        waits = tuple(flat)
        if inc is not None:
            nm = self.sems[id(inc)]
            self.cnt[nm] += incn
            tok = (inc, self.cnt[nm])
        self.pend[eng].append((fn, waits, inc, incn))
        return tok

    def dma(self, eng, dst, src, sem, waits=()):
        return self.op(eng, lambda h, d=dst, s=src: h.dma_start(d, s),
                       waits=waits, inc=sem, incn=16)

    def tok(self, semname):
        return (getattr(self, semname), self.cnt[semname])

    def barrier(self):
        self.bar = tuple(self.tok(nm) for nm in self.cnt)

    def flush(self):
        if not any(self.pend.values()):
            return
        with self.nc.Block() as blk:
            for ename, attr in self.ENG.items():
                items = self.pend[ename]
                if not items:
                    continue
                self.pend[ename] = []

                def body(h, items=items, ename=ename):
                    for fn, waits, inc, incn in items:
                        for (sm, v) in waits:
                            if v <= 0:
                                continue
                            key = (ename, self.sems[id(sm)])
                            if self.waited.get(key, 0) >= v:
                                continue
                            h.wait_ge(sm, v)
                            self.waited[key] = v
                        instr = fn(h)
                        if inc is not None:
                            instr.then_inc(inc, incn)
                getattr(blk, attr)(body)

    def build(self):
        c, nc = self.c, self.nc
        self.es = contextlib.ExitStack()
        d = lambda n, s, ty, k: nc.dram_tensor(n, s, ty, kind=k).ap()
        self.x_in = d("x", [c.SEQS, c.T, c.D], F32, "ExternalInput")
        self.Wq = d("Wq", [c.D, c.D], F32, "ExternalInput")
        self.bq = d("bq", [c.D], F32, "ExternalInput")
        self.Wk = d("Wk", [c.DKV, c.D], F32, "ExternalInput")
        self.bk = d("bk", [c.DKV], F32, "ExternalInput")
        self.Wv = d("Wv", [c.DKV, c.D], F32, "ExternalInput")
        self.bv = d("bv", [c.DKV], F32, "ExternalInput")
        self.Wo = d("Wo", [c.D, c.D], F32, "ExternalInput")
        self.bo = d("bo", [c.D], F32, "ExternalInput")
        self.Wgu = d("Wgu", [c.H2, c.D], F32, "ExternalInput")
        self.Wp = d("Wp", [c.D, c.Hh], F32, "ExternalInput")
        self.cos_in = d("cos_tab", [c.T, c.HD], F32, "ExternalInput")
        self.sin_in = d("sins_tab", [c.T, c.HD], F32, "ExternalInput")
        self.ident_in = d("identb_in", [128, 128], BF16, "ExternalInput")
        self.identf_in = d("identf_in", [128, 128], F32, "ExternalInput")
        self.onescol_in = d("onescol_in", [128, 1], BF16, "ExternalInput")
        self.gmat_in = d("gmat_in", [c.HKV, c.HQ], F32, "ExternalInput")
        self.out = d("out", [c.SEQS, c.T, c.D], F32, "ExternalOutput")
        self.scr_gu = d("scr_gu", [c.D, c.H2], BF16, "Internal")
        self.scr_p = d("scr_p", [c.Hh, c.D], BF16, "Internal")
        self.scr_x1tm = d("scr_x1tm", [c.NTOK, c.D], F32, "Internal")
        self.scr_x1fm = d("scr_x1fm", [c.D, c.NTOK], BF16, "Internal")
        self.scr_sum = d("scr_sum", [2, c.QB], F32, "Internal")
        self.scr_den = d("scr_den", [c.HQ, c.QB], F32, "Internal")
        self.scr_rcp = d("scr_rcp", [2, c.QB], F32, "Internal")
        with self.es:
            self._alloc()
            self._program()
        return nc

    def _alloc(self):
        c, nc, es = self.c, self.nc, self.es
        sb = lambda n, s, ty: es.enter_context(nc.sbuf_tensor(n, s, ty))
        pt = lambda n, s, ty: es.enter_context(nc.psum_tensor(n, s, ty))
        need_attn = (c.KVC * c.T // 2 + (c.TC * c.HKV * 65 + 1) // 2
                     + max(2048, c.DC * 128)
                     + max(c.DC * c.DKV // 2, c.DC * 256) + 1024)
        self.N1 = max(need_attn, 14336)
        self.ar1 = sb("ar1", [128, self.N1], F32)
        self.N2 = max(c.DC * c.QB, 12800)
        self.ar2 = sb("ar2", [128, self.N2], F32)
        self.ar4 = sb("ar4", [128, 4096], F32)
        self.ar5 = sb("ar5", [128, 2 * 2048], F32)
        self.ar6 = sb("ar6", [128, 2048], F32)
        self.ar7 = sb("ar7", [128, 3072], F32)
        self.ar8 = sb("ar8", [128, 2560], F32)
        self.ident = sb("ident", [128, 128], BF16)
        self.identf = sb("identf", [128, 128], F32)
        self.ones_col = sb("ones_col", [128, 1], BF16)
        self.gmat = sb("gmat", [c.HKV, c.HQ], F32)
        self.ones3 = sb("ones3", [65, 128], F32)
        self.brow = sb("brow", [65, max(2 * c.DKV, c.D)], F32)
        self.bn6 = sb("bn6", [128, 24], F32)
        self.colA = sb("colA", [128, 8], F32)
        self.colB = sb("colB", [128, 8], F32)
        self.normQ = sb("normQ", [128, (c.QB // 128) * c.HQ], F32)
        self.normK = sb("normK", [128, c.TC * c.HKV], F32)
        self.biasmat = sb("biasmat", [128, c.HQ], F32)
        self.kmaxc = sb("kmaxc", [c.HKV, 2], F32)
        self.hqc = sb("hqc", [c.HQ, 8], F32)
        self.rcol = sb("rcol", [128, 48], F32)
        self.rowz = self.ar8[0:1, 0:1024]
        self.sums = self.ar8[0:c.HQ, 1024:1024 + c.QB]
        self.rowz_f = self.ar7[0:1, 512:1024]
        self.PS1 = pt("PS1", [128, 2048], F32)
        self.PS2 = pt("PS2", [128, 2048], F32)
        for nm in ("ct", "cv", "cs", "cg", "d_cst", "d_st0", "d_st1", "d_ws",
                   "d_ws1", "d_x1", "d_x1b", "d_xr", "d_su0", "d_su1", "d_rb0", "d_rb1",
                   "d_xw", "d_sg0", "d_sg1", "d_sp0", "d_sp1", "d_sp2", "d_out"):
            setattr(self, nm, self.sem(nm))
        self.st_tok = [None, None]
        self.lc_g = [None, None]
        self.ps1q = [None, None, None, None]
        self.trg = [None, None, None]
        self._nrot = 3
        self._treg = 0
        self._stsl = 0

    def bf(self, ar, off, n, p0=0, p1=128):
        return ar[p0:p1, off:off + n].bitcast(BF16)

    def ps1(self, i):
        return self.PS1[:, i * 512:(i + 1) * 512]

    def transpose_group(self, tiles, dst, waits=(), f32=False, evac=None):
        r = self._treg = (self._treg + 1) % self._nrot
        nt = len(tiles)
        if f32:
            assert nt <= 4
            ps = self.PS2[:, r * 512:r * 512 + nt * 128]
            ident = self.identf
        else:
            assert nt <= 8
            ps = self.PS2[:, r * 512:(r + 1) * 512].bitcast(BF16)[:, 0:nt * 128]
            ident = self.ident
        tk = None
        w0 = tuple(waits) + (self.trg[r],) + self.bar
        for j, s in enumerate(tiles):
            o = ps[:, j * 128:(j + 1) * 128]
            tk = self.op("t", lambda h, o=o, i=s, idn=ident:
                         h.transpose(o, i, idn[:]),
                         waits=w0 if j == 0 else (), inc=self.ct)
        if evac is None:
            pv = ps.rearrange("p (n f) -> p n f", f=128)
            ev = self.op("v", lambda h, d=dst, s=pv: h.tensor_copy(d, s),
                         waits=(tk,), inc=self.cv)
        else:
            ev = evac(ps, tk)
        self.trg[r] = ev
        return ev

    def load_cast(self, src, n, waits=(), eng="q", dst_off=None):
        sl = self._stsl = 1 - self._stsl
        st = [self.d_st0, self.d_st1][sl]
        stage = self.ar6[:, sl * 1024: sl * 1024 + n]
        ld = self.dma(eng, stage, src, st, waits=(self.st_tok[sl],) + self.bar)
        off = sl * 512 if dst_off is None else dst_off
        dst = self.bf(self.ar7, off, n // 2)
        gate = self.lc_g[sl] if dst_off is None else None
        cs = self.op("s", lambda h, d=dst, s=stage: h.copy(d, s),
                     waits=(ld, gate) + tuple(waits) + self.bar, inc=self.cs)
        self.st_tok[sl] = cs
        self._lc_sl = sl
        return dst, cs

    def build_wT(self, W, wT, row0, ncols):
        c = self.c
        ev = None
        for rch in range(ncols // 128):
            for hf in range((c.D + 1023) // 1024):
                w = min(1024, c.D - hf * 1024)
                src = W[row0 + rch * 128: row0 + rch * 128 + 128,
                        hf * 1024: hf * 1024 + w]
                wbf, cs = self.load_cast(src, w)
                sl = self._lc_sl
                tiles = [wbf[:, j * 128:(j + 1) * 128] for j in range(w // 128)]
                dst = wT[:, hf * 8: hf * 8 + w // 128, rch * 128: rch * 128 + 128]
                ev = self.transpose_group(tiles, dst, waits=(cs,))
                self.lc_g[sl] = ev
        return ev

    def _program(self):
        self._consts()
        self.flush()
        self.barrier()
        self._quant()
        self.ln_done = {}
        for seq in range(self.c.SEQS):
            self._attn_seq(seq)
        self.flush()
        self.barrier()
        self.quant_drain()
        self._ffn()
        self._finish()
        self.flush()

    def _consts(self):
        c = self.c
        t = lambda d_, s_: self.dma("q", d_, s_, self.d_cst)
        t(self.ident[:], self.ident_in)
        t(self.identf[:], self.identf_in)
        t(self.ones_col[:], self.onescol_in)
        t(self.gmat[:], self.gmat_in)
        self.o_cos, self.o_sin = 0, c.TC * c.HD
        t(self.ar4[:, self.o_cos:self.o_cos + c.TC * c.HD]
          .rearrange("p (tc f) -> p tc f", f=c.HD),
          self.cos_in.rearrange("(tc p) f -> p tc f", p=128))
        t(self.ar4[:, self.o_sin:self.o_sin + c.TC * c.HD]
          .rearrange("p (tc f) -> p tc f", f=c.HD),
          self.sin_in.rearrange("(tc p) f -> p tc f", p=128))
        t(self.brow[0:1, 0:c.D], self.bq.rearrange("(o d) -> o d", o=1))
        t(self.brow[32:33, 0:c.D], self.bo.rearrange("(o d) -> o d", o=1))
        t(self.brow[64:65, 0:c.DKV], self.bk.rearrange("(o d) -> o d", o=1))
        t(self.brow[64:65, c.DKV:2 * c.DKV], self.bv.rearrange("(o d) -> o d", o=1))
        self.tok_consts = self.tok("d_cst")
        self.tok_ones = self.op("g", lambda h: h.memset(self.ones3[:], 1.0),
                                inc=self.cg)

    def quant_step(self, n=1):
        g = getattr(self, "_qgen", None)
        if g is None:
            return
        for _ in range(n):
            try:
                next(g)
            except StopIteration:
                self._qgen = None
                self.tok_quant = self.tok("d_ws")
                self.tok_quant1 = self.tok("d_ws1")
                return

    def quant_drain(self):
        while getattr(self, "_qgen", None) is not None:
            self.quant_step(8)

    def _quant(self):
        self._qgen = self._quant_gen()
        self.quant_step(5)

    def _quant_gen(self):
        c = self.c
        WBO = [1024, 1536]
        RQ = [2048, 2560]
        STG = [2048, 2560]
        SC = 2048
        cons = [None, None]
        wsd = [None, None]

        def piece(idx, src, dst_fn, n):
            sl = idx % 2
            g = n // 64
            wb, tcast = self.load_cast(src, n, waits=(cons[sl],),
                                       dst_off=WBO[sl])
            wbv = wb.rearrange("p (g k) -> p g k", k=64)
            sums = self.ar8[:, SC + sl * 256: SC + sl * 256 + g]
            sclbf = self.bf(self.ar8, SC + sl * 256 + 32, 8)[:, 0:g]
            rcp = self.ar8[:, SC + sl * 256 + 48: SC + sl * 256 + 48 + g]
            t1 = self.op("v", lambda h, o=sums, i=wbv: h.reduce_sum(
                o.rearrange("p (g o2) -> p g o2", o2=1), i, axis=AX.X,
                apply_absolute_value=True), waits=(tcast,), inc=self.cv)
            t2 = self.op("v", lambda h, o=sclbf, i=sums: h.tensor_scalar(
                o, i, 1.0 / 64, 1e-8, ALU.mult, ALU.max), waits=(t1,), inc=self.cv)
            t3 = self.op("v", lambda h, o=rcp, i=sclbf: h.reciprocal(o, i),
                         waits=(t2,), inc=self.cv)
            r = self.bf(self.ar7, RQ[sl], n // 2)
            rv = r.rearrange("p (g k) -> p g k", k=64)
            rcpb = bass.AP(rcp.tensor, rcp.offset,
                           [list(rcp.ap[0]), [1, g], [0, 64]])
            t4 = self.op("g", lambda h, o=rv, a=wbv, b=rcpb:
                         h.tensor_tensor(o, a, b, ALU.mult),
                         waits=(t3, wsd[sl]) + self.bar, inc=self.cg)
            t5 = self.op("v", lambda h, o=r, i=r: h.tensor_scalar(
                o, i, 192.0, None, ALU.add), waits=(t4,), inc=self.cv)
            t6 = self.op("v", lambda h, o=r, i=r: h.tensor_scalar(
                o, i, -192.0, 1.0, ALU.add, ALU.min), waits=(t5,), inc=self.cv)
            t7 = self.op("v", lambda h, o=r, i=r: h.tensor_scalar(
                o, i, -1.0, None, ALU.max), waits=(t6,), inc=self.cv)
            sclbb = bass.AP(sclbf.tensor, sclbf.offset,
                            [list(sclbf.ap[0]), [1, g], [0, 64]])
            t8 = self.op("g", lambda h, o=wbv, a=rv, b=sclbb:
                         h.tensor_tensor(o, a, b, ALU.mult), waits=(t7,), inc=self.cg)

            def stage2(sl=sl, wb=wb, t8=t8, dst_fn=dst_fn, n=n):
                nt = n // 128
                stg = self.bf(self.ar7, STG[sl], nt * 64)
                stgv = stg.rearrange("p (t f) -> p t f", f=128)
                ev = self.transpose_group(
                    [wb[:, j * 128:(j + 1) * 128] for j in range(nt)], stgv,
                    waits=(t8, wsd[sl]))
                cons[sl] = ev
                self.lc_g[sl] = ev
                last = None
                wsem = [self.d_ws, self.d_ws1][sl]
                for j4 in range(0, nt, 4):
                    k = min(4, nt - j4)
                    last = self.dma("g" if sl else "q", dst_fn(j4, k),
                                    stgv[:, j4:j4 + k, :], wsem, waits=(ev,))
                wsd[sl] = last
            return stage2

        idx = 0
        wcol = min(1024, c.D)
        for rch in range(c.H2 // 128):
            for jc in range(c.D // wcol):
                def dgu(j4, k, rch=rch, jc=jc):
                    r0 = jc * wcol + j4 * 128
                    return self.scr_gu[r0:r0 + k * 128, rch * 128:(rch + 1) * 128] \
                        .rearrange("(n p) f -> p n f", p=128)
                st2 = piece(idx, self.Wgu[rch * 128:(rch + 1) * 128,
                                          jc * wcol:(jc + 1) * wcol], dgu, wcol)
                idx += 1
                yield
                st2()
                yield
        wcol2 = min(1024, c.Hh)
        for rch in range(c.D // 128):
            for jc in range(c.Hh // wcol2):
                def dp(j4, k, rch=rch, jc=jc):
                    r0 = jc * wcol2 + j4 * 128
                    return self.scr_p[r0:r0 + k * 128, rch * 128:(rch + 1) * 128] \
                        .rearrange("(n p) f -> p n f", p=128)
                st2 = piece(idx, self.Wp[rch * 128:(rch + 1) * 128,
                                         jc * wcol2:(jc + 1) * wcol2], dp, wcol2)
                idx += 1
                yield
                st2()
                yield

    def _attn_seq(self, seq):
        c = self.c
        self.o_kfm = 0
        self.o_vh = c.KVC * c.T // 2
        self.o_xw = self.o_vh + (c.TC * c.HKV * 65 + 1) // 2
        self.o_wt = self.o_xw + max(2048, c.DC * 128)
        self.o_tm = self.o_wt + max(c.DC * c.DKV // 2, c.DC * 256)
        kfm = self.bf(self.ar1, self.o_kfm, c.KVC * c.T // 2) \
            .rearrange("p (kc t) -> p kc t", t=c.T)
        nvh = c.TC * c.HKV * 65
        vh = self.bf(self.ar1, self.o_vh, (nvh + 1) // 2)[:, 0:nvh] \
            .rearrange("p (tc kv f) -> p tc kv f", kv=c.HKV, f=65)
        self.flush()
        self.barrier()
        self._s1kv(seq, kfm, vh)
        self.flush()
        for qb in range(c.NQB):
            qfm = self.bf(self.ar2, 0, c.DC * c.QB // 2) \
                .rearrange("p (dc t) -> p dc t", t=c.QB)
            ofm = self.bf(self.ar2, c.DC * c.QB // 2, c.DC * c.QB // 2) \
                .rearrange("p (dc t) -> p dc t", t=c.QB)
            self.barrier()
            self._s1q(seq, qb, qfm)
            self.flush()
            self.barrier()
            self._biasmat()
            self.flush()
            self.barrier()
            self._s2(seq, qb, qfm, kfm, vh, ofm)
            self.flush()
            self.barrier()
            self._s3(seq, qb, ofm)
            self.flush()

    def _xtile(self, seq, tc, off, gate=None):
        c = self.c
        dst = self.bf(self.ar1, off, c.DC * 64) \
            .rearrange("p (dc f) -> p dc f", f=128)
        ev = None
        for hf in range((c.D + 1023) // 1024):
            w = min(1024, c.D - hf * 1024)
            src = self.x_in[seq, tc * 128:(tc + 1) * 128, hf * 1024:hf * 1024 + w]
            xbf, cs = self.load_cast(src, w, waits=(gate,))
            sl = self._lc_sl
            tiles = [xbf[:, j * 128:(j + 1) * 128] for j in range(w // 128)]
            ev = self.transpose_group(tiles, dst[:, hf * 8: hf * 8 + w // 128, :],
                                      waits=(cs,))
            self.lc_g[sl] = ev
        return dst, ev

    def _rope(self, ps, dstview, tc, n, waits):
        par = list(self.ar4[:].ap[0])
        cosb = bass.AP(self.ar4[:].tensor, self.ar4[:, self.o_cos + tc * 64].offset,
                       [par, [0, n], [1, 64]])
        sinb = bass.AP(self.ar4[:].tensor, self.ar4[:, self.o_sin + tc * 64].offset,
                       [par, [0, n], [32, 2], [1, 32]])
        psv = ps.rearrange("p (h f) -> p h f", f=64)
        psw = bass.AP(ps.tensor, ps.offset + 32,
                      [list(ps.ap[0]), [64, n], [-32, 2], [1, 32]])
        ms = self._mrot = 1 - getattr(self, "_mrot", 0)
        m1 = self.ar8[:, ms * 1024:ms * 1024 + n * 64]
        m2 = self.ar8[:, ms * 1024 + 512:ms * 1024 + 512 + n * 64]
        mg = getattr(self, "_rope_gates", [None, None])
        a = self.op("v", lambda h: h.tensor_tensor(
            m1.rearrange("p (h f) -> p h f", f=64), psv, cosb, ALU.mult),
            waits=tuple(waits) + (mg[ms],) + self.bar, inc=self.cv)
        b = self.op("v", lambda h: h.tensor_tensor(
            m2.rearrange("p (h two f) -> p h two f", two=2, f=32), psw, sinb,
            ALU.mult), inc=self.cv)
        e = self.op("v", lambda h: h.tensor_tensor(
            dstview, m1.rearrange("p (h f) -> p h f", f=64),
            m2.rearrange("p (h f) -> p h f", f=64), ALU.add),
            waits=(a, b), inc=self.cv)
        mg[ms] = e
        self._rope_gates = mg
        return e

    def _s1kv(self, seq, kfm, vh):
        c = self.c
        vones = self.op("g", lambda h: h.memset(
            bass.AP(vh.tensor, vh[:, 0, 0, 64:65].offset,
                    [list(vh.ap[0]), [65, c.TC * c.HKV], [1, 1]]), 1.0),
            waits=(self.tok_consts,) + self.bar, inc=self.cg)
        wkT = self.bf(self.ar1, self.o_wt, c.DC * c.DKV // 2) \
            .rearrange("p (dc f) -> p dc f", f=c.DKV)
        wktok = self.build_wT(self.Wk, wkT, 0, c.DKV)
        wvT = self.bf(self.ar2, 0, c.DC * c.DKV // 2) \
            .rearrange("p (dc f) -> p dc f", f=c.DKV)
        wvtok = self.build_wT(self.Wv, wvT, 0, c.DKV)
        self._ktm_g = [None, None]
        for tc in range(c.TC):
            sl = tc % 2
            xt, xtok = self._xtile(seq, tc, self.o_xw + sl * 1024,
                                   gate=getattr(self, "_xt_g", [None, None])[sl])
            ps = self.PS1[:, sl * 1024: sl * 1024 + c.DKV]
            psv = self.PS1[:, sl * 1024 + c.DKV: sl * 1024 + 2 * c.DKV]
            for dc in range(c.DC):
                self.op("t", lambda h, o=ps, l=xt[:, dc, :], r=wkT[:, dc, :],
                        st=(dc == 0):
                        h.matmul(o, l, r, start=st, stop=False,
                                 skip_group_check=True),
                        waits=(xtok, wktok, wvtok, self.ps1q[2 * sl],
                               self.ps1q[2 * sl + 1])
                        + self.bar if dc == 0 else ())
            mm = self.op("t", lambda h, o=ps: h.matmul(
                o, self.ones3[64:65, :], self.brow[64:65, 0:c.DKV],
                start=False, stop=True, skip_group_check=True),
                waits=(self.tok_consts, self.tok_ones), inc=self.ct)
            self.quant_step(2)
            for dc in range(c.DC):
                self.op("t", lambda h, o=psv, l=xt[:, dc, :], r=wvT[:, dc, :],
                        st=(dc == 0):
                        h.matmul(o, l, r, start=st, stop=False,
                                 skip_group_check=True))
            mmv = self.op("t", lambda h, o=psv: h.matmul(
                o, self.ones3[64:65, :], self.brow[64:65, c.DKV:2 * c.DKV],
                start=False, stop=True, skip_group_check=True), inc=self.ct)
            self.quant_step(2)
            xg = getattr(self, "_xt_g", [None, None])
            xg[sl] = mmv
            self._xt_g = xg
            sq = self.ar5[:, 0:c.DKV]
            s1 = self.op("s", lambda h, o=sq, i=ps: h.activation(o, i, AF.Square),
                         waits=(mm, getattr(self, "_sq_g", None)), inc=self.cs)
            nk = self.normK[:, tc * c.HKV:(tc + 1) * c.HKV]
            self._sq_g = self.op(
                "v", lambda h, o=nk, i=sq.rearrange("p (kv f) -> p kv f", f=64):
                h.reduce_sum(o.rearrange("p (kv o2) -> p kv o2", o2=1), i,
                             axis=AX.X), waits=(s1,), inc=self.cv)
            self.quant_step(2)
            ktm = self.bf(self.ar1, self.o_tm + sl * 512, c.DKV // 2)
            rt = self._rope(ps, ktm.rearrange("p (h f) -> p h f", f=64),
                            tc, c.HKV, (mm, s1, self._ktm_g[sl]))
            self.ps1q[2 * sl] = rt
            ev = self.transpose_group(
                [ktm[:, j * 128:(j + 1) * 128] for j in range(c.KVC)],
                bass.AP(kfm.tensor, kfm[:, 0, tc * 128].offset,
                        [list(kfm.ap[0]), [c.T, c.KVC], [1, 128]]),
                waits=(rt,))
            self._ktm_g[sl] = ev
            self.quant_step(2)
            dstv = bass.AP(vh.tensor, vh[:, tc, 0, 0].offset,
                           [list(vh.ap[0]), [65, c.HKV], [1, 64]])
            evv = self.op("v", lambda h, o=dstv,
                          i=psv.rearrange("p (kv f) -> p kv f", f=64):
                          h.tensor_copy(o, i), waits=(mmv, vones), inc=self.cv)
            self.ps1q[2 * sl + 1] = evv
            self.quant_step(2)
        self.vh_tok = self.tok("cv")
        nkv = bass.AP(self.normK[:].tensor, self.normK[:].offset,
                      [list(self.normK[:].ap[0]), [1, c.HKV], [c.HKV, c.TC]])
        t1 = self.op("v", lambda h: h.reduce_max(
            self.colA[:, 0:c.HKV].rearrange("p (kv o2) -> p kv o2", o2=1), nkv,
            axis=AX.X), waits=(self._sq_g,), inc=self.cv)
        t2 = self.op("t", lambda h: h.transpose(
            self.PS2[0:c.HKV, 1536:1664], self.colA[:, 0:c.HKV], self.identf[:]),
            waits=(t1,), inc=self.ct)
        t3 = self.op("v", lambda h: h.reduce_max(
            self.kmaxc[:, 0:1].rearrange("p (o o2) -> p o o2", o2=1),
            self.PS2[0:c.HKV, 1536:1664].rearrange("p (o f) -> p o f", o=1),
            axis=AX.X), waits=(t2,), inc=self.cv)
        self.kmax_tok = t3

    def _s1q(self, seq, qb, qfm):
        c = self.c
        qf_tok = None
        self._qtm_g = [None, None, None, None]
        NQT = c.QB // 128
        xqb = self.bf(self.ar2, c.DC * c.QB // 2, c.DC * c.QB // 2) \
            .rearrange("p (dc t) -> p dc t", t=c.QB)
        xtoks = []
        for tq in range(NQT):
            tcg = qb * NQT + tq
            dst = xqb[:, :, tq * 128:(tq + 1) * 128]
            xtok = None
            for hf in range((c.D + 1023) // 1024):
                w = min(1024, c.D - hf * 1024)
                src = self.x_in[seq, tcg * 128:(tcg + 1) * 128,
                                hf * 1024:hf * 1024 + w]
                xbf, cs = self.load_cast(src, w)
                sl_ = self._lc_sl
                xtok = self.transpose_group(
                    [xbf[:, j * 128:(j + 1) * 128] for j in range(w // 128)],
                    dst[:, hf * 8:hf * 8 + w // 128, :], waits=(cs,))
                self.lc_g[sl_] = xtok
                self.quant_step(1)
            xtoks.append(xtok)
            self.quant_step(1)
        for d8p in range(c.HQ // 8):
            wq8 = self.bf(self.ar1, self.o_wt, c.DC * 256) \
                .rearrange("p (dc f) -> p dc f", f=512)
            wtok = self.build_wT(self.Wq, wq8, d8p * 512, 512)
            for tq in range(NQT):
                tcg = qb * NQT + tq
                sl = tq % 4
                ps = self.ps1(sl)
                for dc in range(c.DC):
                    self.op("t", lambda h, o=ps,
                            l=xqb[:, dc, tq * 128:(tq + 1) * 128],
                            r=wq8[:, dc, :], st=(dc == 0):
                            h.matmul(o, l, r, start=st, stop=False,
                                     skip_group_check=True),
                            waits=(xtoks[tq], wtok, self.ps1q[sl])
                            if dc == 0 else ())
                mm = self.op("t", lambda h, o=ps, d0=d8p * 512: h.matmul(
                    o, self.ones3[0:1, :], self.brow[0:1, d0:d0 + 512],
                    start=False, stop=True, skip_group_check=True), inc=self.ct)
                self.quant_step(1)
                sq = self.ar5[:, 0:512]
                s1 = self.op("s", lambda h, o=sq, i=ps: h.activation(
                    o, i, AF.Square),
                    waits=(mm, getattr(self, "_sq_g", None)), inc=self.cs)
                nq = self.normQ[:, tq * c.HQ + d8p * 8: tq * c.HQ + d8p * 8 + 8]
                self._sq_g = self.op(
                    "v", lambda h, o=nq,
                    i=sq.rearrange("p (g f) -> p g f", f=64):
                    h.reduce_sum(o.rearrange("p (g o2) -> p g o2", o2=1), i,
                                 axis=AX.X), waits=(s1,), inc=self.cv)
                qt_b = self.bf(self.ar1, self.o_tm + sl * 256, 256)
                dstv = bass.AP(qt_b.tensor, qt_b.offset,
                               [list(qt_b.ap[0]), [64, 2], [128, 4], [1, 64]])
                rt = self._rope(ps, dstv, tcg, 8,
                                (mm, s1, self._qtm_g[sl]))
                self.quant_step(1)
                self.ps1q[sl] = rt
                ev = self.transpose_group(
                    [qt_b[:, j * 128:(j + 1) * 128] for j in range(4)],
                    qfm[:, 4 * d8p:4 * d8p + 4, tq * 128:(tq + 1) * 128],
                    waits=(rt,))
                self._qtm_g[sl] = ev
                qf_tok = ev
                self.quant_step(1)
        self.q_ready = qf_tok

    def _biasmat(self):
        c = self.c
        NQT = c.QB // 128
        nqv = bass.AP(self.normQ[:].tensor, self.normQ[:].offset,
                      [list(self.normQ[:].ap[0]), [1, c.HQ], [c.HQ, NQT]])
        qmax_s = self.ar8[:, 512:512 + c.HQ]
        t1 = self.op("v", lambda h: h.reduce_max(
            qmax_s.rearrange("p (hq o2) -> p hq o2", o2=1), nqv, axis=AX.X),
            waits=self.bar, inc=self.cv)
        t2 = self.op("t", lambda h: h.transpose(
            self.PS2[0:c.HQ, 1536:1664], qmax_s, self.identf[:]),
            waits=(t1,) + self.bar, inc=self.ct)
        t3 = self.op("v", lambda h: h.reduce_max(
            self.hqc[:, 0:1].rearrange("p (o o2) -> p o o2", o2=1),
            self.PS2[0:c.HQ, 1536:1664].rearrange("p (o f) -> p o f", o=1),
            axis=AX.X), waits=(t2,), inc=self.cv)
        t4 = self.op("t", lambda h: h.matmul(
            self.PS2[0:c.HQ, 1664:1665], self.gmat[:], self.kmaxc[:, 0:1],
            start=True, stop=True, skip_group_check=True),
            waits=(self.kmax_tok, t3), inc=self.ct)
        t5 = self.op("v", lambda h: h.tensor_copy(
            self.hqc[:, 1:2], self.PS2[0:c.HQ, 1664:1665]), waits=(t4,), inc=self.cv)
        t6 = self.op("v", lambda h: h.tensor_tensor(
            self.hqc[:, 2:3], self.hqc[:, 0:1], self.hqc[:, 1:2], ALU.mult),
            waits=(t5,), inc=self.cv)
        t6b = self.op("v", lambda h: h.tensor_scalar(
            self.hqc[:, 2:3], self.hqc[:, 2:3], 1e-36, None, ALU.max),
            waits=(t6,), inc=self.cv)
        t7 = self.op("s", lambda h: h.activation(
            self.hqc[:, 3:4], self.hqc[:, 2:3], AF.Ln),
            waits=(t6b,) + self.bar, inc=self.cs)
        t8 = self.op("s", lambda h: h.activation(
            self.hqc[:, 4:5], self.hqc[:, 3:4], AF.Exp, scale=0.5),
            waits=(t7,), inc=self.cs)
        t9 = self.op("v", lambda h: h.tensor_scalar(
            self.hqc[:, 5:6], self.hqc[:, 4:5], -0.125, None, ALU.mult),
            waits=(t8,), inc=self.cv)
        ta = self.op("t", lambda h: h.transpose(
            self.PS2[0:1, 1536:1536 + c.HQ], self.hqc[0:c.HQ, 5:6],
            self.identf[0:c.HQ, 0:c.HQ]), waits=(t9,), inc=self.ct)
        tb = self.op("v", lambda h: h.tensor_copy(
            self.rowz[0:1, 0:c.HQ], self.PS2[0:1, 1536:1536 + c.HQ]),
            waits=(ta, getattr(self, "_rowz_g", None)), inc=self.cv)
        tc_ = self.op("t", lambda h: h.matmul(
            self.PS2[:, 1536:1536 + c.HQ], self.ones3[0:1, :],
            self.rowz[0:1, 0:c.HQ], start=True, stop=True,
            skip_group_check=True), waits=(tb,), inc=self.ct)
        td = self.op("v", lambda h: h.tensor_copy(
            self.biasmat[:], self.PS2[:, 1536:1536 + c.HQ]),
            waits=(tc_,), inc=self.cv)
        self._rowz_g = tc_
        self.biasmat_tok = td

    def _s2(self, seq, qb, qfm, kfm, vh, ofm):
        c = self.c
        QB = c.QB
        NJ = QB // 512
        atb_off = [0, QB // 2, 2048, 2048 + QB // 2]
        atb = [[self.bf(self.ar7 if s == 0 else self.ar4,
                        atb_off[2 * s + r], QB // 2)
                for r in range(2)] for s in range(2)]
        exp_g = [None, None]
        av_g = [[None, None], [None, None]]
        o_g = [None, None]
        for ip, (hlo, hhi) in enumerate(c.pairs):
            kvl, kvh_ = hlo // c.G, hhi // c.G
            av_last = [None, None]

            def emit_av(kt, side, kv, first_extra=()):
                rot = kt % 2
                for jh in range(NJ):
                    w = ([exp_g[side]] + list(first_extra)) if jh == 0 else ()
                    self.op("t", lambda h,
                            o=self.PS2[0:65, side * 1024 + jh * 512:
                                       side * 1024 + (jh + 1) * 512],
                            l=vh[:, kt, kv, :],
                            r=atb[side][rot][:, jh * 512:(jh + 1) * 512],
                            st=(kt == 0), sp=(kt == c.KT - 1):
                            h.matmul(o, l, r, start=st, stop=sp,
                                     skip_group_check=True),
                            waits=tuple(w), inc=self.ct if jh == NJ - 1 else None)
                av_g[side][rot] = av_last[side] = self.tok("ct")

            pend_av = None
            for kt in range(c.KT):
                rot = kt % 2
                pslo = self.PS1[:, 0:QB]
                pshi = self.PS1[:, 1024:1024 + QB]
                for jh in range(NJ):
                    w = ((self.q_ready, self.biasmat_tok, self.vh_tok, exp_g[0])
                         + self.bar if jh == 0 else ())
                    self.op("t", lambda h, o=pslo[:, jh * 512:(jh + 1) * 512],
                            l=kfm[0:64, kvl // 2, kt * 128:(kt + 1) * 128],
                            r=qfm[0:64, ip, jh * 512:(jh + 1) * 512]:
                            h.matmul(o, l, r, start=True, stop=True,
                                     tile_position=(0, 0), skip_group_check=True),
                            waits=w, inc=self.ct if jh == NJ - 1 else None)
                mlo = self.tok("ct")
                for jh in range(NJ):
                    w = ((exp_g[1],) if jh == 0 else ())
                    self.op("t", lambda h, o=pshi[:, jh * 512:(jh + 1) * 512],
                            l=kfm[64:128, kvh_ // 2, kt * 128:(kt + 1) * 128],
                            r=qfm[64:128, ip, jh * 512:(jh + 1) * 512]:
                            h.matmul(o, l, r, start=True, stop=True,
                                     tile_position=(64, 0), skip_group_check=True),
                            waits=w, inc=self.ct if jh == NJ - 1 else None)
                mhi = self.tok("ct")
                if pend_av is not None:
                    pend_av()
                elo = self.op("s", lambda h, o=atb[0][rot], i=pslo,
                              b=self.biasmat[:, hlo:hlo + 1]:
                              h.activation(o, i, AF.Exp, bias=b, scale=0.125),
                              waits=(mlo, av_g[0][rot]) + self.bar, inc=self.cs)
                ehi = self.op("s", lambda h, o=atb[1][rot], i=pshi,
                              b=self.biasmat[:, hhi:hhi + 1]:
                              h.activation(o, i, AF.Exp, bias=b, scale=0.125),
                              waits=(mhi, av_g[1][rot]), inc=self.cs)
                exp_g = [elo, ehi]

                def pend(kt=kt, fe0=(o_g[0],) if kt == 0 else (),
                         fe1=(o_g[1],) if kt == 0 else ()):
                    emit_av(kt, 0, kvl, fe0)
                    emit_av(kt, 1, kvh_, fe1)
                pend_av = pend
            pend_av()
            for side, hd in ((0, hlo), (1, hhi)):
                psO = self.PS2[:, side * 1024: side * 1024 + QB]
                ssu = [self.d_su0, self.d_su1][side]
                dchunk, roff = hd // 2, (hd % 2) * 64
                t0 = self.op("v", lambda h, o=ofm[roff:roff + 64, dchunk, :],
                             i=psO[0:64, :]: h.tensor_copy(o, i),
                             waits=(av_last[side],), inc=self.cv)
                t1 = self.op("v", lambda h, o=self.rowz[0:1, 0:QB],
                             i=psO[64:65, :]: h.tensor_copy(o, i),
                             waits=(t0, getattr(self, "_rowz_g", None)),
                             inc=self.cv)
                o_g[side] = t1
                dsum = self.dma("q", self.sums[hd:hd + 1, 0:QB],
                                self.rowz[0:1, 0:QB], ssu, waits=(t1,))
                self._rowz_g = dsum
        rcpv = self.op("v", lambda h: h.reciprocal(
            self.sums[0:c.HQ, 0:QB], self.sums[0:c.HQ, 0:QB]),
            waits=(self.tok("d_su0"), self.tok("d_su1")), inc=self.cv)
        dout = self.dma("q", self.scr_den, self.sums[0:c.HQ, 0:QB],
                        self.d_su0, waits=(rcpv,))
        mult_g = [None] * 4
        for hd in range(c.HQ):
            sl = hd % 4
            dchunk, roff = hd // 2, (hd % 2) * 64
            rbb = self.ar5[roff:roff + 64, sl * 1024: sl * 1024 + QB]
            din = self.dma("q", rbb, bass.AP(
                self.scr_den.tensor, self.scr_den[hd, 0].offset,
                [[0, 64], [1, QB]]), self.d_rb0,
                waits=(dout, mult_g[sl]) if hd < 4 else (mult_g[sl],))
            mult_g[sl] = self.op(
                "v", lambda h, o=ofm[roff:roff + 64, dchunk, :], b=rbb:
                h.tensor_tensor(o, o, b, ALU.mult), waits=(din,), inc=self.cv)
        self._rope_gate = None
        self.ps1q = [self.tok("cs")] * 4
        self.ofm_tok = self.tok("cv")

    def _s3(self, seq, qb, ofm):
        c = self.c
        NQT = c.QB // 128
        DH = c.D // 2
        woT = self.bf(self.ar2, 0, c.DC * DH // 2) \
            .rearrange("p (dc f) -> p dc f", f=DH)
        attnout = self.ar5[:, 0:c.D]
        x_rd = self.ar5[:, c.D:2 * c.D]

        def stash_ap(qt):
            if qt < NQT - 1:
                return self.ar1[:, self.o_xw + qt * DH:
                                self.o_xw + (qt + 1) * DH]
            return self.ar8[:, 0:DH]
        stash_g = [None] * NQT
        ln_g = None
        for half in range(2):
            wtok = self.build_wT(self.Wo, woT, half * DH, DH)
            for qt in range(NQT):
                sl = qt % 2
                ps = self.PS1[:, sl * 1024: sl * 1024 + DH]
                for dc in range(c.DC):
                    for jo in range(0, DH, 512):
                        wo_ = min(512, DH - jo)
                        self.op("t", lambda h, o=ps[:, jo:jo + wo_],
                                l=ofm[:, dc, qt * 128:(qt + 1) * 128],
                                r=woT[:, dc, jo:jo + wo_], st=(dc == 0):
                                h.matmul(o, l, r, start=st, stop=False,
                                         skip_group_check=True),
                                waits=(wtok, self.ofm_tok, self.ps1q[2 * sl],
                                       self.ps1q[2 * sl + 1]) + self.bar
                                if (dc == 0 and jo == 0) else ())
                for jo in range(0, DH, 512):
                    wo_ = min(512, DH - jo)
                    self.op("t", lambda h, o=ps[:, jo:jo + wo_], h0=half * DH,
                            jo=jo, wo_=wo_: h.matmul(
                        o, self.ones3[32:33, :],
                        self.brow[32:33, h0 + jo:h0 + jo + wo_],
                        start=False, stop=True, skip_group_check=True),
                        inc=self.ct if jo + 512 >= DH else None)
                mm = self.tok("ct")
                if half == 0:
                    ev = self.op("v", lambda h, o=stash_ap(qt), i=ps:
                                 h.tensor_copy(o, i),
                                 waits=(mm, stash_g[qt]) + self.bar, inc=self.cv)
                    stash_g[qt] = ev
                    self.ps1q[2 * sl] = self.ps1q[2 * sl + 1] = ev
                else:
                    ev = self.op("v", lambda h, o=attnout[:, DH:2 * DH], i=ps:
                                 h.tensor_copy(o, i), waits=(mm, ln_g), inc=self.cv)
                    self.ps1q[2 * sl] = self.ps1q[2 * sl + 1] = ev
                    cp = self.op("v", lambda h, o=attnout[:, 0:DH], i=stash_ap(qt):
                                 h.tensor_copy(o, i), waits=(stash_g[qt], ev),
                                 inc=self.cv)
                    ln_g = self._ln1(seq, qb, qt, attnout, x_rd, (cp, ev))

    def _ln1(self, seq, qb, qt, attnout, x_rd, waits):
        c = self.c
        tcg = qb * (c.QB // 128) + qt
        gq = seq * (c.T // 128) + tcg
        xl = self.dma("q", x_rd, self.x_in[seq, tcg * 128:(tcg + 1) * 128, :],
                      self.d_xr, waits=(getattr(self, "_xr_g", None),))
        s_ = x_rd
        t1 = self.op("v", lambda h, o=s_, a=attnout, b=x_rd:
                     h.tensor_tensor(o, a, b, ALU.add),
                     waits=tuple(waits) + (xl,), inc=self.cv)
        nbk = (c.D + 511) // 512
        t2 = None
        for bi in range(nbk):
            w = min(512, c.D - bi * 512)
            t2 = self.op("v", lambda h, o=self.bn6[:, bi * 6:bi * 6 + 6],
                         i=s_[:, bi * 512:bi * 512 + w]: h.bn_stats(o, i),
                         waits=(t1,), inc=self.cv)
        t3 = self.op("v", lambda h: h.bn_aggr(self.colA[:, 0:2],
                                              self.bn6[:, 0:6 * nbk]),
                     waits=(t2,), inc=self.cv)
        t4 = self.op("v", lambda h: h.tensor_scalar(
            self.colA[:, 2:3], self.colA[:, 1:2], LN_EPS, None, ALU.add),
            waits=(t3,), inc=self.cv)
        t5 = self.op("v", lambda h: h.reciprocal(
            self.colA[:, 3:4], self.colA[:, 2:3]), waits=(t4,), inc=self.cv)
        t6 = self.op("s", lambda h: h.activation(
            self.colA[:, 4:5], self.colA[:, 3:4], AF.Ln), waits=(t5,), inc=self.cs)
        t7 = self.op("s", lambda h: h.activation(
            self.colA[:, 5:6], self.colA[:, 4:5], AF.Exp, scale=0.5),
            waits=(t6,), inc=self.cs)
        t8 = self.op("v", lambda h: h.tensor_scalar(
            self.colA[:, 6:7], self.colA[:, 0:1], self.colA[:, 5:6], -1.0,
            ALU.mult, ALU.mult), waits=(t7,), inc=self.cv)
        x1 = attnout
        t9 = self.op("s", lambda h, o=x1, i=s_: h.activation(
            o, i, AF.Identity, bias=self.colA[:, 6:7], scale=self.colA[:, 5:6]),
            waits=(t8,), inc=self.cs)
        w1 = self.dma("q", self.scr_x1tm[gq * 128:(gq + 1) * 128, :], x1,
                      self.d_x1, waits=(t9,))
        x1b = self.bf(self.ar7, 2048, c.D // 2)
        ta = self.op("v", lambda h, o=x1b, i=x1: h.tensor_copy(o, i),
                     waits=(t9, getattr(self, "_x1b_g", None)), inc=self.cv)
        evs = {}
        ngrp = (c.DC + 7) // 8
        for gj in range(ngrp):
            nt = min(8, c.DC - gj * 8)
            ev = self.transpose_group(
                [x1b[:, (gj * 8 + j) * 128:(gj * 8 + j + 1) * 128]
                 for j in range(nt)],
                self.bf(self.ar8, 1024 + gj * 512, nt * 64)
                .rearrange("p (n f) -> p n f", f=128),
                waits=(ta, getattr(self, "_x1s_g", None)))
            for j in range(nt):
                evs[gj * 8 + j] = ev
        last = None
        for d0 in range(0, c.DC, 4):
            k = min(4, c.DC - d0)
            src = self.bf(self.ar8, 1024 + d0 * 64, k * 64) \
                .rearrange("p (n f) -> p n f", f=128)
            dst = self.scr_x1fm[d0 * 128:(d0 + k) * 128,
                                gq * 128:(gq + 1) * 128] \
                .rearrange("(n p) f -> p n f", p=128)
            last = self.dma("q", dst, src, self.d_x1b, waits=(evs[d0],))
        self._x1s_g = last
        self._x1b_g = evs[(ngrp - 1) * 8]
        self._xr_g = t9
        self.ln_done[gq] = last
        return w1

    def _ffn(self):
        c = self.c
        TW = c.TWIN
        HC = c.HC
        NTC = TW // 128
        n1h = min(HC, self.N1 // (TW // 2))

        def hbuf(hc):
            if hc < n1h:
                return self.bf(self.ar1, hc * (TW // 2), TW // 2)
            return self.bf(self.ar2, 12288 + (hc - n1h) * (TW // 2), TW // 2)
        if HC > n1h:
            assert 12288 + (HC - n1h) * (TW // 2) <= self.N2
        x1w = self.bf(self.ar2, 0, c.DC * TW // 2) \
            .rearrange("p (dc t) -> p dc t", t=TW)
        ytm = [self.ar4[:, 0:c.D], self.ar4[:, 2048:2048 + c.D],
               self.ar5[:, 0:c.D], self.ar8[:, 0:c.D]][:NTC]
        h_g = [None] * HC
        ytm_g = [None] * NTC
        sg_g = [None, None]
        sg_cons = [None, None]
        sp_g = [None, None, None]
        sp_cons = [None, None, None]
        x1w_g = None
        zsb_g = [None, None]
        sq_g = [None, None]
        pend_sos = None
        sos_prev = None
        self._nrot = 2
        self._treg = 0
        self.flush()
        self.barrier()
        x1_all = (self.tok("d_x1"), self.tok("d_x1b"))
        for win in range(c.NWIN):
            tok0 = win * TW
            lnw = x1_all
            xw = None
            for d4 in range(c.DC // 4):
                xw = self.dma("q", x1w[:, d4 * 4:(d4 + 1) * 4, :],
                              bass.AP(self.scr_x1fm.tensor,
                                      self.scr_x1fm[d4 * 512:d4 * 512 + 128,
                                                    tok0:tok0 + TW].offset,
                                      [[c.NTOK, 128], [128 * c.NTOK, 4], [1, TW]]),
                              self.d_xw,
                              waits=lnw + (x1w_g,) + self.bar if d4 == 0 else ())
            for hc in range(HC):
                ssl = (hc // 2) % 2
                if hc % 2 == 0:
                    sgsem = [self.d_sg0, self.d_sg1][ssl]
                    gslab = self.bf(self.ar2, 4096 + ssl * 2048, c.DC * 128) \
                        .rearrange("p (dc f) -> p dc f", f=256)
                    uslab = self.bf(self.ar2, 8192 + ssl * 2048, c.DC * 128) \
                        .rearrange("p (dc f) -> p dc f", f=256)
                    w = (self.tok_quant, self.tok_quant1, sg_g[ssl],
                         sg_cons[ssl]) + self.bar
                    last = None
                    for four, (slab, col0) in enumerate(
                            ((gslab, hc * 128), (uslab, c.Hh + hc * 128))):
                        for dq in range(c.DC // 4):
                            last = self.dma(
                                "q", slab[:, dq * 4:(dq + 1) * 4, :],
                                bass.AP(self.scr_gu.tensor,
                                        self.scr_gu[dq * 512:dq * 512 + 128,
                                                    col0:col0 + 256].offset,
                                        [[c.H2, 128], [128 * c.H2, 4], [1, 256]]),
                                sgsem, waits=w if (four == 0 and dq == 0) else ())
                    sg_g[ssl] = last
                    cur = (gslab, uslab, last)
                gslab, uslab, slab_tok = cur
                hoff = (hc % 2) * 128
                qg, qu = 2 * (hc % 2), 2 * (hc % 2) + 1
                psg = self.PS1[:, qg * 512: qg * 512 + TW]
                psu = self.PS1[:, qu * 512: qu * 512 + TW]
                for dc in range(c.DC):
                    self.op("t", lambda h, o=psg, l=gslab[:, dc, hoff:hoff + 128],
                            r=x1w[:, dc, :], st=(dc == 0), sp=(dc == c.DC - 1):
                            h.matmul(o, l, r, start=st, stop=sp,
                                     skip_group_check=True),
                            waits=(xw, slab_tok, self.ps1q[qg]) + self.bar
                            if dc == 0 else (),
                            inc=self.ct if dc == c.DC - 1 else None)
                mg = self.tok("ct")
                for dc in range(c.DC):
                    self.op("t", lambda h, o=psu, l=uslab[:, dc, hoff:hoff + 128],
                            r=x1w[:, dc, :], st=(dc == 0), sp=(dc == c.DC - 1):
                            h.matmul(o, l, r, start=st, stop=sp,
                                     skip_group_check=True),
                            waits=(self.ps1q[qu],) if dc == 0 else (),
                            inc=self.ct if dc == c.DC - 1 else None)
                mu = self.tok("ct")
                sg_cons[ssl] = mu
                xw_cons = mu
                gsil = self.ar8[:, 2048:2048 + TW]
                t1 = self.op("s", lambda h, o=gsil, i=psg:
                             h.activation(o, i, AF.Silu),
                             waits=(mg, getattr(self, "_gsil_g", None)) + self.bar,
                             inc=self.cs)
                self.ps1q[qg] = t1
                hv = hbuf(hc)
                t2 = self.op("v", lambda h, o=hv, a=gsil, b=psu:
                             h.tensor_tensor(o, a, b, ALU.mult),
                             waits=(t1, mu, h_g[hc]) + self.bar, inc=self.cv)
                self._gsil_g = t2
                self.ps1q[qu] = t2
                h_g[hc] = t2
                sqh = self.bf(self.ar7, (hc % 2) * 256, TW // 2)
                t3 = self.op("v", lambda h, o=sqh, a=hv, b=hv:
                             h.tensor_tensor(o, a, b, ALU.mult),
                             waits=(t2, sq_g[hc % 2]), inc=self.cv)

                def sos_mm(hc=hc, t3=t3, sqh=sqh):
                    self.op("t", lambda h, o=self.PS2[0:1, 1024:1024 + TW], i=sqh,
                            st=(hc == 0), sp=(hc == HC - 1):
                            h.matmul(o, self.ones_col[:], i, start=st,
                                     stop=sp, skip_group_check=True),
                            waits=(t3,) + ((sos_prev,) if hc == 0 else ()),
                            inc=self.ct if hc == HC - 1 else None)
                    sq_g[hc % 2] = self.tok("ct")
                if pend_sos is not None:
                    pend_sos()
                pend_sos = sos_mm
            pend_sos()
            pend_sos = None
            sos_tok = self.tok("ct")
            r0 = self.op("v", lambda h: h.tensor_scalar(
                self.rowz_f[0:1, 0:TW], self.PS2[0:1, 1024:1024 + TW],
                1.0 / c.Hh, RMS_EPS, ALU.mult, ALU.add),
                waits=(sos_tok, getattr(self, "_rowz_g", None)), inc=self.cv)
            r1 = self.op("s", lambda h: h.activation(
                self.rowz_f[0:1, 0:TW], self.rowz_f[0:1, 0:TW], AF.Ln),
                waits=(r0,), inc=self.cs)
            r2 = self.op("s", lambda h: h.activation(
                self.rowz_f[0:1, 0:TW], self.rowz_f[0:1, 0:TW], AF.Exp, scale=-0.5),
                waits=(r1,), inc=self.cs)
            sos_prev = r2
            self._rowz_g = r2
            rc_t = []
            rcm_prev = getattr(self, "_rcm_g", None)
            for tcw in range(NTC):
                m = self.op("t", lambda h, o=self.PS2[:, 1536 + tcw:1537 + tcw],
                            l=self.rowz_f[0:1, tcw * 128:(tcw + 1) * 128]:
                            h.matmul(o, l, self.ones3[0:1, 0:1], start=True,
                                     stop=True, skip_group_check=True),
                            waits=(r2, rcm_prev), inc=self.ct)
                e = self.op("v", lambda h, o=self.colB[:, tcw:tcw + 1],
                            i=self.PS2[:, 1536 + tcw:1537 + tcw]:
                            h.tensor_copy(o, i), waits=(m,), inc=self.cv)
                rc_t.append(e)
            self._rcm_g = rc_t[-1]
            h_last = h_g[HC - 1]
            xw_cons_f = xw_cons
            for dc in range(c.DC):
                psl = dc % 3
                spsem = [self.d_sp0, self.d_sp1, self.d_sp2][psl]
                pslab = self.bf(self.ar2, [4096, 8192, 0][psl], HC * 64) \
                    .rearrange("p (hc f) -> p hc f", f=128)
                w0 = ((sp_g[psl], sp_cons[psl], mg, mu, xw_cons)
                      if dc < 3 else (sp_g[psl], sp_cons[psl]))
                sp_last = None
                for h8 in range(HC // 8):
                    sp_last = self.dma(
                        "q", pslab[:, h8 * 8:(h8 + 1) * 8, :],
                        bass.AP(self.scr_p.tensor,
                                self.scr_p[h8 * 1024:h8 * 1024 + 128,
                                           dc * 128:(dc + 1) * 128].offset,
                                [[c.D, 128], [128 * c.D, 8], [1, 128]]),
                        spsem, waits=w0 if h8 == 0 else ())
                sp_g[psl] = sp_last
                qz = dc % 4
                psz = self.PS1[:, qz * 512: qz * 512 + TW]
                for hc in range(HC):
                    self.op("t", lambda h, o=psz, l=pslab[:, hc, :], r=hbuf(hc),
                            st=(hc == 0), sp=(hc == HC - 1):
                            h.matmul(o, l, r, start=st, stop=sp,
                                     skip_group_check=True),
                            waits=(sp_last, h_last, self.ps1q[qz]) if hc == 0 else (),
                            inc=self.ct if hc == HC - 1 else None)
                mz = self.tok("ct")
                sp_cons[psl] = mz
                zsb = self.ar7[:, 1024 + (dc % 2) * 512:1024 + (dc % 2) * 512 + TW]
                ez = self.op("v", lambda h, o=zsb, i=psz: h.tensor_copy(o, i),
                             waits=(mz, zsb_g[dc % 2]), inc=self.cv)
                self.ps1q[qz] = ez
                trk = None
                for tcw in range(NTC):
                    r = self._treg = 1 - self._treg
                    pst = self.PS2[:, r * 512:r * 512 + 128]
                    trk = self.op("t", lambda h, o=pst,
                                  i=zsb[:, tcw * 128:(tcw + 1) * 128]:
                                  h.transpose(o, i, self.identf[:]),
                                  waits=(ez, self.trg[r]), inc=self.ct)
                    ya = self.op("s", lambda h, o=ytm[tcw][:, dc * 128:(dc + 1) * 128],
                                 i=pst, sc=self.colB[:, tcw:tcw + 1]:
                                 h.activation(o, i, AF.Identity, scale=sc),
                                 waits=(trk, rc_t[tcw], ytm_g[tcw]), inc=self.cs)
                    self.trg[r] = ya
                    if dc == c.DC - 1:
                        ytm_g[tcw] = ya
                zsb_g[dc % 2] = trk
            for tcw in range(NTC):
                gq = (tok0 // 128) + tcw
                seq = gq // (c.T // 128)
                tloc = gq % (c.T // 128)
                x1r = self.ar5[:, 2048:2048 + c.D]
                xl = self.dma("q", x1r, self.scr_x1tm[gq * 128:(gq + 1) * 128, :],
                              self.d_xr, waits=(getattr(self, "_xr_g", None),))
                s_ = self.ar6[:, 0:c.D]
                t1 = self.op("v", lambda h, o=s_, a=ytm[tcw], b=x1r:
                             h.tensor_tensor(o, a, b, ALU.add),
                             waits=(xl, ytm_g[tcw], self.st_tok[0], self.st_tok[1]),
                             inc=self.cv)
                self.st_tok = [t1, t1]
                nbk = (c.D + 511) // 512
                t2 = None
                for bi in range(nbk):
                    w = min(512, c.D - bi * 512)
                    t2 = self.op("v", lambda h, o=self.bn6[:, bi * 6:bi * 6 + 6],
                                 i=s_[:, bi * 512:bi * 512 + w]: h.bn_stats(o, i),
                                 waits=(t1,), inc=self.cv)
                t3 = self.op("v", lambda h: h.bn_aggr(self.colA[:, 0:2],
                                                      self.bn6[:, 0:6 * nbk]),
                             waits=(t2,), inc=self.cv)
                t4 = self.op("v", lambda h: h.tensor_scalar(
                    self.colA[:, 2:3], self.colA[:, 1:2], LN_EPS, None, ALU.add),
                    waits=(t3,), inc=self.cv)
                t5 = self.op("v", lambda h: h.reciprocal(
                    self.colA[:, 3:4], self.colA[:, 2:3]), waits=(t4,), inc=self.cv)
                t6 = self.op("s", lambda h: h.activation(
                    self.colA[:, 4:5], self.colA[:, 3:4], AF.Ln),
                    waits=(t5,), inc=self.cs)
                t7 = self.op("s", lambda h: h.activation(
                    self.colA[:, 5:6], self.colA[:, 4:5], AF.Exp, scale=0.5),
                    waits=(t6,), inc=self.cs)
                t8 = self.op("v", lambda h: h.tensor_scalar(
                    self.colA[:, 6:7], self.colA[:, 0:1], self.colA[:, 5:6], -1.0,
                    ALU.mult, ALU.mult), waits=(t7,), inc=self.cv)
                t9 = self.op("s", lambda h, o=ytm[tcw], i=s_: h.activation(
                    o, i, AF.Identity, bias=self.colA[:, 6:7],
                    scale=self.colA[:, 5:6]), waits=(t8,), inc=self.cs)
                ytm_g[tcw] = self.dma(
                    "q", self.out[seq, tloc * 128:(tloc + 1) * 128, :],
                    ytm[tcw], self.d_out, waits=(t9,))
                self._xr_g = t1
            x1w_g = [self.tok("ct"), sp_g[2]]

    def _finish(self):
        waits = tuple(self.tok(nm) for nm in self.cnt if nm.startswith("d_"))
        self.op("q", lambda h: h.nop(), waits=waits)


_BUILT = {}


def _get_nc(cfg_key="full"):
    if cfg_key not in _BUILT:
        c = Cfg()
        b = Builder(c)
        nc = b.build()
        _BUILT[cfg_key] = (c, nc)
    return _BUILT[cfg_key]


def kernel(**inputs):
    c, nc = _get_nc()
    consts = host_consts(c)
    x = np.asarray(inputs["L0_attn_x"])
    base = {
        "Wq": np.ascontiguousarray(inputs["L0_attn_Wq"], np.float32),
        "bq": np.ascontiguousarray(inputs["L0_attn_bq"], np.float32),
        "Wk": np.ascontiguousarray(inputs["L0_attn_Wk"], np.float32),
        "bk": np.ascontiguousarray(inputs["L0_attn_bk"], np.float32),
        "Wv": np.ascontiguousarray(inputs["L0_attn_Wv"], np.float32),
        "bv": np.ascontiguousarray(inputs["L0_attn_bv"], np.float32),
        "Wo": np.ascontiguousarray(inputs["L0_attn_Wo"], np.float32),
        "bo": np.ascontiguousarray(inputs["L0_attn_bo"], np.float32),
        "Wgu": np.ascontiguousarray(inputs["L0_ffn_w_gate_up"], np.float32),
        "Wp": np.ascontiguousarray(inputs["L0_ffn_w_proj"], np.float32),
    }
    base.update({"identb_in": consts["ident"], "identf_in": consts["identf"],
                 "onescol_in": consts["ones_col"], "gmat_in": consts["gmat"],
                 "cos_tab": consts["cos_tab"], "sins_tab": consts["sins_tab"]})
    in_maps = []
    for core in range(NCORES):
        m = dict(base)
        m["x"] = np.ascontiguousarray(
            x[core * c.SEQS:(core + 1) * c.SEQS], np.float32)
        in_maps.append(m)
    res = run_bass_kernel_spmd(nc, in_maps, core_ids=list(range(NCORES)))
    return np.concatenate([r["out"] for r in res.results], axis=0)


# version 16
# speedup vs baseline: 2.1391x; 1.0807x over previous
import contextlib
import numpy as np
import ml_dtypes
import concourse.bass as bass
import concourse.mybir as mybir
from concourse.bass_utils import run_bass_kernel_spmd

F32 = mybir.dt.float32
BF16 = mybir.dt.bfloat16
AF = mybir.ActivationFunctionType
ALU = mybir.AluOpType
AX = mybir.AxisListType

NCORES = 8
LN_EPS = 1e-5
RMS_EPS = 1.1920929e-7


class Cfg:
    def __init__(self, seqs=2, T=2048, D=2048, HQ=32, HKV=8, QB=1024, TWIN=512):
        self.SEQS, self.T, self.D, self.HQ, self.HKV = seqs, T, D, HQ, HKV
        self.HD = 64
        self.G = HQ // HKV
        self.DKV = HKV * self.HD
        self.Hh = 4 * D
        self.H2 = 8 * D
        self.QB, self.TWIN = QB, TWIN
        assert D == HQ * self.HD and T % QB == 0 and QB % 512 == 0
        self.DC = D // 128
        self.TC = T // 128
        self.KT = T // 128
        self.KVC = self.DKV // 128
        self.NQB = T // QB
        self.NTOK = seqs * T
        self.NWIN = self.NTOK // TWIN
        self.HC = self.Hh // 128
        lo = [h for h in range(HQ) if (h // self.G) % 2 == 0]
        hi = [h for h in range(HQ) if (h // self.G) % 2 == 1]
        self.pairs = list(zip(lo, hi))


def host_consts(c: Cfg):
    t = np.arange(c.T, dtype=np.float32)
    inv_freq = (1.0 / 10000.0 ** (np.arange(0, c.HD, 2, dtype=np.float32) / c.HD))
    ang = t[:, None].astype(np.float64) * inv_freq[None, :].astype(np.float64)
    cos = np.cos(ang).astype(np.float32)
    sin = np.sin(ang).astype(np.float32)
    return {
        "cos_tab": np.ascontiguousarray(np.concatenate([cos, cos], 1)),
        "sins_tab": np.ascontiguousarray(
            np.concatenate([-sin, sin], 1)).astype(np.float32),
        "ident": np.eye(128, dtype=ml_dtypes.bfloat16),
        "identf": np.eye(128, dtype=np.float32),
        "ones_col": np.ones((128, 1), dtype=ml_dtypes.bfloat16),
        "gmat": np.stack([(np.arange(c.HQ) // c.G == kv).astype(np.float32)
                          for kv in range(c.HKV)]),
    }


class Builder:
    ENG = {"t": "tensor", "v": "vector", "s": "scalar", "g": "gpsimd", "q": "sync"}

    def __init__(self, c: Cfg):
        self.c = c
        self.nc = bass.Bass("TRN2", target_bir_lowering=False)
        self.cnt = {}
        self.waited = {}
        self.pend = {k: [] for k in self.ENG}
        self.sems = {}
        self.bar = ()

    def sem(self, name):
        s = self.es.enter_context(self.nc.semaphore(name))
        self.sems[id(s)] = name
        self.cnt[name] = 0
        return s

    def op(self, eng, fn, waits=(), inc=None, incn=1):
        tok = None
        flat = []
        for w in waits:
            if w is None:
                continue
            if isinstance(w, list):
                flat.extend(x for x in w if x is not None)
            else:
                flat.append(w)
        waits = tuple(flat)
        if inc is not None:
            nm = self.sems[id(inc)]
            self.cnt[nm] += incn
            tok = (inc, self.cnt[nm])
        self.pend[eng].append((fn, waits, inc, incn))
        return tok

    def dma(self, eng, dst, src, sem, waits=()):
        return self.op(eng, lambda h, d=dst, s=src: h.dma_start(d, s),
                       waits=waits, inc=sem, incn=16)

    def tok(self, semname):
        return (getattr(self, semname), self.cnt[semname])

    def barrier(self):
        self.bar = tuple(self.tok(nm) for nm in self.cnt)

    def flush(self):
        if not any(self.pend.values()):
            return
        with self.nc.Block() as blk:
            for ename, attr in self.ENG.items():
                items = self.pend[ename]
                if not items:
                    continue
                self.pend[ename] = []

                def body(h, items=items, ename=ename):
                    for fn, waits, inc, incn in items:
                        for (sm, v) in waits:
                            if v <= 0:
                                continue
                            key = (ename, self.sems[id(sm)])
                            if self.waited.get(key, 0) >= v:
                                continue
                            h.wait_ge(sm, v)
                            self.waited[key] = v
                        instr = fn(h)
                        if inc is not None:
                            instr.then_inc(inc, incn)
                getattr(blk, attr)(body)

    def build(self):
        c, nc = self.c, self.nc
        self.es = contextlib.ExitStack()
        d = lambda n, s, ty, k: nc.dram_tensor(n, s, ty, kind=k).ap()
        self.x_in = d("x", [c.SEQS, c.T, c.D], F32, "ExternalInput")
        self.Wq = d("Wq", [c.D, c.D], F32, "ExternalInput")
        self.bq = d("bq", [c.D], F32, "ExternalInput")
        self.Wk = d("Wk", [c.DKV, c.D], F32, "ExternalInput")
        self.bk = d("bk", [c.DKV], F32, "ExternalInput")
        self.Wv = d("Wv", [c.DKV, c.D], F32, "ExternalInput")
        self.bv = d("bv", [c.DKV], F32, "ExternalInput")
        self.Wo = d("Wo", [c.D, c.D], F32, "ExternalInput")
        self.bo = d("bo", [c.D], F32, "ExternalInput")
        self.Wgu = d("Wgu", [c.H2, c.D], F32, "ExternalInput")
        self.Wp = d("Wp", [c.D, c.Hh], F32, "ExternalInput")
        self.cos_in = d("cos_tab", [c.T, c.HD], F32, "ExternalInput")
        self.sin_in = d("sins_tab", [c.T, c.HD], F32, "ExternalInput")
        self.ident_in = d("identb_in", [128, 128], BF16, "ExternalInput")
        self.identf_in = d("identf_in", [128, 128], F32, "ExternalInput")
        self.onescol_in = d("onescol_in", [128, 1], BF16, "ExternalInput")
        self.gmat_in = d("gmat_in", [c.HKV, c.HQ], F32, "ExternalInput")
        self.out = d("out", [c.SEQS, c.T, c.D], F32, "ExternalOutput")
        self.scr_gu = d("scr_gu", [c.D, c.H2], BF16, "Internal")
        self.scr_p = d("scr_p", [c.Hh, c.D], BF16, "Internal")
        self.scr_x1tm = d("scr_x1tm", [c.NTOK, c.D], F32, "Internal")
        self.scr_x1fm = d("scr_x1fm", [c.D, c.NTOK], BF16, "Internal")
        self.scr_sum = d("scr_sum", [2, c.QB], F32, "Internal")
        self.scr_den = d("scr_den", [c.HQ, c.QB], F32, "Internal")
        self.scr_rcp = d("scr_rcp", [2, c.QB], F32, "Internal")
        with self.es:
            self._alloc()
            self._program()
        return nc

    def _alloc(self):
        c, nc, es = self.c, self.nc, self.es
        sb = lambda n, s, ty: es.enter_context(nc.sbuf_tensor(n, s, ty))
        pt = lambda n, s, ty: es.enter_context(nc.psum_tensor(n, s, ty))
        need_attn = (c.KVC * c.T // 2 + (c.TC * c.HKV * 65 + 1) // 2
                     + max(2048, c.DC * 128)
                     + max(c.DC * c.DKV // 2, c.DC * 256) + 1024)
        self.N1 = max(need_attn, 14336)
        self.ar1 = sb("ar1", [128, self.N1], F32)
        self.N2 = max(c.DC * c.QB, 12800)
        self.ar2 = sb("ar2", [128, self.N2], F32)
        self.ar4 = sb("ar4", [128, 4096], F32)
        self.ar5 = sb("ar5", [128, 2 * 2048], F32)
        self.ar6 = sb("ar6", [128, 2048], F32)
        self.ar7 = sb("ar7", [128, 3072], F32)
        self.ar8 = sb("ar8", [128, 2560], F32)
        self.ident = sb("ident", [128, 128], BF16)
        self.identf = sb("identf", [128, 128], F32)
        self.ones_col = sb("ones_col", [128, 1], BF16)
        self.gmat = sb("gmat", [c.HKV, c.HQ], F32)
        self.ones3 = sb("ones3", [65, 128], F32)
        self.brow = sb("brow", [65, max(2 * c.DKV, c.D)], F32)
        self.bn6 = sb("bn6", [128, 24], F32)
        self.colA = sb("colA", [128, 8], F32)
        self.colB = sb("colB", [128, 8], F32)
        self.normQ = sb("normQ", [128, (c.QB // 128) * c.HQ], F32)
        self.normK = sb("normK", [128, c.TC * c.HKV], F32)
        self.biasmat = sb("biasmat", [128, c.HQ], F32)
        self.kmaxc = sb("kmaxc", [c.HKV, 2], F32)
        self.hqc = sb("hqc", [c.HQ, 8], F32)
        self.rcol = sb("rcol", [128, 48], F32)
        self.rowz = self.ar8[0:1, 0:1024]
        self.sums = self.ar8[0:c.HQ, 1024:1024 + c.QB]
        self.rowz_f = self.ar7[0:1, 512:1024]
        self.PS1 = pt("PS1", [128, 2048], F32)
        self.PS2 = pt("PS2", [128, 2048], F32)
        for nm in ("ct", "cv", "cs", "cg", "d_cst", "d_st0", "d_st1", "d_ws",
                   "d_ws1", "d_x1", "d_x1b", "d_xr", "d_su0", "d_su1", "d_rb0", "d_rb1",
                   "d_xw", "d_sg0", "d_sg1", "d_sp0", "d_sp1", "d_sp2", "d_out"):
            setattr(self, nm, self.sem(nm))
        self.st_tok = [None, None]
        self.lc_g = [None, None]
        self.ps1q = [None, None, None, None]
        self.trg = [None, None, None]
        self._nrot = 3
        self._treg = 0
        self._stsl = 0

    def bf(self, ar, off, n, p0=0, p1=128):
        return ar[p0:p1, off:off + n].bitcast(BF16)

    def ps1(self, i):
        return self.PS1[:, i * 512:(i + 1) * 512]

    def transpose_group(self, tiles, dst, waits=(), f32=False, evac=None):
        r = self._treg = (self._treg + 1) % self._nrot
        nt = len(tiles)
        if f32:
            assert nt <= 4
            ps = self.PS2[:, r * 512:r * 512 + nt * 128]
            ident = self.identf
        else:
            assert nt <= 8
            ps = self.PS2[:, r * 512:(r + 1) * 512].bitcast(BF16)[:, 0:nt * 128]
            ident = self.ident
        tk = None
        w0 = tuple(waits) + (self.trg[r],) + self.bar
        for j, s in enumerate(tiles):
            o = ps[:, j * 128:(j + 1) * 128]
            tk = self.op("t", lambda h, o=o, i=s, idn=ident:
                         h.transpose(o, i, idn[:]),
                         waits=w0 if j == 0 else (), inc=self.ct)
        if evac is None:
            pv = ps.rearrange("p (n f) -> p n f", f=128)
            ev = self.op("v", lambda h, d=dst, s=pv: h.tensor_copy(d, s),
                         waits=(tk,), inc=self.cv)
        else:
            ev = evac(ps, tk)
        self.trg[r] = ev
        return ev

    def load_cast(self, src, n, waits=(), eng="q", dst_off=None, dst_ar=None):
        sl = self._stsl = 1 - self._stsl
        st = [self.d_st0, self.d_st1][sl]
        stage = self.ar6[:, sl * 1024: sl * 1024 + n]
        ld = self.dma(eng, stage, src, st, waits=(self.st_tok[sl],) + self.bar)
        off = sl * 512 if dst_off is None else dst_off
        dst = self.bf(self.ar7 if dst_ar is None else dst_ar, off, n // 2)
        gate = self.lc_g[sl] if dst_off is None else None
        cs = self.op("s", lambda h, d=dst, s=stage: h.copy(d, s),
                     waits=(ld, gate) + tuple(waits) + self.bar, inc=self.cs)
        self.st_tok[sl] = cs
        self._lc_sl = sl
        return dst, cs

    def build_wT(self, W, wT, row0, ncols):
        c = self.c
        ev = None
        for rch in range(ncols // 128):
            for hf in range((c.D + 1023) // 1024):
                w = min(1024, c.D - hf * 1024)
                src = W[row0 + rch * 128: row0 + rch * 128 + 128,
                        hf * 1024: hf * 1024 + w]
                wbf, cs = self.load_cast(src, w)
                sl = self._lc_sl
                tiles = [wbf[:, j * 128:(j + 1) * 128] for j in range(w // 128)]
                dst = wT[:, hf * 8: hf * 8 + w // 128, rch * 128: rch * 128 + 128]
                ev = self.transpose_group(tiles, dst, waits=(cs,))
                self.lc_g[sl] = ev
        return ev

    def _program(self):
        self._consts()
        self.flush()
        self.barrier()
        self._quant()
        self.ln_done = {}
        for seq in range(self.c.SEQS):
            self._attn_seq(seq)
        self.flush()
        self.barrier()
        self.quant_drain()
        self._ffn()
        self._finish()
        self.flush()

    def _consts(self):
        c = self.c
        t = lambda d_, s_: self.dma("q", d_, s_, self.d_cst)
        t(self.ident[:], self.ident_in)
        t(self.identf[:], self.identf_in)
        t(self.ones_col[:], self.onescol_in)
        t(self.gmat[:], self.gmat_in)
        self.o_cos, self.o_sin = 0, c.TC * c.HD
        t(self.ar4[:, self.o_cos:self.o_cos + c.TC * c.HD]
          .rearrange("p (tc f) -> p tc f", f=c.HD),
          self.cos_in.rearrange("(tc p) f -> p tc f", p=128))
        t(self.ar4[:, self.o_sin:self.o_sin + c.TC * c.HD]
          .rearrange("p (tc f) -> p tc f", f=c.HD),
          self.sin_in.rearrange("(tc p) f -> p tc f", p=128))
        t(self.brow[0:1, 0:c.D], self.bq.rearrange("(o d) -> o d", o=1))
        t(self.brow[32:33, 0:c.D], self.bo.rearrange("(o d) -> o d", o=1))
        t(self.brow[64:65, 0:c.DKV], self.bk.rearrange("(o d) -> o d", o=1))
        t(self.brow[64:65, c.DKV:2 * c.DKV], self.bv.rearrange("(o d) -> o d", o=1))
        self.tok_consts = self.tok("d_cst")
        self.tok_ones = self.op("g", lambda h: h.memset(self.ones3[:], 1.0),
                                inc=self.cg)

    def quant_step(self, n=1):
        g = getattr(self, "_qgen", None)
        if g is None:
            return
        for _ in range(n):
            try:
                next(g)
            except StopIteration:
                self._qgen = None
                self.tok_quant = self.tok("d_ws")
                self.tok_quant1 = self.tok("d_ws1")
                return

    def quant_drain(self):
        while getattr(self, "_qgen", None) is not None:
            self.quant_step(8)

    def _quant(self):
        self._qgen = self._quant_gen()
        self.quant_step(5)

    def _quant_gen(self):
        c = self.c
        WBO = [1024, 1536, 3072, 3584]
        WBA = [None, None, self.ar4, self.ar4]
        RQ = [2048, 2560]
        STG = [2048, 2560]
        SC = 2048
        cons = [None, None, None, None]
        wsd = [None, None]

        def piece(idx, src, dst_fn, n):
            sl = idx % 2
            s4 = idx % 4
            g = n // 64
            wb, tcast = self.load_cast(src, n, waits=(cons[s4],),
                                       dst_off=WBO[s4], dst_ar=WBA[s4])
            wbv = wb.rearrange("p (g k) -> p g k", k=64)
            sums = self.ar8[:, SC + s4 * 128: SC + s4 * 128 + g]
            sclbf = self.bf(self.ar8, SC + s4 * 128 + 32, 8)[:, 0:g]
            rcp = self.ar8[:, SC + s4 * 128 + 48: SC + s4 * 128 + 48 + g]
            t1 = self.op("v", lambda h, o=sums, i=wbv: h.reduce_sum(
                o.rearrange("p (g o2) -> p g o2", o2=1), i, axis=AX.X,
                apply_absolute_value=True), waits=(tcast,), inc=self.cv)
            t2 = self.op("v", lambda h, o=sclbf, i=sums: h.tensor_scalar(
                o, i, 1.0 / 64, 1e-8, ALU.mult, ALU.max), waits=(t1,), inc=self.cv)
            t3 = self.op("v", lambda h, o=rcp, i=sclbf: h.reciprocal(o, i),
                         waits=(t2,), inc=self.cv)
            r = self.bf(self.ar7, RQ[sl], n // 2)
            rv = r.rearrange("p (g k) -> p g k", k=64)
            rcpb = bass.AP(rcp.tensor, rcp.offset,
                           [list(rcp.ap[0]), [1, g], [0, 64]])
            t4 = self.op("g", lambda h, o=rv, a=wbv, b=rcpb:
                         h.tensor_tensor(o, a, b, ALU.mult),
                         waits=(t3, wsd[sl]) + self.bar, inc=self.cg)
            t5 = self.op("v", lambda h, o=r, i=r: h.tensor_scalar(
                o, i, 192.0, None, ALU.add), waits=(t4,), inc=self.cv)
            t6 = self.op("v", lambda h, o=r, i=r: h.tensor_scalar(
                o, i, -192.0, 1.0, ALU.add, ALU.min), waits=(t5,), inc=self.cv)
            t7 = self.op("v", lambda h, o=r, i=r: h.tensor_scalar(
                o, i, -1.0, None, ALU.max), waits=(t6,), inc=self.cv)
            sclbb = bass.AP(sclbf.tensor, sclbf.offset,
                            [list(sclbf.ap[0]), [1, g], [0, 64]])
            t8 = self.op("g", lambda h, o=wbv, a=rv, b=sclbb:
                         h.tensor_tensor(o, a, b, ALU.mult), waits=(t7,), inc=self.cg)

            def stage2(sl=sl, s4=s4, wb=wb, t8=t8, dst_fn=dst_fn, n=n):
                nt = n // 128
                stg = self.bf(self.ar7, STG[sl], nt * 64)
                stgv = stg.rearrange("p (t f) -> p t f", f=128)
                ev = self.transpose_group(
                    [wb[:, j * 128:(j + 1) * 128] for j in range(nt)], stgv,
                    waits=(t8, wsd[sl]))
                cons[s4] = ev
                last = None
                wsem = [self.d_ws, self.d_ws1][sl]
                for j4 in range(0, nt, 4):
                    k = min(4, nt - j4)
                    last = self.dma("g" if sl else "q", dst_fn(j4, k),
                                    stgv[:, j4:j4 + k, :], wsem, waits=(ev,))
                wsd[sl] = last
            return stage2

        def sources():
            wcol = min(1024, c.D)
            for rch in range(c.H2 // 128):
                for jc in range(c.D // wcol):
                    def dgu(j4, k, rch=rch, jc=jc):
                        r0 = jc * wcol + j4 * 128
                        return self.scr_gu[r0:r0 + k * 128,
                                           rch * 128:(rch + 1) * 128] \
                            .rearrange("(n p) f -> p n f", p=128)
                    yield (self.Wgu[rch * 128:(rch + 1) * 128,
                                    jc * wcol:(jc + 1) * wcol], dgu, wcol)
            wcol2 = min(1024, c.Hh)
            for rch in range(c.D // 128):
                for jc in range(c.Hh // wcol2):
                    def dp(j4, k, rch=rch, jc=jc):
                        r0 = jc * wcol2 + j4 * 128
                        return self.scr_p[r0:r0 + k * 128,
                                          rch * 128:(rch + 1) * 128] \
                            .rearrange("(n p) f -> p n f", p=128)
                    yield (self.Wp[rch * 128:(rch + 1) * 128,
                                   jc * wcol2:(jc + 1) * wcol2], dp, wcol2)
        idx = 0
        pend2 = []
        for srcw, dfn, ncol in sources():
            st2 = piece(idx, srcw, dfn, ncol)
            idx += 1
            yield
            pend2.append(st2)
            if len(pend2) >= 2:
                pend2.pop(0)()
                yield
        while pend2:
            pend2.pop(0)()
            yield

    def _attn_seq(self, seq):
        c = self.c
        self.o_kfm = 0
        self.o_vh = c.KVC * c.T // 2
        self.o_xw = self.o_vh + (c.TC * c.HKV * 65 + 1) // 2
        self.o_wt = self.o_xw + max(2048, c.DC * 128)
        self.o_tm = self.o_wt + max(c.DC * c.DKV // 2, c.DC * 256)
        kfm = self.bf(self.ar1, self.o_kfm, c.KVC * c.T // 2) \
            .rearrange("p (kc t) -> p kc t", t=c.T)
        nvh = c.TC * c.HKV * 65
        vh = self.bf(self.ar1, self.o_vh, (nvh + 1) // 2)[:, 0:nvh] \
            .rearrange("p (tc kv f) -> p tc kv f", kv=c.HKV, f=65)
        self.flush()
        self.barrier()
        self._s1kv(seq, kfm, vh)
        self.flush()
        for qb in range(c.NQB):
            qfm = self.bf(self.ar2, 0, c.DC * c.QB // 2) \
                .rearrange("p (dc t) -> p dc t", t=c.QB)
            ofm = self.bf(self.ar2, c.DC * c.QB // 2, c.DC * c.QB // 2) \
                .rearrange("p (dc t) -> p dc t", t=c.QB)
            self.barrier()
            self._s1q(seq, qb, qfm)
            self.flush()
            self.barrier()
            self._biasmat()
            self.flush()
            self.barrier()
            self._s2(seq, qb, qfm, kfm, vh, ofm)
            self.flush()
            self.barrier()
            self._s3(seq, qb, ofm)
            self.flush()

    def _xtile(self, seq, tc, off, gate=None):
        c = self.c
        dst = self.bf(self.ar1, off, c.DC * 64) \
            .rearrange("p (dc f) -> p dc f", f=128)
        ev = None
        for hf in range((c.D + 1023) // 1024):
            w = min(1024, c.D - hf * 1024)
            src = self.x_in[seq, tc * 128:(tc + 1) * 128, hf * 1024:hf * 1024 + w]
            xbf, cs = self.load_cast(src, w, waits=(gate,))
            sl = self._lc_sl
            tiles = [xbf[:, j * 128:(j + 1) * 128] for j in range(w // 128)]
            ev = self.transpose_group(tiles, dst[:, hf * 8: hf * 8 + w // 128, :],
                                      waits=(cs,))
            self.lc_g[sl] = ev
        return dst, ev

    def _rope(self, ps, dstview, tc, n, waits):
        par = list(self.ar4[:].ap[0])
        cosb = bass.AP(self.ar4[:].tensor, self.ar4[:, self.o_cos + tc * 64].offset,
                       [par, [0, n], [1, 64]])
        sinb = bass.AP(self.ar4[:].tensor, self.ar4[:, self.o_sin + tc * 64].offset,
                       [par, [0, n], [32, 2], [1, 32]])
        psv = ps.rearrange("p (h f) -> p h f", f=64)
        psw = bass.AP(ps.tensor, ps.offset + 32,
                      [list(ps.ap[0]), [64, n], [-32, 2], [1, 32]])
        ms = self._mrot = 1 - getattr(self, "_mrot", 0)
        m1 = self.ar8[:, ms * 1024:ms * 1024 + n * 64]
        m2 = self.ar8[:, ms * 1024 + 512:ms * 1024 + 512 + n * 64]
        mg = getattr(self, "_rope_gates", [None, None])
        a = self.op("v", lambda h: h.tensor_tensor(
            m1.rearrange("p (h f) -> p h f", f=64), psv, cosb, ALU.mult),
            waits=tuple(waits) + (mg[ms],) + self.bar, inc=self.cv)
        b = self.op("v", lambda h: h.tensor_tensor(
            m2.rearrange("p (h two f) -> p h two f", two=2, f=32), psw, sinb,
            ALU.mult), inc=self.cv)
        e = self.op("v", lambda h: h.tensor_tensor(
            dstview, m1.rearrange("p (h f) -> p h f", f=64),
            m2.rearrange("p (h f) -> p h f", f=64), ALU.add),
            waits=(a, b), inc=self.cv)
        mg[ms] = e
        self._rope_gates = mg
        return e

    def _s1kv(self, seq, kfm, vh):
        c = self.c
        vones = self.op("g", lambda h: h.memset(
            bass.AP(vh.tensor, vh[:, 0, 0, 64:65].offset,
                    [list(vh.ap[0]), [65, c.TC * c.HKV], [1, 1]]), 1.0),
            waits=(self.tok_consts,) + self.bar, inc=self.cg)
        wkT = self.bf(self.ar1, self.o_wt, c.DC * c.DKV // 2) \
            .rearrange("p (dc f) -> p dc f", f=c.DKV)
        wktok = self.build_wT(self.Wk, wkT, 0, c.DKV)
        wvT = self.bf(self.ar2, 0, c.DC * c.DKV // 2) \
            .rearrange("p (dc f) -> p dc f", f=c.DKV)
        wvtok = self.build_wT(self.Wv, wvT, 0, c.DKV)
        self._ktm_g = [None, None]
        for tc in range(c.TC):
            sl = tc % 2
            xt, xtok = self._xtile(seq, tc, self.o_xw + sl * 1024,
                                   gate=getattr(self, "_xt_g", [None, None])[sl])
            ps = self.PS1[:, sl * 1024: sl * 1024 + c.DKV]
            psv = self.PS1[:, sl * 1024 + c.DKV: sl * 1024 + 2 * c.DKV]
            for dc in range(c.DC):
                self.op("t", lambda h, o=ps, l=xt[:, dc, :], r=wkT[:, dc, :],
                        st=(dc == 0):
                        h.matmul(o, l, r, start=st, stop=False,
                                 skip_group_check=True),
                        waits=(xtok, wktok, wvtok, self.ps1q[2 * sl],
                               self.ps1q[2 * sl + 1])
                        + self.bar if dc == 0 else ())
            mm = self.op("t", lambda h, o=ps: h.matmul(
                o, self.ones3[64:65, :], self.brow[64:65, 0:c.DKV],
                start=False, stop=True, skip_group_check=True),
                waits=(self.tok_consts, self.tok_ones), inc=self.ct)
            self.quant_step(2)
            for dc in range(c.DC):
                self.op("t", lambda h, o=psv, l=xt[:, dc, :], r=wvT[:, dc, :],
                        st=(dc == 0):
                        h.matmul(o, l, r, start=st, stop=False,
                                 skip_group_check=True))
            mmv = self.op("t", lambda h, o=psv: h.matmul(
                o, self.ones3[64:65, :], self.brow[64:65, c.DKV:2 * c.DKV],
                start=False, stop=True, skip_group_check=True), inc=self.ct)
            self.quant_step(2)
            xg = getattr(self, "_xt_g", [None, None])
            xg[sl] = mmv
            self._xt_g = xg
            sq = self.ar5[:, 0:c.DKV]
            s1 = self.op("s", lambda h, o=sq, i=ps: h.activation(o, i, AF.Square),
                         waits=(mm, getattr(self, "_sq_g", None)), inc=self.cs)
            nk = self.normK[:, tc * c.HKV:(tc + 1) * c.HKV]
            self._sq_g = self.op(
                "v", lambda h, o=nk, i=sq.rearrange("p (kv f) -> p kv f", f=64):
                h.reduce_sum(o.rearrange("p (kv o2) -> p kv o2", o2=1), i,
                             axis=AX.X), waits=(s1,), inc=self.cv)
            self.quant_step(2)
            ktm = self.bf(self.ar1, self.o_tm + sl * 512, c.DKV // 2)
            rt = self._rope(ps, ktm.rearrange("p (h f) -> p h f", f=64),
                            tc, c.HKV, (mm, s1, self._ktm_g[sl]))
            self.ps1q[2 * sl] = rt
            ev = self.transpose_group(
                [ktm[:, j * 128:(j + 1) * 128] for j in range(c.KVC)],
                bass.AP(kfm.tensor, kfm[:, 0, tc * 128].offset,
                        [list(kfm.ap[0]), [c.T, c.KVC], [1, 128]]),
                waits=(rt,))
            self._ktm_g[sl] = ev
            self.quant_step(2)
            dstv = bass.AP(vh.tensor, vh[:, tc, 0, 0].offset,
                           [list(vh.ap[0]), [65, c.HKV], [1, 64]])
            evv = self.op("v", lambda h, o=dstv,
                          i=psv.rearrange("p (kv f) -> p kv f", f=64):
                          h.tensor_copy(o, i), waits=(mmv, vones), inc=self.cv)
            self.ps1q[2 * sl + 1] = evv
            self.quant_step(2)
        self.vh_tok = self.tok("cv")
        nkv = bass.AP(self.normK[:].tensor, self.normK[:].offset,
                      [list(self.normK[:].ap[0]), [1, c.HKV], [c.HKV, c.TC]])
        t1 = self.op("v", lambda h: h.reduce_max(
            self.colA[:, 0:c.HKV].rearrange("p (kv o2) -> p kv o2", o2=1), nkv,
            axis=AX.X), waits=(self._sq_g,), inc=self.cv)
        t2 = self.op("t", lambda h: h.transpose(
            self.PS2[0:c.HKV, 1536:1664], self.colA[:, 0:c.HKV], self.identf[:]),
            waits=(t1,), inc=self.ct)
        t3 = self.op("v", lambda h: h.reduce_max(
            self.kmaxc[:, 0:1].rearrange("p (o o2) -> p o o2", o2=1),
            self.PS2[0:c.HKV, 1536:1664].rearrange("p (o f) -> p o f", o=1),
            axis=AX.X), waits=(t2,), inc=self.cv)
        self.kmax_tok = t3

    def _s1q(self, seq, qb, qfm):
        c = self.c
        qf_tok = None
        self._qtm_g = [None, None, None, None]
        NQT = c.QB // 128
        xqb = self.bf(self.ar2, c.DC * c.QB // 2, c.DC * c.QB // 2) \
            .rearrange("p (dc t) -> p dc t", t=c.QB)
        xtoks = []
        for tq in range(NQT):
            tcg = qb * NQT + tq
            dst = xqb[:, :, tq * 128:(tq + 1) * 128]
            xtok = None
            for hf in range((c.D + 1023) // 1024):
                w = min(1024, c.D - hf * 1024)
                src = self.x_in[seq, tcg * 128:(tcg + 1) * 128,
                                hf * 1024:hf * 1024 + w]
                xbf, cs = self.load_cast(src, w)
                sl_ = self._lc_sl
                xtok = self.transpose_group(
                    [xbf[:, j * 128:(j + 1) * 128] for j in range(w // 128)],
                    dst[:, hf * 8:hf * 8 + w // 128, :], waits=(cs,))
                self.lc_g[sl_] = xtok
                self.quant_step(1)
            xtoks.append(xtok)
            self.quant_step(1)
        for d8p in range(c.HQ // 8):
            wq8 = self.bf(self.ar1, self.o_wt, c.DC * 256) \
                .rearrange("p (dc f) -> p dc f", f=512)
            wtok = self.build_wT(self.Wq, wq8, d8p * 512, 512)
            for tq in range(NQT):
                tcg = qb * NQT + tq
                sl = tq % 4
                ps = self.ps1(sl)
                for dc in range(c.DC):
                    self.op("t", lambda h, o=ps,
                            l=xqb[:, dc, tq * 128:(tq + 1) * 128],
                            r=wq8[:, dc, :], st=(dc == 0):
                            h.matmul(o, l, r, start=st, stop=False,
                                     skip_group_check=True),
                            waits=(xtoks[tq], wtok, self.ps1q[sl])
                            if dc == 0 else ())
                mm = self.op("t", lambda h, o=ps, d0=d8p * 512: h.matmul(
                    o, self.ones3[0:1, :], self.brow[0:1, d0:d0 + 512],
                    start=False, stop=True, skip_group_check=True), inc=self.ct)
                self.quant_step(1)
                sq = self.ar5[:, 0:512]
                s1 = self.op("s", lambda h, o=sq, i=ps: h.activation(
                    o, i, AF.Square),
                    waits=(mm, getattr(self, "_sq_g", None)), inc=self.cs)
                nq = self.normQ[:, tq * c.HQ + d8p * 8: tq * c.HQ + d8p * 8 + 8]
                self._sq_g = self.op(
                    "v", lambda h, o=nq,
                    i=sq.rearrange("p (g f) -> p g f", f=64):
                    h.reduce_sum(o.rearrange("p (g o2) -> p g o2", o2=1), i,
                                 axis=AX.X), waits=(s1,), inc=self.cv)
                qt_b = self.bf(self.ar1, self.o_tm + sl * 256, 256)
                dstv = bass.AP(qt_b.tensor, qt_b.offset,
                               [list(qt_b.ap[0]), [64, 2], [128, 4], [1, 64]])
                rt = self._rope(ps, dstv, tcg, 8,
                                (mm, s1, self._qtm_g[sl]))
                self.quant_step(1)
                self.ps1q[sl] = rt
                ev = self.transpose_group(
                    [qt_b[:, j * 128:(j + 1) * 128] for j in range(4)],
                    qfm[:, 4 * d8p:4 * d8p + 4, tq * 128:(tq + 1) * 128],
                    waits=(rt,))
                self._qtm_g[sl] = ev
                qf_tok = ev
                self.quant_step(1)
        self.q_ready = qf_tok

    def _biasmat(self):
        c = self.c
        NQT = c.QB // 128
        nqv = bass.AP(self.normQ[:].tensor, self.normQ[:].offset,
                      [list(self.normQ[:].ap[0]), [1, c.HQ], [c.HQ, NQT]])
        qmax_s = self.ar8[:, 512:512 + c.HQ]
        t1 = self.op("v", lambda h: h.reduce_max(
            qmax_s.rearrange("p (hq o2) -> p hq o2", o2=1), nqv, axis=AX.X),
            waits=self.bar, inc=self.cv)
        t2 = self.op("t", lambda h: h.transpose(
            self.PS2[0:c.HQ, 1536:1664], qmax_s, self.identf[:]),
            waits=(t1,) + self.bar, inc=self.ct)
        t3 = self.op("v", lambda h: h.reduce_max(
            self.hqc[:, 0:1].rearrange("p (o o2) -> p o o2", o2=1),
            self.PS2[0:c.HQ, 1536:1664].rearrange("p (o f) -> p o f", o=1),
            axis=AX.X), waits=(t2,), inc=self.cv)
        t4 = self.op("t", lambda h: h.matmul(
            self.PS2[0:c.HQ, 1664:1665], self.gmat[:], self.kmaxc[:, 0:1],
            start=True, stop=True, skip_group_check=True),
            waits=(self.kmax_tok, t3), inc=self.ct)
        t5 = self.op("v", lambda h: h.tensor_copy(
            self.hqc[:, 1:2], self.PS2[0:c.HQ, 1664:1665]), waits=(t4,), inc=self.cv)
        t6 = self.op("v", lambda h: h.tensor_tensor(
            self.hqc[:, 2:3], self.hqc[:, 0:1], self.hqc[:, 1:2], ALU.mult),
            waits=(t5,), inc=self.cv)
        t6b = self.op("v", lambda h: h.tensor_scalar(
            self.hqc[:, 2:3], self.hqc[:, 2:3], 1e-36, None, ALU.max),
            waits=(t6,), inc=self.cv)
        t7 = self.op("s", lambda h: h.activation(
            self.hqc[:, 3:4], self.hqc[:, 2:3], AF.Ln),
            waits=(t6b,) + self.bar, inc=self.cs)
        t8 = self.op("s", lambda h: h.activation(
            self.hqc[:, 4:5], self.hqc[:, 3:4], AF.Exp, scale=0.5),
            waits=(t7,), inc=self.cs)
        t9 = self.op("v", lambda h: h.tensor_scalar(
            self.hqc[:, 5:6], self.hqc[:, 4:5], -0.125, None, ALU.mult),
            waits=(t8,), inc=self.cv)
        ta = self.op("t", lambda h: h.transpose(
            self.PS2[0:1, 1536:1536 + c.HQ], self.hqc[0:c.HQ, 5:6],
            self.identf[0:c.HQ, 0:c.HQ]), waits=(t9,), inc=self.ct)
        tb = self.op("v", lambda h: h.tensor_copy(
            self.rowz[0:1, 0:c.HQ], self.PS2[0:1, 1536:1536 + c.HQ]),
            waits=(ta, getattr(self, "_rowz_g", None)), inc=self.cv)
        tc_ = self.op("t", lambda h: h.matmul(
            self.PS2[:, 1536:1536 + c.HQ], self.ones3[0:1, :],
            self.rowz[0:1, 0:c.HQ], start=True, stop=True,
            skip_group_check=True), waits=(tb,), inc=self.ct)
        td = self.op("v", lambda h: h.tensor_copy(
            self.biasmat[:], self.PS2[:, 1536:1536 + c.HQ]),
            waits=(tc_,), inc=self.cv)
        self._rowz_g = tc_
        self.biasmat_tok = td

    def _s2(self, seq, qb, qfm, kfm, vh, ofm):
        c = self.c
        QB = c.QB
        NJ = QB // 512
        atb_off = [0, QB // 2, 2048, 2048 + QB // 2]
        atb = [[self.bf(self.ar7 if s == 0 else self.ar4,
                        atb_off[2 * s + r], QB // 2)
                for r in range(2)] for s in range(2)]
        exp_g = [None, None]
        av_g = [[None, None], [None, None]]
        o_g = [None, None]
        for ip, (hlo, hhi) in enumerate(c.pairs):
            kvl, kvh_ = hlo // c.G, hhi // c.G
            av_last = [None, None]

            def emit_av(kt, side, kv, first_extra=()):
                rot = kt % 2
                for jh in range(NJ):
                    w = ([exp_g[side]] + list(first_extra)) if jh == 0 else ()
                    self.op("t", lambda h,
                            o=self.PS2[0:65, side * 1024 + jh * 512:
                                       side * 1024 + (jh + 1) * 512],
                            l=vh[:, kt, kv, :],
                            r=atb[side][rot][:, jh * 512:(jh + 1) * 512],
                            st=(kt == 0), sp=(kt == c.KT - 1):
                            h.matmul(o, l, r, start=st, stop=sp,
                                     skip_group_check=True),
                            waits=tuple(w), inc=self.ct if jh == NJ - 1 else None)
                av_g[side][rot] = av_last[side] = self.tok("ct")

            pend_av = None
            for kt in range(c.KT):
                rot = kt % 2
                pslo = self.PS1[:, 0:QB]
                pshi = self.PS1[:, 1024:1024 + QB]
                for jh in range(NJ):
                    w = ((self.q_ready, self.biasmat_tok, self.vh_tok, exp_g[0])
                         + self.bar if jh == 0 else ())
                    self.op("t", lambda h, o=pslo[:, jh * 512:(jh + 1) * 512],
                            l=kfm[0:64, kvl // 2, kt * 128:(kt + 1) * 128],
                            r=qfm[0:64, ip, jh * 512:(jh + 1) * 512]:
                            h.matmul(o, l, r, start=True, stop=True,
                                     tile_position=(0, 0), skip_group_check=True),
                            waits=w, inc=self.ct if jh == NJ - 1 else None)
                mlo = self.tok("ct")
                for jh in range(NJ):
                    w = ((exp_g[1],) if jh == 0 else ())
                    self.op("t", lambda h, o=pshi[:, jh * 512:(jh + 1) * 512],
                            l=kfm[64:128, kvh_ // 2, kt * 128:(kt + 1) * 128],
                            r=qfm[64:128, ip, jh * 512:(jh + 1) * 512]:
                            h.matmul(o, l, r, start=True, stop=True,
                                     tile_position=(64, 0), skip_group_check=True),
                            waits=w, inc=self.ct if jh == NJ - 1 else None)
                mhi = self.tok("ct")
                if pend_av is not None:
                    pend_av()
                elo = self.op("s", lambda h, o=atb[0][rot], i=pslo,
                              b=self.biasmat[:, hlo:hlo + 1]:
                              h.activation(o, i, AF.Exp, bias=b, scale=0.125),
                              waits=(mlo, av_g[0][rot]) + self.bar, inc=self.cs)
                ehi = self.op("s", lambda h, o=atb[1][rot], i=pshi,
                              b=self.biasmat[:, hhi:hhi + 1]:
                              h.activation(o, i, AF.Exp, bias=b, scale=0.125),
                              waits=(mhi, av_g[1][rot]), inc=self.cs)
                exp_g = [elo, ehi]

                def pend(kt=kt, fe0=(o_g[0],) if kt == 0 else (),
                         fe1=(o_g[1],) if kt == 0 else ()):
                    emit_av(kt, 0, kvl, fe0)
                    emit_av(kt, 1, kvh_, fe1)
                pend_av = pend
            pend_av()
            for side, hd in ((0, hlo), (1, hhi)):
                psO = self.PS2[:, side * 1024: side * 1024 + QB]
                ssu = [self.d_su0, self.d_su1][side]
                dchunk, roff = hd // 2, (hd % 2) * 64
                t0 = self.op("v", lambda h, o=ofm[roff:roff + 64, dchunk, :],
                             i=psO[0:64, :]: h.tensor_copy(o, i),
                             waits=(av_last[side],), inc=self.cv)
                t1 = self.op("v", lambda h, o=self.rowz[0:1, 0:QB],
                             i=psO[64:65, :]: h.tensor_copy(o, i),
                             waits=(t0, getattr(self, "_rowz_g", None)),
                             inc=self.cv)
                o_g[side] = t1
                dsum = self.dma("q", self.sums[hd:hd + 1, 0:QB],
                                self.rowz[0:1, 0:QB], ssu, waits=(t1,))
                self._rowz_g = dsum
        rcpv = self.op("v", lambda h: h.reciprocal(
            self.sums[0:c.HQ, 0:QB], self.sums[0:c.HQ, 0:QB]),
            waits=(self.tok("d_su0"), self.tok("d_su1")), inc=self.cv)
        dout = self.dma("q", self.scr_den, self.sums[0:c.HQ, 0:QB],
                        self.d_su0, waits=(rcpv,))
        mult_g = [None] * 4
        for hd in range(c.HQ):
            sl = hd % 4
            dchunk, roff = hd // 2, (hd % 2) * 64
            rbb = self.ar5[roff:roff + 64, sl * 1024: sl * 1024 + QB]
            din = self.dma("q", rbb, bass.AP(
                self.scr_den.tensor, self.scr_den[hd, 0].offset,
                [[0, 64], [1, QB]]), self.d_rb0,
                waits=(dout, mult_g[sl]) if hd < 4 else (mult_g[sl],))
            mult_g[sl] = self.op(
                "v", lambda h, o=ofm[roff:roff + 64, dchunk, :], b=rbb:
                h.tensor_tensor(o, o, b, ALU.mult), waits=(din,), inc=self.cv)
        self._rope_gate = None
        self.ps1q = [self.tok("cs")] * 4
        self.ofm_tok = self.tok("cv")

    def _s3(self, seq, qb, ofm):
        c = self.c
        NQT = c.QB // 128
        DH = c.D // 2
        woT = self.bf(self.ar2, 0, c.DC * DH // 2) \
            .rearrange("p (dc f) -> p dc f", f=DH)
        attnout = self.ar5[:, 0:c.D]
        x_rd = self.ar5[:, c.D:2 * c.D]

        def stash_ap(qt):
            if qt < NQT - 1:
                return self.ar1[:, self.o_xw + qt * DH:
                                self.o_xw + (qt + 1) * DH]
            return self.ar8[:, 0:DH]
        stash_g = [None] * NQT
        ln_g = None
        for half in range(2):
            wtok = self.build_wT(self.Wo, woT, half * DH, DH)
            for qt in range(NQT):
                sl = qt % 2
                ps = self.PS1[:, sl * 1024: sl * 1024 + DH]
                for dc in range(c.DC):
                    for jo in range(0, DH, 512):
                        wo_ = min(512, DH - jo)
                        self.op("t", lambda h, o=ps[:, jo:jo + wo_],
                                l=ofm[:, dc, qt * 128:(qt + 1) * 128],
                                r=woT[:, dc, jo:jo + wo_], st=(dc == 0):
                                h.matmul(o, l, r, start=st, stop=False,
                                         skip_group_check=True),
                                waits=(wtok, self.ofm_tok, self.ps1q[2 * sl],
                                       self.ps1q[2 * sl + 1]) + self.bar
                                if (dc == 0 and jo == 0) else ())
                for jo in range(0, DH, 512):
                    wo_ = min(512, DH - jo)
                    self.op("t", lambda h, o=ps[:, jo:jo + wo_], h0=half * DH,
                            jo=jo, wo_=wo_: h.matmul(
                        o, self.ones3[32:33, :],
                        self.brow[32:33, h0 + jo:h0 + jo + wo_],
                        start=False, stop=True, skip_group_check=True),
                        inc=self.ct if jo + 512 >= DH else None)
                mm = self.tok("ct")
                if half == 0:
                    ev = self.op("v", lambda h, o=stash_ap(qt), i=ps:
                                 h.tensor_copy(o, i),
                                 waits=(mm, stash_g[qt]) + self.bar, inc=self.cv)
                    stash_g[qt] = ev
                    self.ps1q[2 * sl] = self.ps1q[2 * sl + 1] = ev
                else:
                    ev = self.op("v", lambda h, o=attnout[:, DH:2 * DH], i=ps:
                                 h.tensor_copy(o, i), waits=(mm, ln_g), inc=self.cv)
                    self.ps1q[2 * sl] = self.ps1q[2 * sl + 1] = ev
                    cp = self.op("v", lambda h, o=attnout[:, 0:DH], i=stash_ap(qt):
                                 h.tensor_copy(o, i), waits=(stash_g[qt], ev),
                                 inc=self.cv)
                    ln_g = self._ln1(seq, qb, qt, attnout, x_rd, (cp, ev))

    def _ln1(self, seq, qb, qt, attnout, x_rd, waits):
        c = self.c
        tcg = qb * (c.QB // 128) + qt
        gq = seq * (c.T // 128) + tcg
        xl = self.dma("q", x_rd, self.x_in[seq, tcg * 128:(tcg + 1) * 128, :],
                      self.d_xr, waits=(getattr(self, "_xr_g", None),))
        s_ = x_rd
        t1 = self.op("v", lambda h, o=s_, a=attnout, b=x_rd:
                     h.tensor_tensor(o, a, b, ALU.add),
                     waits=tuple(waits) + (xl,), inc=self.cv)
        nbk = (c.D + 511) // 512
        t2 = None
        for bi in range(nbk):
            w = min(512, c.D - bi * 512)
            t2 = self.op("v", lambda h, o=self.bn6[:, bi * 6:bi * 6 + 6],
                         i=s_[:, bi * 512:bi * 512 + w]: h.bn_stats(o, i),
                         waits=(t1,), inc=self.cv)
        t3 = self.op("v", lambda h: h.bn_aggr(self.colA[:, 0:2],
                                              self.bn6[:, 0:6 * nbk]),
                     waits=(t2,), inc=self.cv)
        t4 = self.op("v", lambda h: h.tensor_scalar(
            self.colA[:, 2:3], self.colA[:, 1:2], LN_EPS, None, ALU.add),
            waits=(t3,), inc=self.cv)
        t5 = self.op("v", lambda h: h.reciprocal(
            self.colA[:, 3:4], self.colA[:, 2:3]), waits=(t4,), inc=self.cv)
        t6 = self.op("s", lambda h: h.activation(
            self.colA[:, 4:5], self.colA[:, 3:4], AF.Ln), waits=(t5,), inc=self.cs)
        t7 = self.op("s", lambda h: h.activation(
            self.colA[:, 5:6], self.colA[:, 4:5], AF.Exp, scale=0.5),
            waits=(t6,), inc=self.cs)
        t8 = self.op("v", lambda h: h.tensor_scalar(
            self.colA[:, 6:7], self.colA[:, 0:1], self.colA[:, 5:6], -1.0,
            ALU.mult, ALU.mult), waits=(t7,), inc=self.cv)
        x1 = attnout
        t9 = self.op("s", lambda h, o=x1, i=s_: h.activation(
            o, i, AF.Identity, bias=self.colA[:, 6:7], scale=self.colA[:, 5:6]),
            waits=(t8,), inc=self.cs)
        w1 = self.dma("q", self.scr_x1tm[gq * 128:(gq + 1) * 128, :], x1,
                      self.d_x1, waits=(t9,))
        x1b = self.bf(self.ar7, 2048, c.D // 2)
        ta = self.op("v", lambda h, o=x1b, i=x1: h.tensor_copy(o, i),
                     waits=(t9, getattr(self, "_x1b_g", None)), inc=self.cv)
        evs = {}
        ngrp = (c.DC + 7) // 8
        for gj in range(ngrp):
            nt = min(8, c.DC - gj * 8)
            ev = self.transpose_group(
                [x1b[:, (gj * 8 + j) * 128:(gj * 8 + j + 1) * 128]
                 for j in range(nt)],
                self.bf(self.ar8, 1024 + gj * 512, nt * 64)
                .rearrange("p (n f) -> p n f", f=128),
                waits=(ta, getattr(self, "_x1s_g", None)))
            for j in range(nt):
                evs[gj * 8 + j] = ev
        last = None
        for d0 in range(0, c.DC, 4):
            k = min(4, c.DC - d0)
            src = self.bf(self.ar8, 1024 + d0 * 64, k * 64) \
                .rearrange("p (n f) -> p n f", f=128)
            dst = self.scr_x1fm[d0 * 128:(d0 + k) * 128,
                                gq * 128:(gq + 1) * 128] \
                .rearrange("(n p) f -> p n f", p=128)
            last = self.dma("q", dst, src, self.d_x1b, waits=(evs[d0],))
        self._x1s_g = last
        self._x1b_g = evs[(ngrp - 1) * 8]
        self._xr_g = t9
        self.ln_done[gq] = last
        return w1

    def _ffn(self):
        c = self.c
        TW = c.TWIN
        HC = c.HC
        NTC = TW // 128
        n1h = min(HC, self.N1 // (TW // 2))

        def hbuf(hc):
            if hc < n1h:
                return self.bf(self.ar1, hc * (TW // 2), TW // 2)
            return self.bf(self.ar2, 12288 + (hc - n1h) * (TW // 2), TW // 2)
        if HC > n1h:
            assert 12288 + (HC - n1h) * (TW // 2) <= self.N2
        x1w = self.bf(self.ar2, 0, c.DC * TW // 2) \
            .rearrange("p (dc t) -> p dc t", t=TW)
        ytm = [self.ar4[:, 0:c.D], self.ar4[:, 2048:2048 + c.D],
               self.ar5[:, 0:c.D], self.ar8[:, 0:c.D]][:NTC]
        h_g = [None] * HC
        ytm_g = [None] * NTC
        sg_g = [None, None]
        sg_cons = [None, None]
        sp_g = [None, None, None]
        sp_cons = [None, None, None]
        x1w_g = None
        zsb_g = [None, None]
        sq_g = [None, None]
        pend_sos = None
        sos_prev = None
        self._nrot = 2
        self._treg = 0
        self.flush()
        self.barrier()
        x1_all = (self.tok("d_x1"), self.tok("d_x1b"))
        for win in range(c.NWIN):
            tok0 = win * TW
            lnw = x1_all
            xw = None
            for d4 in range(c.DC // 4):
                xw = self.dma("q", x1w[:, d4 * 4:(d4 + 1) * 4, :],
                              bass.AP(self.scr_x1fm.tensor,
                                      self.scr_x1fm[d4 * 512:d4 * 512 + 128,
                                                    tok0:tok0 + TW].offset,
                                      [[c.NTOK, 128], [128 * c.NTOK, 4], [1, TW]]),
                              self.d_xw,
                              waits=lnw + (x1w_g,) + self.bar if d4 == 0 else ())
            for hc in range(HC):
                ssl = (hc // 2) % 2
                if hc % 2 == 0:
                    sgsem = [self.d_sg0, self.d_sg1][ssl]
                    gslab = self.bf(self.ar2, 4096 + ssl * 2048, c.DC * 128) \
                        .rearrange("p (dc f) -> p dc f", f=256)
                    uslab = self.bf(self.ar2, 8192 + ssl * 2048, c.DC * 128) \
                        .rearrange("p (dc f) -> p dc f", f=256)
                    w = (self.tok_quant, self.tok_quant1, sg_g[ssl],
                         sg_cons[ssl]) + self.bar
                    last = None
                    for four, (slab, col0) in enumerate(
                            ((gslab, hc * 128), (uslab, c.Hh + hc * 128))):
                        for dq in range(c.DC // 4):
                            last = self.dma(
                                "q", slab[:, dq * 4:(dq + 1) * 4, :],
                                bass.AP(self.scr_gu.tensor,
                                        self.scr_gu[dq * 512:dq * 512 + 128,
                                                    col0:col0 + 256].offset,
                                        [[c.H2, 128], [128 * c.H2, 4], [1, 256]]),
                                sgsem, waits=w if (four == 0 and dq == 0) else ())
                    sg_g[ssl] = last
                    cur = (gslab, uslab, last)
                gslab, uslab, slab_tok = cur
                hoff = (hc % 2) * 128
                qg, qu = 2 * (hc % 2), 2 * (hc % 2) + 1
                psg = self.PS1[:, qg * 512: qg * 512 + TW]
                psu = self.PS1[:, qu * 512: qu * 512 + TW]
                for dc in range(c.DC):
                    self.op("t", lambda h, o=psg, l=gslab[:, dc, hoff:hoff + 128],
                            r=x1w[:, dc, :], st=(dc == 0), sp=(dc == c.DC - 1):
                            h.matmul(o, l, r, start=st, stop=sp,
                                     skip_group_check=True),
                            waits=(xw, slab_tok, self.ps1q[qg]) + self.bar
                            if dc == 0 else (),
                            inc=self.ct if dc == c.DC - 1 else None)
                mg = self.tok("ct")
                for dc in range(c.DC):
                    self.op("t", lambda h, o=psu, l=uslab[:, dc, hoff:hoff + 128],
                            r=x1w[:, dc, :], st=(dc == 0), sp=(dc == c.DC - 1):
                            h.matmul(o, l, r, start=st, stop=sp,
                                     skip_group_check=True),
                            waits=(self.ps1q[qu],) if dc == 0 else (),
                            inc=self.ct if dc == c.DC - 1 else None)
                mu = self.tok("ct")
                sg_cons[ssl] = mu
                xw_cons = mu
                gsil = self.ar8[:, 2048:2048 + TW]
                t1 = self.op("s", lambda h, o=gsil, i=psg:
                             h.activation(o, i, AF.Silu),
                             waits=(mg, getattr(self, "_gsil_g", None)) + self.bar,
                             inc=self.cs)
                self.ps1q[qg] = t1
                hv = hbuf(hc)
                t2 = self.op("v", lambda h, o=hv, a=gsil, b=psu:
                             h.tensor_tensor(o, a, b, ALU.mult),
                             waits=(t1, mu, h_g[hc]) + self.bar, inc=self.cv)
                self._gsil_g = t2
                self.ps1q[qu] = t2
                h_g[hc] = t2
                sqh = self.bf(self.ar7, (hc % 2) * 256, TW // 2)
                t3 = self.op("v", lambda h, o=sqh, a=hv, b=hv:
                             h.tensor_tensor(o, a, b, ALU.mult),
                             waits=(t2, sq_g[hc % 2]), inc=self.cv)

                def sos_mm(hc=hc, t3=t3, sqh=sqh):
                    self.op("t", lambda h, o=self.PS2[0:1, 1024:1024 + TW], i=sqh,
                            st=(hc == 0), sp=(hc == HC - 1):
                            h.matmul(o, self.ones_col[:], i, start=st,
                                     stop=sp, skip_group_check=True),
                            waits=(t3,) + ((sos_prev,) if hc == 0 else ()),
                            inc=self.ct if hc == HC - 1 else None)
                    sq_g[hc % 2] = self.tok("ct")
                if pend_sos is not None:
                    pend_sos()
                pend_sos = sos_mm
            pend_sos()
            pend_sos = None
            sos_tok = self.tok("ct")
            r0 = self.op("v", lambda h: h.tensor_scalar(
                self.rowz_f[0:1, 0:TW], self.PS2[0:1, 1024:1024 + TW],
                1.0 / c.Hh, RMS_EPS, ALU.mult, ALU.add),
                waits=(sos_tok, getattr(self, "_rowz_g", None)), inc=self.cv)
            r1 = self.op("s", lambda h: h.activation(
                self.rowz_f[0:1, 0:TW], self.rowz_f[0:1, 0:TW], AF.Ln),
                waits=(r0,), inc=self.cs)
            r2 = self.op("s", lambda h: h.activation(
                self.rowz_f[0:1, 0:TW], self.rowz_f[0:1, 0:TW], AF.Exp, scale=-0.5),
                waits=(r1,), inc=self.cs)
            sos_prev = r2
            self._rowz_g = r2
            rc_t = []
            rcm_prev = getattr(self, "_rcm_g", None)
            for tcw in range(NTC):
                m = self.op("t", lambda h, o=self.PS2[:, 1536 + tcw:1537 + tcw],
                            l=self.rowz_f[0:1, tcw * 128:(tcw + 1) * 128]:
                            h.matmul(o, l, self.ones3[0:1, 0:1], start=True,
                                     stop=True, skip_group_check=True),
                            waits=(r2, rcm_prev), inc=self.ct)
                e = self.op("v", lambda h, o=self.colB[:, tcw:tcw + 1],
                            i=self.PS2[:, 1536 + tcw:1537 + tcw]:
                            h.tensor_copy(o, i), waits=(m,), inc=self.cv)
                rc_t.append(e)
            self._rcm_g = rc_t[-1]
            h_last = h_g[HC - 1]
            xw_cons_f = xw_cons
            for dc in range(c.DC):
                psl = dc % 3
                spsem = [self.d_sp0, self.d_sp1, self.d_sp2][psl]
                pslab = self.bf(self.ar2, [4096, 8192, 0][psl], HC * 64) \
                    .rearrange("p (hc f) -> p hc f", f=128)
                w0 = ((sp_g[psl], sp_cons[psl], mg, mu, xw_cons)
                      if dc < 3 else (sp_g[psl], sp_cons[psl]))
                sp_last = None
                for h8 in range(HC // 8):
                    sp_last = self.dma(
                        "q", pslab[:, h8 * 8:(h8 + 1) * 8, :],
                        bass.AP(self.scr_p.tensor,
                                self.scr_p[h8 * 1024:h8 * 1024 + 128,
                                           dc * 128:(dc + 1) * 128].offset,
                                [[c.D, 128], [128 * c.D, 8], [1, 128]]),
                        spsem, waits=w0 if h8 == 0 else ())
                sp_g[psl] = sp_last
                qz = dc % 4
                psz = self.PS1[:, qz * 512: qz * 512 + TW]
                for hc in range(HC):
                    self.op("t", lambda h, o=psz, l=pslab[:, hc, :], r=hbuf(hc),
                            st=(hc == 0), sp=(hc == HC - 1):
                            h.matmul(o, l, r, start=st, stop=sp,
                                     skip_group_check=True),
                            waits=(sp_last, h_last, self.ps1q[qz]) if hc == 0 else (),
                            inc=self.ct if hc == HC - 1 else None)
                mz = self.tok("ct")
                sp_cons[psl] = mz
                zsb = self.ar7[:, 1024 + (dc % 2) * 512:1024 + (dc % 2) * 512 + TW]
                ez = self.op("v", lambda h, o=zsb, i=psz: h.tensor_copy(o, i),
                             waits=(mz, zsb_g[dc % 2]), inc=self.cv)
                self.ps1q[qz] = ez
                trk = None
                for tcw in range(NTC):
                    r = self._treg = 1 - self._treg
                    pst = self.PS2[:, r * 512:r * 512 + 128]
                    trk = self.op("t", lambda h, o=pst,
                                  i=zsb[:, tcw * 128:(tcw + 1) * 128]:
                                  h.transpose(o, i, self.identf[:]),
                                  waits=(ez, self.trg[r]), inc=self.ct)
                    ya = self.op("s", lambda h, o=ytm[tcw][:, dc * 128:(dc + 1) * 128],
                                 i=pst, sc=self.colB[:, tcw:tcw + 1]:
                                 h.activation(o, i, AF.Identity, scale=sc),
                                 waits=(trk, rc_t[tcw], ytm_g[tcw]), inc=self.cs)
                    self.trg[r] = ya
                    if dc == c.DC - 1:
                        ytm_g[tcw] = ya
                zsb_g[dc % 2] = trk
            for tcw in range(NTC):
                gq = (tok0 // 128) + tcw
                seq = gq // (c.T // 128)
                tloc = gq % (c.T // 128)
                x1r = self.ar5[:, 2048:2048 + c.D]
                xl = self.dma("q", x1r, self.scr_x1tm[gq * 128:(gq + 1) * 128, :],
                              self.d_xr, waits=(getattr(self, "_xr_g", None),))
                s_ = self.ar6[:, 0:c.D]
                t1 = self.op("v", lambda h, o=s_, a=ytm[tcw], b=x1r:
                             h.tensor_tensor(o, a, b, ALU.add),
                             waits=(xl, ytm_g[tcw], self.st_tok[0], self.st_tok[1]),
                             inc=self.cv)
                self.st_tok = [t1, t1]
                nbk = (c.D + 511) // 512
                t2 = None
                for bi in range(nbk):
                    w = min(512, c.D - bi * 512)
                    t2 = self.op("v", lambda h, o=self.bn6[:, bi * 6:bi * 6 + 6],
                                 i=s_[:, bi * 512:bi * 512 + w]: h.bn_stats(o, i),
                                 waits=(t1,), inc=self.cv)
                t3 = self.op("v", lambda h: h.bn_aggr(self.colA[:, 0:2],
                                                      self.bn6[:, 0:6 * nbk]),
                             waits=(t2,), inc=self.cv)
                t4 = self.op("v", lambda h: h.tensor_scalar(
                    self.colA[:, 2:3], self.colA[:, 1:2], LN_EPS, None, ALU.add),
                    waits=(t3,), inc=self.cv)
                t5 = self.op("v", lambda h: h.reciprocal(
                    self.colA[:, 3:4], self.colA[:, 2:3]), waits=(t4,), inc=self.cv)
                t6 = self.op("s", lambda h: h.activation(
                    self.colA[:, 4:5], self.colA[:, 3:4], AF.Ln),
                    waits=(t5,), inc=self.cs)
                t7 = self.op("s", lambda h: h.activation(
                    self.colA[:, 5:6], self.colA[:, 4:5], AF.Exp, scale=0.5),
                    waits=(t6,), inc=self.cs)
                t8 = self.op("v", lambda h: h.tensor_scalar(
                    self.colA[:, 6:7], self.colA[:, 0:1], self.colA[:, 5:6], -1.0,
                    ALU.mult, ALU.mult), waits=(t7,), inc=self.cv)
                t9 = self.op("s", lambda h, o=ytm[tcw], i=s_: h.activation(
                    o, i, AF.Identity, bias=self.colA[:, 6:7],
                    scale=self.colA[:, 5:6]), waits=(t8,), inc=self.cs)
                ytm_g[tcw] = self.dma(
                    "q", self.out[seq, tloc * 128:(tloc + 1) * 128, :],
                    ytm[tcw], self.d_out, waits=(t9,))
                self._xr_g = t1
            x1w_g = [self.tok("ct"), sp_g[2]]

    def _finish(self):
        waits = tuple(self.tok(nm) for nm in self.cnt if nm.startswith("d_"))
        self.op("q", lambda h: h.nop(), waits=waits)


_BUILT = {}


def _get_nc(cfg_key="full"):
    if cfg_key not in _BUILT:
        c = Cfg()
        b = Builder(c)
        nc = b.build()
        _BUILT[cfg_key] = (c, nc)
    return _BUILT[cfg_key]


def kernel(**inputs):
    c, nc = _get_nc()
    consts = host_consts(c)
    x = np.asarray(inputs["L0_attn_x"])
    base = {
        "Wq": np.ascontiguousarray(inputs["L0_attn_Wq"], np.float32),
        "bq": np.ascontiguousarray(inputs["L0_attn_bq"], np.float32),
        "Wk": np.ascontiguousarray(inputs["L0_attn_Wk"], np.float32),
        "bk": np.ascontiguousarray(inputs["L0_attn_bk"], np.float32),
        "Wv": np.ascontiguousarray(inputs["L0_attn_Wv"], np.float32),
        "bv": np.ascontiguousarray(inputs["L0_attn_bv"], np.float32),
        "Wo": np.ascontiguousarray(inputs["L0_attn_Wo"], np.float32),
        "bo": np.ascontiguousarray(inputs["L0_attn_bo"], np.float32),
        "Wgu": np.ascontiguousarray(inputs["L0_ffn_w_gate_up"], np.float32),
        "Wp": np.ascontiguousarray(inputs["L0_ffn_w_proj"], np.float32),
    }
    base.update({"identb_in": consts["ident"], "identf_in": consts["identf"],
                 "onescol_in": consts["ones_col"], "gmat_in": consts["gmat"],
                 "cos_tab": consts["cos_tab"], "sins_tab": consts["sins_tab"]})
    in_maps = []
    for core in range(NCORES):
        m = dict(base)
        m["x"] = np.ascontiguousarray(
            x[core * c.SEQS:(core + 1) * c.SEQS], np.float32)
        in_maps.append(m)
    res = run_bass_kernel_spmd(nc, in_maps, core_ids=list(range(NCORES)))
    return np.concatenate([r["out"] for r in res.results], axis=0)


# version 18 (final)
# speedup vs baseline: 2.1870x; 1.0224x over previous
import contextlib
import numpy as np
import ml_dtypes
import concourse.bass as bass
import concourse.mybir as mybir
from concourse.bass_utils import run_bass_kernel_spmd

F32 = mybir.dt.float32
BF16 = mybir.dt.bfloat16
AF = mybir.ActivationFunctionType
ALU = mybir.AluOpType
AX = mybir.AxisListType

NCORES = 8
LN_EPS = 1e-5
RMS_EPS = 1.1920929e-7


class Cfg:
    def __init__(self, seqs=2, T=2048, D=2048, HQ=32, HKV=8, QB=1024, TWIN=512):
        self.SEQS, self.T, self.D, self.HQ, self.HKV = seqs, T, D, HQ, HKV
        self.HD = 64
        self.G = HQ // HKV
        self.DKV = HKV * self.HD
        self.Hh = 4 * D
        self.H2 = 8 * D
        self.QB, self.TWIN = QB, TWIN
        assert D == HQ * self.HD and T % QB == 0 and QB % 512 == 0
        self.DC = D // 128
        self.TC = T // 128
        self.KT = T // 128
        self.KVC = self.DKV // 128
        self.NQB = T // QB
        self.NTOK = seqs * T
        self.NWIN = self.NTOK // TWIN
        self.HC = self.Hh // 128
        lo = [h for h in range(HQ) if (h // self.G) % 2 == 0]
        hi = [h for h in range(HQ) if (h // self.G) % 2 == 1]
        self.pairs = list(zip(lo, hi))


def host_consts(c: Cfg):
    t = np.arange(c.T, dtype=np.float32)
    inv_freq = (1.0 / 10000.0 ** (np.arange(0, c.HD, 2, dtype=np.float32) / c.HD))
    ang = t[:, None].astype(np.float64) * inv_freq[None, :].astype(np.float64)
    cos = np.cos(ang).astype(np.float32)
    sin = np.sin(ang).astype(np.float32)
    return {
        "cos_tab": np.ascontiguousarray(np.concatenate([cos, cos], 1)),
        "sins_tab": np.ascontiguousarray(
            np.concatenate([-sin, sin], 1)).astype(np.float32),
        "ident": np.eye(128, dtype=ml_dtypes.bfloat16),
        "identf": np.eye(128, dtype=np.float32),
        "ones_col": np.ones((128, 1), dtype=ml_dtypes.bfloat16),
        "gmat": np.stack([(np.arange(c.HQ) // c.G == kv).astype(np.float32)
                          for kv in range(c.HKV)]),
    }


class Builder:
    ENG = {"t": "tensor", "v": "vector", "s": "scalar", "g": "gpsimd", "q": "sync"}

    def __init__(self, c: Cfg):
        self.c = c
        self.nc = bass.Bass("TRN2", target_bir_lowering=False)
        self.cnt = {}
        self.waited = {}
        self.pend = {k: [] for k in self.ENG}
        self.sems = {}
        self.bar = ()

    def sem(self, name):
        s = self.es.enter_context(self.nc.semaphore(name))
        self.sems[id(s)] = name
        self.cnt[name] = 0
        return s

    def op(self, eng, fn, waits=(), inc=None, incn=1):
        tok = None
        flat = []
        for w in waits:
            if w is None:
                continue
            if isinstance(w, list):
                flat.extend(x for x in w if x is not None)
            else:
                flat.append(w)
        waits = tuple(flat)
        if inc is not None:
            nm = self.sems[id(inc)]
            self.cnt[nm] += incn
            tok = (inc, self.cnt[nm])
        self.pend[eng].append((fn, waits, inc, incn))
        return tok

    def dma(self, eng, dst, src, sem, waits=()):
        return self.op(eng, lambda h, d=dst, s=src: h.dma_start(d, s),
                       waits=waits, inc=sem, incn=16)

    def tok(self, semname):
        return (getattr(self, semname), self.cnt[semname])

    def barrier(self):
        self.bar = tuple(self.tok(nm) for nm in self.cnt)

    def flush(self):
        if not any(self.pend.values()):
            return
        with self.nc.Block() as blk:
            for ename, attr in self.ENG.items():
                items = self.pend[ename]
                if not items:
                    continue
                self.pend[ename] = []

                def body(h, items=items, ename=ename):
                    for fn, waits, inc, incn in items:
                        for (sm, v) in waits:
                            if v <= 0:
                                continue
                            key = (ename, self.sems[id(sm)])
                            if self.waited.get(key, 0) >= v:
                                continue
                            h.wait_ge(sm, v)
                            self.waited[key] = v
                        instr = fn(h)
                        if inc is not None:
                            instr.then_inc(inc, incn)
                getattr(blk, attr)(body)

    def build(self):
        c, nc = self.c, self.nc
        self.es = contextlib.ExitStack()
        d = lambda n, s, ty, k: nc.dram_tensor(n, s, ty, kind=k).ap()
        self.x_in = d("x", [c.SEQS, c.T, c.D], F32, "ExternalInput")
        self.Wq = d("Wq", [c.D, c.D], F32, "ExternalInput")
        self.bq = d("bq", [c.D], F32, "ExternalInput")
        self.Wk = d("Wk", [c.DKV, c.D], F32, "ExternalInput")
        self.bk = d("bk", [c.DKV], F32, "ExternalInput")
        self.Wv = d("Wv", [c.DKV, c.D], F32, "ExternalInput")
        self.bv = d("bv", [c.DKV], F32, "ExternalInput")
        self.Wo = d("Wo", [c.D, c.D], F32, "ExternalInput")
        self.bo = d("bo", [c.D], F32, "ExternalInput")
        self.Wgu = d("Wgu", [c.H2, c.D], F32, "ExternalInput")
        self.Wp = d("Wp", [c.D, c.Hh], F32, "ExternalInput")
        self.cos_in = d("cos_tab", [c.T, c.HD], F32, "ExternalInput")
        self.sin_in = d("sins_tab", [c.T, c.HD], F32, "ExternalInput")
        self.ident_in = d("identb_in", [128, 128], BF16, "ExternalInput")
        self.identf_in = d("identf_in", [128, 128], F32, "ExternalInput")
        self.onescol_in = d("onescol_in", [128, 1], BF16, "ExternalInput")
        self.gmat_in = d("gmat_in", [c.HKV, c.HQ], F32, "ExternalInput")
        self.out = d("out", [c.SEQS, c.T, c.D], F32, "ExternalOutput")
        self.scr_gu = d("scr_gu", [c.D, c.H2], BF16, "Internal")
        self.scr_p = d("scr_p", [c.Hh, c.D], BF16, "Internal")
        self.scr_x1tm = d("scr_x1tm", [c.NTOK, c.D], F32, "Internal")
        self.scr_x1fm = d("scr_x1fm", [c.D, c.NTOK], BF16, "Internal")
        self.scr_sum = d("scr_sum", [2, c.QB], F32, "Internal")
        self.scr_den = d("scr_den", [c.HQ, c.QB], F32, "Internal")
        self.scr_wqT = d("scr_wqT", [c.D, c.D], BF16, "Internal")
        self.scr_wkT = d("scr_wkT", [c.D, c.DKV], BF16, "Internal")
        self.scr_wvT = d("scr_wvT", [c.D, c.DKV], BF16, "Internal")
        self.scr_woT = d("scr_woT", [c.D, c.D], BF16, "Internal")
        self.scr_xfm = d("scr_xfm", [c.D, c.NTOK], BF16, "Internal")
        self.scr_rcp = d("scr_rcp", [2, c.QB], F32, "Internal")
        with self.es:
            self._alloc()
            self._program()
        return nc

    def _alloc(self):
        c, nc, es = self.c, self.nc, self.es
        sb = lambda n, s, ty: es.enter_context(nc.sbuf_tensor(n, s, ty))
        pt = lambda n, s, ty: es.enter_context(nc.psum_tensor(n, s, ty))
        need_attn = (c.KVC * c.T // 2 + (c.TC * c.HKV * 65 + 1) // 2
                     + max(2048, c.DC * 128)
                     + max(c.DC * c.DKV // 2, c.DC * 256) + 1024)
        self.N1 = max(need_attn, 14336)
        self.ar1 = sb("ar1", [128, self.N1], F32)
        self.N2 = max(c.DC * c.QB, 12800)
        self.ar2 = sb("ar2", [128, self.N2], F32)
        self.ar4 = sb("ar4", [128, 4096], F32)
        self.ar5 = sb("ar5", [128, 2 * 2048], F32)
        self.ar6 = sb("ar6", [128, 2048], F32)
        self.ar7 = sb("ar7", [128, 3072], F32)
        self.ar8 = sb("ar8", [128, 2560], F32)
        self.ident = sb("ident", [128, 128], BF16)
        self.identf = sb("identf", [128, 128], F32)
        self.ones_col = sb("ones_col", [128, 1], BF16)
        self.gmat = sb("gmat", [c.HKV, c.HQ], F32)
        self.ones3 = sb("ones3", [65, 128], F32)
        self.brow = sb("brow", [65, max(2 * c.DKV, c.D)], F32)
        self.bn6 = sb("bn6", [128, 24], F32)
        self.colA = sb("colA", [128, 8], F32)
        self.colB = sb("colB", [128, 8], F32)
        self.normQ = sb("normQ", [128, (c.QB // 128) * c.HQ], F32)
        self.normK = sb("normK", [128, c.TC * c.HKV], F32)
        self.biasmat = sb("biasmat", [128, c.HQ], F32)
        self.kmaxc = sb("kmaxc", [c.HKV, 2], F32)
        self.hqc = sb("hqc", [c.HQ, 8], F32)
        self.rcol = sb("rcol", [128, 48], F32)
        self.rowz = self.ar8[0:1, 0:1024]
        self.sums = self.ar8[0:c.HQ, 1024:1024 + c.QB]
        self.rowz_f = self.ar7[0:1, 512:1024]
        self.PS1 = pt("PS1", [128, 2048], F32)
        self.PS2 = pt("PS2", [128, 2048], F32)
        for nm in ("ct", "cv", "cs", "cg", "d_cst", "d_st0", "d_st1", "d_ws",
                   "d_ws1", "d_wsl", "d_xf", "d_x1", "d_x1b", "d_xr", "d_su0", "d_su1", "d_rb0", "d_rb1",
                   "d_xw", "d_sg0", "d_sg1", "d_sp0", "d_sp1", "d_sp2", "d_out"):
            setattr(self, nm, self.sem(nm))
        self.st_tok = [None, None]
        self.lc_g = [None, None]
        self.ps1q = [None, None, None, None]
        self.trg = [None, None, None]
        self._nrot = 3
        self._treg = 0
        self._stsl = 0

    def bf(self, ar, off, n, p0=0, p1=128):
        return ar[p0:p1, off:off + n].bitcast(BF16)

    def ps1(self, i):
        return self.PS1[:, i * 512:(i + 1) * 512]

    def transpose_group(self, tiles, dst, waits=(), f32=False, evac=None):
        r = self._treg = (self._treg + 1) % self._nrot
        nt = len(tiles)
        if f32:
            assert nt <= 4
            ps = self.PS2[:, r * 512:r * 512 + nt * 128]
            ident = self.identf
        else:
            assert nt <= 8
            ps = self.PS2[:, r * 512:(r + 1) * 512].bitcast(BF16)[:, 0:nt * 128]
            ident = self.ident
        tk = None
        w0 = tuple(waits) + (self.trg[r],) + self.bar
        for j, s in enumerate(tiles):
            o = ps[:, j * 128:(j + 1) * 128]
            tk = self.op("t", lambda h, o=o, i=s, idn=ident:
                         h.transpose(o, i, idn[:]),
                         waits=w0 if j == 0 else (), inc=self.ct)
        if evac is None:
            pv = ps.rearrange("p (n f) -> p n f", f=128)
            ev = self.op("v", lambda h, d=dst, s=pv: h.tensor_copy(d, s),
                         waits=(tk,), inc=self.cv)
        else:
            ev = evac(ps, tk)
        self.trg[r] = ev
        return ev

    def load_cast(self, src, n, waits=(), eng="q", dst_off=None, dst_ar=None):
        sl = self._stsl = 1 - self._stsl
        st = [self.d_st0, self.d_st1][sl]
        stage = self.ar6[:, sl * 1024: sl * 1024 + n]
        ld = self.dma(eng, stage, src, st, waits=(self.st_tok[sl],) + self.bar)
        off = sl * 512 if dst_off is None else dst_off
        dst = self.bf(self.ar7 if dst_ar is None else dst_ar, off, n // 2)
        gate = self.lc_g[sl] if dst_off is None else None
        cs = self.op("s", lambda h, d=dst, s=stage: h.copy(d, s),
                     waits=(ld, gate) + tuple(waits) + self.bar, inc=self.cs)
        self.st_tok[sl] = cs
        self._lc_sl = sl
        return dst, cs

    def wslab(self, scrT, tot, dst, col0, ncols, waits=()):
        c = self.c
        last = None
        for d4 in range(c.DC // 4):
            srcap = bass.AP(scrT.tensor, scrT[d4 * 512, col0].offset,
                            [[tot, 128], [128 * tot, 4], [1, ncols]])
            last = self.dma("q", dst[:, d4 * 4:(d4 + 1) * 4, :], srcap,
                            self.d_wsl, waits=waits if d4 == 0 else ())
        return last

    def build_wT(self, W, wT, row0, ncols):
        c = self.c
        ev = None
        for rch in range(ncols // 128):
            for hf in range((c.D + 1023) // 1024):
                w = min(1024, c.D - hf * 1024)
                src = W[row0 + rch * 128: row0 + rch * 128 + 128,
                        hf * 1024: hf * 1024 + w]
                wbf, cs = self.load_cast(src, w)
                sl = self._lc_sl
                tiles = [wbf[:, j * 128:(j + 1) * 128] for j in range(w // 128)]
                dst = wT[:, hf * 8: hf * 8 + w // 128, rch * 128: rch * 128 + 128]
                ev = self.transpose_group(tiles, dst, waits=(cs,))
                self.lc_g[sl] = ev
        return ev

    def _program(self):
        self._consts()
        self.flush()
        self.barrier()
        self._quant()
        self.ln_done = {}
        for seq in range(self.c.SEQS):
            self._attn_seq(seq)
        self.flush()
        self.barrier()
        self.quant_drain()
        self._ffn()
        self._finish()
        self.flush()

    def _consts(self):
        c = self.c
        t = lambda d_, s_: self.dma("q", d_, s_, self.d_cst)
        t(self.ident[:], self.ident_in)
        t(self.identf[:], self.identf_in)
        t(self.ones_col[:], self.onescol_in)
        t(self.gmat[:], self.gmat_in)
        self.o_cos, self.o_sin = 0, c.TC * c.HD
        t(self.ar4[:, self.o_cos:self.o_cos + c.TC * c.HD]
          .rearrange("p (tc f) -> p tc f", f=c.HD),
          self.cos_in.rearrange("(tc p) f -> p tc f", p=128))
        t(self.ar4[:, self.o_sin:self.o_sin + c.TC * c.HD]
          .rearrange("p (tc f) -> p tc f", f=c.HD),
          self.sin_in.rearrange("(tc p) f -> p tc f", p=128))
        t(self.brow[0:1, 0:c.D], self.bq.rearrange("(o d) -> o d", o=1))
        t(self.brow[32:33, 0:c.D], self.bo.rearrange("(o d) -> o d", o=1))
        t(self.brow[64:65, 0:c.DKV], self.bk.rearrange("(o d) -> o d", o=1))
        t(self.brow[64:65, c.DKV:2 * c.DKV], self.bv.rearrange("(o d) -> o d", o=1))
        self.tok_consts = self.tok("d_cst")
        self.tok_ones = self.op("g", lambda h: h.memset(self.ones3[:], 1.0),
                                inc=self.cg)

    def quant_step(self, n=1):
        g = getattr(self, "_qgen", None)
        if g is None:
            return
        for _ in range(n):
            try:
                next(g)
            except StopIteration:
                self._qgen = None
                self.tok_quant = self.tok("d_ws")
                self.tok_quant1 = self.tok("d_ws1")
                return

    def quant_drain(self):
        while getattr(self, "_qgen", None) is not None:
            self.quant_step(8)

    def _quant(self):
        self.wdone = {}
        self._wt_marks = {8: "wk", 16: "wv", 48: "wq", 80: "wo"}
        self._qgen = self._quant_gen()
        self.quant_step(38)

    def _quant_gen(self):
        c = self.c
        WBO = [1024, 1536, 3072, 3584]
        WBA = [None, None, self.ar4, self.ar4]
        RQ = [2048, 2560]
        STG = [2048, 2560]
        SC = 2048
        cons = [None, None, None, None]
        wsd = [None, None]

        def piece(idx, src, dst_fn, n, quant=True):
            sl = idx % 2
            s4 = idx % 4
            g = n // 64
            wb, tcast = self.load_cast(src, n, waits=(cons[s4],),
                                       dst_off=WBO[s4], dst_ar=WBA[s4])
            if not quant:
                t8 = tcast
                return _mk_stage2(sl, s4, wb, t8, dst_fn, n)
            wbv = wb.rearrange("p (g k) -> p g k", k=64)
            sums = self.ar8[:, SC + s4 * 128: SC + s4 * 128 + g]
            sclbf = self.bf(self.ar8, SC + s4 * 128 + 32, 8)[:, 0:g]
            rcp = self.ar8[:, SC + s4 * 128 + 48: SC + s4 * 128 + 48 + g]
            t1 = self.op("v", lambda h, o=sums, i=wbv: h.reduce_sum(
                o.rearrange("p (g o2) -> p g o2", o2=1), i, axis=AX.X,
                apply_absolute_value=True), waits=(tcast,), inc=self.cv)
            t2 = self.op("v", lambda h, o=sclbf, i=sums: h.tensor_scalar(
                o, i, 1.0 / 64, 1e-8, ALU.mult, ALU.max), waits=(t1,), inc=self.cv)
            t3 = self.op("v", lambda h, o=rcp, i=sclbf: h.reciprocal(o, i),
                         waits=(t2,), inc=self.cv)
            r = self.bf(self.ar7, RQ[sl], n // 2)
            rv = r.rearrange("p (g k) -> p g k", k=64)
            rcpb = bass.AP(rcp.tensor, rcp.offset,
                           [list(rcp.ap[0]), [1, g], [0, 64]])
            t4 = self.op("g", lambda h, o=rv, a=wbv, b=rcpb:
                         h.tensor_tensor(o, a, b, ALU.mult),
                         waits=(t3, wsd[sl]) + self.bar, inc=self.cg)
            t5 = self.op("v", lambda h, o=r, i=r: h.tensor_scalar(
                o, i, 192.0, None, ALU.add), waits=(t4,), inc=self.cv)
            t6 = self.op("v", lambda h, o=r, i=r: h.tensor_scalar(
                o, i, -192.0, 1.0, ALU.add, ALU.min), waits=(t5,), inc=self.cv)
            t7 = self.op("v", lambda h, o=r, i=r: h.tensor_scalar(
                o, i, -1.0, None, ALU.max), waits=(t6,), inc=self.cv)
            sclbb = bass.AP(sclbf.tensor, sclbf.offset,
                            [list(sclbf.ap[0]), [1, g], [0, 64]])
            t8 = self.op("g", lambda h, o=wbv, a=rv, b=sclbb:
                         h.tensor_tensor(o, a, b, ALU.mult), waits=(t7,), inc=self.cg)

            return _mk_stage2(sl, s4, wb, t8, dst_fn, n)

        def _mk_stage2(sl, s4, wb, t8, dst_fn, n):
            def stage2():
                nt = n // 128
                stg = self.bf(self.ar7, STG[sl], nt * 64)
                stgv = stg.rearrange("p (t f) -> p t f", f=128)
                ev = self.transpose_group(
                    [wb[:, j * 128:(j + 1) * 128] for j in range(nt)], stgv,
                    waits=(t8, wsd[sl]))
                cons[s4] = ev
                last = None
                wsem = [self.d_ws, self.d_ws1][sl]
                for j4 in range(0, nt, 4):
                    k = min(4, nt - j4)
                    last = self.dma("g" if sl else "q", dst_fn(j4, k),
                                    stgv[:, j4:j4 + k, :], wsem, waits=(ev,))
                wsd[sl] = last
                self._st2_done = getattr(self, "_st2_done", 0) + 1
                if self._st2_done in self._wt_marks:
                    self.wdone[self._wt_marks[self._st2_done]] = (
                        self.tok("d_ws"), self.tok("d_ws1"))
            return stage2

        def sources():
            for W, scrT, outr in ((self.Wk, self.scr_wkT, c.DKV),
                                  (self.Wv, self.scr_wvT, c.DKV),
                                  (self.Wq, self.scr_wqT, c.D),
                                  (self.Wo, self.scr_woT, c.D)):
                for rch in range(outr // 128):
                    for jc in range(c.D // 1024):
                        def dW(j4, k, rch=rch, jc=jc, scrT=scrT):
                            r0 = jc * 1024 + j4 * 128
                            return scrT[r0:r0 + k * 128,
                                        rch * 128:(rch + 1) * 128] \
                                .rearrange("(n p) f -> p n f", p=128)
                        yield (W[rch * 128:(rch + 1) * 128,
                                 jc * 1024:(jc + 1) * 1024], dW, 1024, False)
            wcol = min(1024, c.D)
            for rch in range(c.H2 // 128):
                for jc in range(c.D // wcol):
                    def dgu(j4, k, rch=rch, jc=jc):
                        r0 = jc * wcol + j4 * 128
                        return self.scr_gu[r0:r0 + k * 128,
                                           rch * 128:(rch + 1) * 128] \
                            .rearrange("(n p) f -> p n f", p=128)
                    yield (self.Wgu[rch * 128:(rch + 1) * 128,
                                    jc * wcol:(jc + 1) * wcol], dgu, wcol, True)
            wcol2 = min(1024, c.Hh)
            for rch in range(c.D // 128):
                for jc in range(c.Hh // wcol2):
                    def dp(j4, k, rch=rch, jc=jc):
                        r0 = jc * wcol2 + j4 * 128
                        return self.scr_p[r0:r0 + k * 128,
                                          rch * 128:(rch + 1) * 128] \
                            .rearrange("(n p) f -> p n f", p=128)
                    yield (self.Wp[rch * 128:(rch + 1) * 128,
                                   jc * wcol2:(jc + 1) * wcol2], dp, wcol2, True)
        idx = 0
        pend2 = []
        for srcw, dfn, ncol, isq in sources():
            st2 = piece(idx, srcw, dfn, ncol, quant=isq)
            idx += 1
            yield
            pend2.append(st2)
            if len(pend2) >= 2:
                pend2.pop(0)()
                yield
        while pend2:
            pend2.pop(0)()
            yield

    def _attn_seq(self, seq):
        c = self.c
        self.o_kfm = 0
        self.o_vh = c.KVC * c.T // 2
        self.o_xw = self.o_vh + (c.TC * c.HKV * 65 + 1) // 2
        self.o_wt = self.o_xw + max(2048, c.DC * 128)
        self.o_tm = self.o_wt + max(c.DC * c.DKV // 2, c.DC * 256)
        kfm = self.bf(self.ar1, self.o_kfm, c.KVC * c.T // 2) \
            .rearrange("p (kc t) -> p kc t", t=c.T)
        nvh = c.TC * c.HKV * 65
        vh = self.bf(self.ar1, self.o_vh, (nvh + 1) // 2)[:, 0:nvh] \
            .rearrange("p (tc kv f) -> p tc kv f", kv=c.HKV, f=65)
        self.flush()
        self.barrier()
        self._s1kv(seq, kfm, vh)
        self.flush()
        for qb in range(c.NQB):
            qfm = self.bf(self.ar2, 0, c.DC * c.QB // 2) \
                .rearrange("p (dc t) -> p dc t", t=c.QB)
            ofm = self.bf(self.ar2, c.DC * c.QB // 2, c.DC * c.QB // 2) \
                .rearrange("p (dc t) -> p dc t", t=c.QB)
            self.barrier()
            self._s1q(seq, qb, qfm)
            self.flush()
            self.barrier()
            self._biasmat()
            self.flush()
            self.barrier()
            self._s2(seq, qb, qfm, kfm, vh, ofm)
            self.flush()
            self.barrier()
            self._s3(seq, qb, ofm)
            self.flush()

    def _xtile(self, seq, tc, off, gate=None):
        c = self.c
        dst = self.bf(self.ar1, off, c.DC * 64) \
            .rearrange("p (dc f) -> p dc f", f=128)
        ev = None
        for hf in range((c.D + 1023) // 1024):
            w = min(1024, c.D - hf * 1024)
            src = self.x_in[seq, tc * 128:(tc + 1) * 128, hf * 1024:hf * 1024 + w]
            xbf, cs = self.load_cast(src, w, waits=(gate,))
            sl = self._lc_sl
            tiles = [xbf[:, j * 128:(j + 1) * 128] for j in range(w // 128)]
            ev = self.transpose_group(tiles, dst[:, hf * 8: hf * 8 + w // 128, :],
                                      waits=(cs,))
            self.lc_g[sl] = ev
        return dst, ev

    def _rope(self, ps, dstview, tc, n, waits):
        par = list(self.ar4[:].ap[0])
        cosb = bass.AP(self.ar4[:].tensor, self.ar4[:, self.o_cos + tc * 64].offset,
                       [par, [0, n], [1, 64]])
        sinb = bass.AP(self.ar4[:].tensor, self.ar4[:, self.o_sin + tc * 64].offset,
                       [par, [0, n], [32, 2], [1, 32]])
        psv = ps.rearrange("p (h f) -> p h f", f=64)
        psw = bass.AP(ps.tensor, ps.offset + 32,
                      [list(ps.ap[0]), [64, n], [-32, 2], [1, 32]])
        ms = self._mrot = 1 - getattr(self, "_mrot", 0)
        m1 = self.ar8[:, ms * 1024:ms * 1024 + n * 64]
        m2 = self.ar8[:, ms * 1024 + 512:ms * 1024 + 512 + n * 64]
        mg = getattr(self, "_rope_gates", [None, None])
        a = self.op("v", lambda h: h.tensor_tensor(
            m1.rearrange("p (h f) -> p h f", f=64), psv, cosb, ALU.mult),
            waits=tuple(waits) + (mg[ms],) + self.bar, inc=self.cv)
        b = self.op("v", lambda h: h.tensor_tensor(
            m2.rearrange("p (h two f) -> p h two f", two=2, f=32), psw, sinb,
            ALU.mult), inc=self.cv)
        e = self.op("v", lambda h: h.tensor_tensor(
            dstview, m1.rearrange("p (h f) -> p h f", f=64),
            m2.rearrange("p (h f) -> p h f", f=64), ALU.add),
            waits=(a, b), inc=self.cv)
        mg[ms] = e
        self._rope_gates = mg
        return e

    def _s1kv(self, seq, kfm, vh):
        c = self.c
        vones = self.op("g", lambda h: h.memset(
            bass.AP(vh.tensor, vh[:, 0, 0, 64:65].offset,
                    [list(vh.ap[0]), [65, c.TC * c.HKV], [1, 1]]), 1.0),
            waits=(self.tok_consts,) + self.bar, inc=self.cg)
        wkT = self.bf(self.ar1, self.o_wt, c.DC * c.DKV // 2) \
            .rearrange("p (dc f) -> p dc f", f=c.DKV)
        wktok = self.wslab(self.scr_wkT, c.DKV, wkT, 0, c.DKV,
                           waits=self.wdone["wk"] + self.bar)
        wvT = self.bf(self.ar2, 0, c.DC * c.DKV // 2) \
            .rearrange("p (dc f) -> p dc f", f=c.DKV)
        wvtok = self.wslab(self.scr_wvT, c.DKV, wvT, 0, c.DKV,
                           waits=self.wdone["wv"] + self.bar)
        self._ktm_g = [None, None]
        for tc in range(c.TC):
            sl = tc % 2
            xt, xtok = self._xtile(seq, tc, self.o_xw + sl * 1024,
                                   gate=getattr(self, "_xt_g", [None, None])[sl])
            col0 = seq * c.T + tc * 128
            xfout = self.dma("q", bass.AP(
                self.scr_xfm.tensor, self.scr_xfm[0, col0].offset,
                [[c.NTOK, 128], [128 * c.NTOK, c.DC], [1, 128]]),
                xt, self.d_xf, waits=(xtok,))
            ps = self.PS1[:, sl * 1024: sl * 1024 + c.DKV]
            psv = self.PS1[:, sl * 1024 + c.DKV: sl * 1024 + 2 * c.DKV]
            for dc in range(c.DC):
                self.op("t", lambda h, o=ps, l=xt[:, dc, :], r=wkT[:, dc, :],
                        st=(dc == 0):
                        h.matmul(o, l, r, start=st, stop=False,
                                 skip_group_check=True),
                        waits=(xtok, wktok, wvtok, self.ps1q[2 * sl],
                               self.ps1q[2 * sl + 1])
                        + self.bar if dc == 0 else ())
            mm = self.op("t", lambda h, o=ps: h.matmul(
                o, self.ones3[64:65, :], self.brow[64:65, 0:c.DKV],
                start=False, stop=True, skip_group_check=True),
                waits=(self.tok_consts, self.tok_ones), inc=self.ct)
            self.quant_step(2)
            for dc in range(c.DC):
                self.op("t", lambda h, o=psv, l=xt[:, dc, :], r=wvT[:, dc, :],
                        st=(dc == 0):
                        h.matmul(o, l, r, start=st, stop=False,
                                 skip_group_check=True))
            mmv = self.op("t", lambda h, o=psv: h.matmul(
                o, self.ones3[64:65, :], self.brow[64:65, c.DKV:2 * c.DKV],
                start=False, stop=True, skip_group_check=True), inc=self.ct)
            self.quant_step(2)
            xg = getattr(self, "_xt_g", [None, None])
            xg[sl] = [mmv, xfout]
            self._xt_g = xg
            sq = self.ar5[:, 0:c.DKV]
            s1 = self.op("s", lambda h, o=sq, i=ps: h.activation(o, i, AF.Square),
                         waits=(mm, getattr(self, "_sq_g", None)), inc=self.cs)
            nk = self.normK[:, tc * c.HKV:(tc + 1) * c.HKV]
            self._sq_g = self.op(
                "v", lambda h, o=nk, i=sq.rearrange("p (kv f) -> p kv f", f=64):
                h.reduce_sum(o.rearrange("p (kv o2) -> p kv o2", o2=1), i,
                             axis=AX.X), waits=(s1,), inc=self.cv)
            self.quant_step(2)
            ktm = self.bf(self.ar1, self.o_tm + sl * 512, c.DKV // 2)
            rt = self._rope(ps, ktm.rearrange("p (h f) -> p h f", f=64),
                            tc, c.HKV, (mm, self._ktm_g[sl]))
            self.ps1q[2 * sl] = [rt, s1]
            ev = self.transpose_group(
                [ktm[:, j * 128:(j + 1) * 128] for j in range(c.KVC)],
                bass.AP(kfm.tensor, kfm[:, 0, tc * 128].offset,
                        [list(kfm.ap[0]), [c.T, c.KVC], [1, 128]]),
                waits=(rt,))
            self._ktm_g[sl] = ev
            self.quant_step(2)
            dstv = bass.AP(vh.tensor, vh[:, tc, 0, 0].offset,
                           [list(vh.ap[0]), [65, c.HKV], [1, 64]])
            evv = self.op("v", lambda h, o=dstv,
                          i=psv.rearrange("p (kv f) -> p kv f", f=64):
                          h.tensor_copy(o, i), waits=(mmv, vones), inc=self.cv)
            self.ps1q[2 * sl + 1] = evv
            self.quant_step(2)
        self.vh_tok = self.tok("cv")
        self.xf_tok = self.tok("d_xf")
        nkv = bass.AP(self.normK[:].tensor, self.normK[:].offset,
                      [list(self.normK[:].ap[0]), [1, c.HKV], [c.HKV, c.TC]])
        t1 = self.op("v", lambda h: h.reduce_max(
            self.colA[:, 0:c.HKV].rearrange("p (kv o2) -> p kv o2", o2=1), nkv,
            axis=AX.X), waits=(self._sq_g,), inc=self.cv)
        t2 = self.op("t", lambda h: h.transpose(
            self.PS2[0:c.HKV, 1536:1664], self.colA[:, 0:c.HKV], self.identf[:]),
            waits=(t1,), inc=self.ct)
        t3 = self.op("v", lambda h: h.reduce_max(
            self.kmaxc[:, 0:1].rearrange("p (o o2) -> p o o2", o2=1),
            self.PS2[0:c.HKV, 1536:1664].rearrange("p (o f) -> p o f", o=1),
            axis=AX.X), waits=(t2,), inc=self.cv)
        self.kmax_tok = t3

    def _s1q(self, seq, qb, qfm):
        c = self.c
        qf_tok = None
        self._qtm_g = [None, None, None, None]
        NQT = c.QB // 128
        xqb = self.bf(self.ar2, c.DC * c.QB // 2, c.DC * c.QB // 2) \
            .rearrange("p (dc t) -> p dc t", t=c.QB)
        col0 = seq * c.T + qb * c.QB
        xlast = None
        for d4 in range(c.DC // 4):
            xlast = self.dma("q", xqb[:, d4 * 4:(d4 + 1) * 4, :], bass.AP(
                self.scr_xfm.tensor, self.scr_xfm[d4 * 512, col0].offset,
                [[c.NTOK, 128], [128 * c.NTOK, 4], [1, c.QB]]), self.d_xf,
                waits=(self.xf_tok,) + self.bar if d4 == 0 else ())
            self.quant_step(6)
        xtoks = [xlast] * NQT
        wq_cons = None
        for d8p in range(c.HQ // 8):
            wq8 = self.bf(self.ar1, self.o_wt, c.DC * 256) \
                .rearrange("p (dc f) -> p dc f", f=512)
            wtok = self.wslab(self.scr_wqT, c.D, wq8, d8p * 512, 512,
                              waits=self.wdone["wq"] + (wq_cons,) + self.bar)
            for tq in range(NQT):
                tcg = qb * NQT + tq
                sl = tq % 4
                ps = self.ps1(sl)
                for dc in range(c.DC):
                    self.op("t", lambda h, o=ps,
                            l=xqb[:, dc, tq * 128:(tq + 1) * 128],
                            r=wq8[:, dc, :], st=(dc == 0):
                            h.matmul(o, l, r, start=st, stop=False,
                                     skip_group_check=True),
                            waits=(xtoks[tq], wtok, self.ps1q[sl])
                            if dc == 0 else ())
                mm = self.op("t", lambda h, o=ps, d0=d8p * 512: h.matmul(
                    o, self.ones3[0:1, :], self.brow[0:1, d0:d0 + 512],
                    start=False, stop=True, skip_group_check=True), inc=self.ct)
                self.quant_step(1)
                sq = self.ar5[:, 0:512]
                s1 = self.op("s", lambda h, o=sq, i=ps: h.activation(
                    o, i, AF.Square),
                    waits=(mm, getattr(self, "_sq_g", None)), inc=self.cs)
                nq = self.normQ[:, tq * c.HQ + d8p * 8: tq * c.HQ + d8p * 8 + 8]
                self._sq_g = self.op(
                    "v", lambda h, o=nq,
                    i=sq.rearrange("p (g f) -> p g f", f=64):
                    h.reduce_sum(o.rearrange("p (g o2) -> p g o2", o2=1), i,
                                 axis=AX.X), waits=(s1,), inc=self.cv)
                qt_b = self.bf(self.ar1, self.o_tm + sl * 256, 256)
                dstv = bass.AP(qt_b.tensor, qt_b.offset,
                               [list(qt_b.ap[0]), [64, 2], [128, 4], [1, 64]])
                rt = self._rope(ps, dstv, tcg, 8,
                                (mm, self._qtm_g[sl]))
                self.quant_step(1)
                self.ps1q[sl] = [rt, s1]
                ev = self.transpose_group(
                    [qt_b[:, j * 128:(j + 1) * 128] for j in range(4)],
                    qfm[:, 4 * d8p:4 * d8p + 4, tq * 128:(tq + 1) * 128],
                    waits=(rt,))
                self._qtm_g[sl] = ev
                qf_tok = ev
                self.quant_step(2)
                if tq == NQT - 1:
                    wq_cons = mm
        self.q_ready = qf_tok

    def _biasmat(self):
        c = self.c
        NQT = c.QB // 128
        nqv = bass.AP(self.normQ[:].tensor, self.normQ[:].offset,
                      [list(self.normQ[:].ap[0]), [1, c.HQ], [c.HQ, NQT]])
        qmax_s = self.ar8[:, 512:512 + c.HQ]
        t1 = self.op("v", lambda h: h.reduce_max(
            qmax_s.rearrange("p (hq o2) -> p hq o2", o2=1), nqv, axis=AX.X),
            waits=self.bar, inc=self.cv)
        t2 = self.op("t", lambda h: h.transpose(
            self.PS2[0:c.HQ, 1536:1664], qmax_s, self.identf[:]),
            waits=(t1,) + self.bar, inc=self.ct)
        t3 = self.op("v", lambda h: h.reduce_max(
            self.hqc[:, 0:1].rearrange("p (o o2) -> p o o2", o2=1),
            self.PS2[0:c.HQ, 1536:1664].rearrange("p (o f) -> p o f", o=1),
            axis=AX.X), waits=(t2,), inc=self.cv)
        t4 = self.op("t", lambda h: h.matmul(
            self.PS2[0:c.HQ, 1664:1665], self.gmat[:], self.kmaxc[:, 0:1],
            start=True, stop=True, skip_group_check=True),
            waits=(self.kmax_tok, t3), inc=self.ct)
        t5 = self.op("v", lambda h: h.tensor_copy(
            self.hqc[:, 1:2], self.PS2[0:c.HQ, 1664:1665]), waits=(t4,), inc=self.cv)
        t6 = self.op("v", lambda h: h.tensor_tensor(
            self.hqc[:, 2:3], self.hqc[:, 0:1], self.hqc[:, 1:2], ALU.mult),
            waits=(t5,), inc=self.cv)
        t6b = self.op("v", lambda h: h.tensor_scalar(
            self.hqc[:, 2:3], self.hqc[:, 2:3], 1e-36, None, ALU.max),
            waits=(t6,), inc=self.cv)
        t7 = self.op("s", lambda h: h.activation(
            self.hqc[:, 3:4], self.hqc[:, 2:3], AF.Ln),
            waits=(t6b,) + self.bar, inc=self.cs)
        t8 = self.op("s", lambda h: h.activation(
            self.hqc[:, 4:5], self.hqc[:, 3:4], AF.Exp, scale=0.5),
            waits=(t7,), inc=self.cs)
        t9 = self.op("v", lambda h: h.tensor_scalar(
            self.hqc[:, 5:6], self.hqc[:, 4:5], -0.125, None, ALU.mult),
            waits=(t8,), inc=self.cv)
        ta = self.op("t", lambda h: h.transpose(
            self.PS2[0:1, 1536:1536 + c.HQ], self.hqc[0:c.HQ, 5:6],
            self.identf[0:c.HQ, 0:c.HQ]), waits=(t9,), inc=self.ct)
        tb = self.op("v", lambda h: h.tensor_copy(
            self.rowz[0:1, 0:c.HQ], self.PS2[0:1, 1536:1536 + c.HQ]),
            waits=(ta, getattr(self, "_rowz_g", None)), inc=self.cv)
        tc_ = self.op("t", lambda h: h.matmul(
            self.PS2[:, 1536:1536 + c.HQ], self.ones3[0:1, :],
            self.rowz[0:1, 0:c.HQ], start=True, stop=True,
            skip_group_check=True), waits=(tb,), inc=self.ct)
        td = self.op("v", lambda h: h.tensor_copy(
            self.biasmat[:], self.PS2[:, 1536:1536 + c.HQ]),
            waits=(tc_,), inc=self.cv)
        self._rowz_g = tc_
        self.biasmat_tok = td

    def _s2(self, seq, qb, qfm, kfm, vh, ofm):
        c = self.c
        QB = c.QB
        NJ = QB // 512
        atb_off = [0, QB // 2, 2048, 2048 + QB // 2]
        atb = [[self.bf(self.ar7 if s == 0 else self.ar4,
                        atb_off[2 * s + r], QB // 2)
                for r in range(2)] for s in range(2)]
        exp_g = [None, None]
        av_g = [[None, None], [None, None]]
        o_g = [None, None]
        for ip, (hlo, hhi) in enumerate(c.pairs):
            kvl, kvh_ = hlo // c.G, hhi // c.G
            av_last = [None, None]

            def emit_av(kt, side, kv, first_extra=()):
                rot = kt % 2
                for jh in range(NJ):
                    w = ([exp_g[side]] + list(first_extra)) if jh == 0 else ()
                    self.op("t", lambda h,
                            o=self.PS2[0:65, side * 1024 + jh * 512:
                                       side * 1024 + (jh + 1) * 512],
                            l=vh[:, kt, kv, :],
                            r=atb[side][rot][:, jh * 512:(jh + 1) * 512],
                            st=(kt == 0), sp=(kt == c.KT - 1):
                            h.matmul(o, l, r, start=st, stop=sp,
                                     skip_group_check=True),
                            waits=tuple(w), inc=self.ct if jh == NJ - 1 else None)
                av_g[side][rot] = av_last[side] = self.tok("ct")

            pend_av = None
            for kt in range(c.KT):
                rot = kt % 2
                pslo = self.PS1[:, 0:QB]
                pshi = self.PS1[:, 1024:1024 + QB]
                for jh in range(NJ):
                    w = ((self.q_ready, self.biasmat_tok, self.vh_tok, exp_g[0])
                         + self.bar if jh == 0 else ())
                    self.op("t", lambda h, o=pslo[:, jh * 512:(jh + 1) * 512],
                            l=kfm[0:64, kvl // 2, kt * 128:(kt + 1) * 128],
                            r=qfm[0:64, ip, jh * 512:(jh + 1) * 512]:
                            h.matmul(o, l, r, start=True, stop=True,
                                     tile_position=(0, 0), skip_group_check=True),
                            waits=w, inc=self.ct if jh == NJ - 1 else None)
                mlo = self.tok("ct")
                for jh in range(NJ):
                    w = ((exp_g[1],) if jh == 0 else ())
                    self.op("t", lambda h, o=pshi[:, jh * 512:(jh + 1) * 512],
                            l=kfm[64:128, kvh_ // 2, kt * 128:(kt + 1) * 128],
                            r=qfm[64:128, ip, jh * 512:(jh + 1) * 512]:
                            h.matmul(o, l, r, start=True, stop=True,
                                     tile_position=(64, 0), skip_group_check=True),
                            waits=w, inc=self.ct if jh == NJ - 1 else None)
                mhi = self.tok("ct")
                if pend_av is not None:
                    pend_av()
                elo = self.op("s", lambda h, o=atb[0][rot], i=pslo,
                              b=self.biasmat[:, hlo:hlo + 1]:
                              h.activation(o, i, AF.Exp, bias=b, scale=0.125),
                              waits=(mlo, av_g[0][rot]) + self.bar, inc=self.cs)
                ehi = self.op("s", lambda h, o=atb[1][rot], i=pshi,
                              b=self.biasmat[:, hhi:hhi + 1]:
                              h.activation(o, i, AF.Exp, bias=b, scale=0.125),
                              waits=(mhi, av_g[1][rot]), inc=self.cs)
                exp_g = [elo, ehi]

                def pend(kt=kt, fe0=(o_g[0],) if kt == 0 else (),
                         fe1=(o_g[1],) if kt == 0 else ()):
                    emit_av(kt, 0, kvl, fe0)
                    emit_av(kt, 1, kvh_, fe1)
                pend_av = pend
            pend_av()
            for side, hd in ((0, hlo), (1, hhi)):
                psO = self.PS2[:, side * 1024: side * 1024 + QB]
                ssu = [self.d_su0, self.d_su1][side]
                dchunk, roff = hd // 2, (hd % 2) * 64
                t0 = self.op("v", lambda h, o=ofm[roff:roff + 64, dchunk, :],
                             i=psO[0:64, :]: h.tensor_copy(o, i),
                             waits=(av_last[side],), inc=self.cv)
                t1 = self.op("v", lambda h, o=self.rowz[0:1, 0:QB],
                             i=psO[64:65, :]: h.tensor_copy(o, i),
                             waits=(t0, getattr(self, "_rowz_g", None)),
                             inc=self.cv)
                o_g[side] = t1
                dsum = self.dma("q", self.sums[hd:hd + 1, 0:QB],
                                self.rowz[0:1, 0:QB], ssu, waits=(t1,))
                self._rowz_g = dsum
        rcpv = self.op("v", lambda h: h.reciprocal(
            self.sums[0:c.HQ, 0:QB], self.sums[0:c.HQ, 0:QB]),
            waits=(self.tok("d_su0"), self.tok("d_su1")), inc=self.cv)
        dout = self.dma("q", self.scr_den, self.sums[0:c.HQ, 0:QB],
                        self.d_su0, waits=(rcpv,))
        mult_g = [None] * 4
        for hd in range(c.HQ):
            sl = hd % 4
            dchunk, roff = hd // 2, (hd % 2) * 64
            rbb = self.ar5[roff:roff + 64, sl * 1024: sl * 1024 + QB]
            din = self.dma("q", rbb, bass.AP(
                self.scr_den.tensor, self.scr_den[hd, 0].offset,
                [[0, 64], [1, QB]]), self.d_rb0,
                waits=(dout, mult_g[sl]) if hd < 4 else (mult_g[sl],))
            mult_g[sl] = self.op(
                "v", lambda h, o=ofm[roff:roff + 64, dchunk, :], b=rbb:
                h.tensor_tensor(o, o, b, ALU.mult), waits=(din,), inc=self.cv)
        self._rope_gate = None
        self.ps1q = [self.tok("cs")] * 4
        self.ofm_tok = self.tok("cv")

    def _s3(self, seq, qb, ofm):
        c = self.c
        NQT = c.QB // 128
        DH = c.D // 2
        woT = self.bf(self.ar2, 0, c.DC * DH // 2) \
            .rearrange("p (dc f) -> p dc f", f=DH)
        attnout = self.ar5[:, 0:c.D]
        x_rd = self.ar5[:, c.D:2 * c.D]

        def stash_ap(qt):
            if qt < NQT - 1:
                return self.ar1[:, self.o_xw + qt * DH:
                                self.o_xw + (qt + 1) * DH]
            return self.ar8[:, 0:DH]
        stash_g = [None] * NQT
        ln_g = None
        wo_cons = None
        for half in range(2):
            wtok = self.wslab(self.scr_woT, c.D, woT, half * DH, DH,
                              waits=self.wdone["wo"] + (wo_cons,) + self.bar)
            for qt in range(NQT):
                sl = qt % 2
                ps = self.PS1[:, sl * 1024: sl * 1024 + DH]
                for dc in range(c.DC):
                    for jo in range(0, DH, 512):
                        wo_ = min(512, DH - jo)
                        self.op("t", lambda h, o=ps[:, jo:jo + wo_],
                                l=ofm[:, dc, qt * 128:(qt + 1) * 128],
                                r=woT[:, dc, jo:jo + wo_], st=(dc == 0):
                                h.matmul(o, l, r, start=st, stop=False,
                                         skip_group_check=True),
                                waits=(wtok, self.ofm_tok, self.ps1q[2 * sl],
                                       self.ps1q[2 * sl + 1]) + self.bar
                                if (dc == 0 and jo == 0) else ())
                for jo in range(0, DH, 512):
                    wo_ = min(512, DH - jo)
                    self.op("t", lambda h, o=ps[:, jo:jo + wo_], h0=half * DH,
                            jo=jo, wo_=wo_: h.matmul(
                        o, self.ones3[32:33, :],
                        self.brow[32:33, h0 + jo:h0 + jo + wo_],
                        start=False, stop=True, skip_group_check=True),
                        inc=self.ct if jo + 512 >= DH else None)
                mm = self.tok("ct")
                if half == 0:
                    ev = self.op("v", lambda h, o=stash_ap(qt), i=ps:
                                 h.tensor_copy(o, i),
                                 waits=(mm, stash_g[qt]) + self.bar, inc=self.cv)
                    stash_g[qt] = ev
                    self.ps1q[2 * sl] = self.ps1q[2 * sl + 1] = ev
                else:
                    ev = self.op("v", lambda h, o=attnout[:, DH:2 * DH], i=ps:
                                 h.tensor_copy(o, i), waits=(mm, ln_g), inc=self.cv)
                    self.ps1q[2 * sl] = self.ps1q[2 * sl + 1] = ev
                    cp = self.op("v", lambda h, o=attnout[:, 0:DH], i=stash_ap(qt):
                                 h.tensor_copy(o, i), waits=(stash_g[qt], ev),
                                 inc=self.cv)
                    ln_g = self._ln1(seq, qb, qt, attnout, x_rd, (cp, ev))
                if qt == NQT - 1:
                    wo_cons = mm

    def _ln1(self, seq, qb, qt, attnout, x_rd, waits):
        c = self.c
        tcg = qb * (c.QB // 128) + qt
        gq = seq * (c.T // 128) + tcg
        xl = self.dma("q", x_rd, self.x_in[seq, tcg * 128:(tcg + 1) * 128, :],
                      self.d_xr, waits=(getattr(self, "_xr_g", None),))
        s_ = x_rd
        t1 = self.op("v", lambda h, o=s_, a=attnout, b=x_rd:
                     h.tensor_tensor(o, a, b, ALU.add),
                     waits=tuple(waits) + (xl,), inc=self.cv)
        nbk = (c.D + 511) // 512
        t2 = None
        for bi in range(nbk):
            w = min(512, c.D - bi * 512)
            t2 = self.op("v", lambda h, o=self.bn6[:, bi * 6:bi * 6 + 6],
                         i=s_[:, bi * 512:bi * 512 + w]: h.bn_stats(o, i),
                         waits=(t1,), inc=self.cv)
        t3 = self.op("v", lambda h: h.bn_aggr(self.colA[:, 0:2],
                                              self.bn6[:, 0:6 * nbk]),
                     waits=(t2,), inc=self.cv)
        t4 = self.op("v", lambda h: h.tensor_scalar(
            self.colA[:, 2:3], self.colA[:, 1:2], LN_EPS, None, ALU.add),
            waits=(t3,), inc=self.cv)
        t5 = self.op("v", lambda h: h.reciprocal(
            self.colA[:, 3:4], self.colA[:, 2:3]), waits=(t4,), inc=self.cv)
        t6 = self.op("s", lambda h: h.activation(
            self.colA[:, 4:5], self.colA[:, 3:4], AF.Ln), waits=(t5,), inc=self.cs)
        t7 = self.op("s", lambda h: h.activation(
            self.colA[:, 5:6], self.colA[:, 4:5], AF.Exp, scale=0.5),
            waits=(t6,), inc=self.cs)
        t8 = self.op("v", lambda h: h.tensor_scalar(
            self.colA[:, 6:7], self.colA[:, 0:1], self.colA[:, 5:6], -1.0,
            ALU.mult, ALU.mult), waits=(t7,), inc=self.cv)
        x1 = attnout
        t9 = self.op("s", lambda h, o=x1, i=s_: h.activation(
            o, i, AF.Identity, bias=self.colA[:, 6:7], scale=self.colA[:, 5:6]),
            waits=(t8,), inc=self.cs)
        w1 = self.dma("q", self.scr_x1tm[gq * 128:(gq + 1) * 128, :], x1,
                      self.d_x1, waits=(t9,))
        x1b = self.bf(self.ar7, 2048, c.D // 2)
        ta = self.op("v", lambda h, o=x1b, i=x1: h.tensor_copy(o, i),
                     waits=(t9, getattr(self, "_x1b_g", None)), inc=self.cv)
        evs = {}
        ngrp = (c.DC + 7) // 8
        for gj in range(ngrp):
            nt = min(8, c.DC - gj * 8)
            ev = self.transpose_group(
                [x1b[:, (gj * 8 + j) * 128:(gj * 8 + j + 1) * 128]
                 for j in range(nt)],
                self.bf(self.ar8, 1024 + gj * 512, nt * 64)
                .rearrange("p (n f) -> p n f", f=128),
                waits=(ta, getattr(self, "_x1s_g", None)))
            for j in range(nt):
                evs[gj * 8 + j] = ev
        last = None
        for d0 in range(0, c.DC, 4):
            k = min(4, c.DC - d0)
            src = self.bf(self.ar8, 1024 + d0 * 64, k * 64) \
                .rearrange("p (n f) -> p n f", f=128)
            dst = self.scr_x1fm[d0 * 128:(d0 + k) * 128,
                                gq * 128:(gq + 1) * 128] \
                .rearrange("(n p) f -> p n f", p=128)
            last = self.dma("q", dst, src, self.d_x1b, waits=(evs[d0],))
        self._x1s_g = last
        self._x1b_g = evs[(ngrp - 1) * 8]
        self._xr_g = t9
        self.ln_done[gq] = last
        return w1

    def _ffn(self):
        c = self.c
        TW = c.TWIN
        HC = c.HC
        NTC = TW // 128
        n1h = min(HC, self.N1 // (TW // 2))

        def hbuf(hc):
            if hc < n1h:
                return self.bf(self.ar1, hc * (TW // 2), TW // 2)
            return self.bf(self.ar2, 12288 + (hc - n1h) * (TW // 2), TW // 2)
        if HC > n1h:
            assert 12288 + (HC - n1h) * (TW // 2) <= self.N2
        x1w = self.bf(self.ar2, 0, c.DC * TW // 2) \
            .rearrange("p (dc t) -> p dc t", t=TW)
        ytm = [self.ar4[:, 0:c.D], self.ar4[:, 2048:2048 + c.D],
               self.ar5[:, 0:c.D], self.ar8[:, 0:c.D]][:NTC]
        h_g = [None] * HC
        ytm_g = [None] * NTC
        sg_g = [None, None]
        sg_cons = [None, None]
        sp_g = [None, None, None]
        sp_cons = [None, None, None]
        x1w_g = None
        zsb_g = [None, None]
        sq_g = [None, None]
        pend_sos = None
        sos_prev = None
        self._nrot = 2
        self._treg = 0
        self.flush()
        self.barrier()
        x1_all = (self.tok("d_x1"), self.tok("d_x1b"))
        for win in range(c.NWIN):
            tok0 = win * TW
            lnw = x1_all
            xw = None
            for d4 in range(c.DC // 4):
                xw = self.dma("q", x1w[:, d4 * 4:(d4 + 1) * 4, :],
                              bass.AP(self.scr_x1fm.tensor,
                                      self.scr_x1fm[d4 * 512:d4 * 512 + 128,
                                                    tok0:tok0 + TW].offset,
                                      [[c.NTOK, 128], [128 * c.NTOK, 4], [1, TW]]),
                              self.d_xw,
                              waits=lnw + (x1w_g,) + self.bar if d4 == 0 else ())
            for hc in range(HC):
                ssl = (hc // 2) % 2
                if hc % 2 == 0:
                    sgsem = [self.d_sg0, self.d_sg1][ssl]
                    gslab = self.bf(self.ar2, 4096 + ssl * 2048, c.DC * 128) \
                        .rearrange("p (dc f) -> p dc f", f=256)
                    uslab = self.bf(self.ar2, 8192 + ssl * 2048, c.DC * 128) \
                        .rearrange("p (dc f) -> p dc f", f=256)
                    w = (self.tok_quant, self.tok_quant1, sg_g[ssl],
                         sg_cons[ssl]) + self.bar
                    last = None
                    for four, (slab, col0) in enumerate(
                            ((gslab, hc * 128), (uslab, c.Hh + hc * 128))):
                        for dq in range(c.DC // 4):
                            last = self.dma(
                                "q", slab[:, dq * 4:(dq + 1) * 4, :],
                                bass.AP(self.scr_gu.tensor,
                                        self.scr_gu[dq * 512:dq * 512 + 128,
                                                    col0:col0 + 256].offset,
                                        [[c.H2, 128], [128 * c.H2, 4], [1, 256]]),
                                sgsem, waits=w if (four == 0 and dq == 0) else ())
                    sg_g[ssl] = last
                    cur = (gslab, uslab, last)
                gslab, uslab, slab_tok = cur
                hoff = (hc % 2) * 128
                qg, qu = 2 * (hc % 2), 2 * (hc % 2) + 1
                psg = self.PS1[:, qg * 512: qg * 512 + TW]
                psu = self.PS1[:, qu * 512: qu * 512 + TW]
                for dc in range(c.DC):
                    self.op("t", lambda h, o=psg, l=gslab[:, dc, hoff:hoff + 128],
                            r=x1w[:, dc, :], st=(dc == 0), sp=(dc == c.DC - 1):
                            h.matmul(o, l, r, start=st, stop=sp,
                                     skip_group_check=True),
                            waits=(xw, slab_tok, self.ps1q[qg]) + self.bar
                            if dc == 0 else (),
                            inc=self.ct if dc == c.DC - 1 else None)
                mg = self.tok("ct")
                for dc in range(c.DC):
                    self.op("t", lambda h, o=psu, l=uslab[:, dc, hoff:hoff + 128],
                            r=x1w[:, dc, :], st=(dc == 0), sp=(dc == c.DC - 1):
                            h.matmul(o, l, r, start=st, stop=sp,
                                     skip_group_check=True),
                            waits=(self.ps1q[qu],) if dc == 0 else (),
                            inc=self.ct if dc == c.DC - 1 else None)
                mu = self.tok("ct")
                sg_cons[ssl] = mu
                xw_cons = mu
                gsil = self.ar8[:, 2048:2048 + TW]
                t1 = self.op("s", lambda h, o=gsil, i=psg:
                             h.activation(o, i, AF.Silu),
                             waits=(mg, getattr(self, "_gsil_g", None)) + self.bar,
                             inc=self.cs)
                self.ps1q[qg] = t1
                hv = hbuf(hc)
                t2 = self.op("v", lambda h, o=hv, a=gsil, b=psu:
                             h.tensor_tensor(o, a, b, ALU.mult),
                             waits=(t1, mu, h_g[hc]) + self.bar, inc=self.cv)
                self._gsil_g = t2
                self.ps1q[qu] = t2
                h_g[hc] = t2
                sqh = self.bf(self.ar7, (hc % 2) * 256, TW // 2)
                t3 = self.op("v", lambda h, o=sqh, a=hv, b=hv:
                             h.tensor_tensor(o, a, b, ALU.mult),
                             waits=(t2, sq_g[hc % 2]), inc=self.cv)

                def sos_mm(hc=hc, t3=t3, sqh=sqh):
                    self.op("t", lambda h, o=self.PS2[0:1, 1024:1024 + TW], i=sqh,
                            st=(hc == 0), sp=(hc == HC - 1):
                            h.matmul(o, self.ones_col[:], i, start=st,
                                     stop=sp, skip_group_check=True),
                            waits=(t3,) + ((sos_prev,) if hc == 0 else ()),
                            inc=self.ct if hc == HC - 1 else None)
                    sq_g[hc % 2] = self.tok("ct")
                if pend_sos is not None:
                    pend_sos()
                pend_sos = sos_mm
            pend_sos()
            pend_sos = None
            sos_tok = self.tok("ct")
            r0 = self.op("v", lambda h: h.tensor_scalar(
                self.rowz_f[0:1, 0:TW], self.PS2[0:1, 1024:1024 + TW],
                1.0 / c.Hh, RMS_EPS, ALU.mult, ALU.add),
                waits=(sos_tok, getattr(self, "_rowz_g", None)), inc=self.cv)
            r1 = self.op("s", lambda h: h.activation(
                self.rowz_f[0:1, 0:TW], self.rowz_f[0:1, 0:TW], AF.Ln),
                waits=(r0,), inc=self.cs)
            r2 = self.op("s", lambda h: h.activation(
                self.rowz_f[0:1, 0:TW], self.rowz_f[0:1, 0:TW], AF.Exp, scale=-0.5),
                waits=(r1,), inc=self.cs)
            sos_prev = r2
            self._rowz_g = r2
            rc_t = []
            rcm_prev = getattr(self, "_rcm_g", None)
            for tcw in range(NTC):
                m = self.op("t", lambda h, o=self.PS2[:, 1536 + tcw:1537 + tcw],
                            l=self.rowz_f[0:1, tcw * 128:(tcw + 1) * 128]:
                            h.matmul(o, l, self.ones3[0:1, 0:1], start=True,
                                     stop=True, skip_group_check=True),
                            waits=(r2, rcm_prev), inc=self.ct)
                e = self.op("v", lambda h, o=self.colB[:, tcw:tcw + 1],
                            i=self.PS2[:, 1536 + tcw:1537 + tcw]:
                            h.tensor_copy(o, i), waits=(m,), inc=self.cv)
                rc_t.append(e)
            self._rcm_g = rc_t[-1]
            h_last = h_g[HC - 1]
            xw_cons_f = xw_cons
            for dc in range(c.DC):
                psl = dc % 3
                spsem = [self.d_sp0, self.d_sp1, self.d_sp2][psl]
                pslab = self.bf(self.ar2, [4096, 8192, 0][psl], HC * 64) \
                    .rearrange("p (hc f) -> p hc f", f=128)
                w0 = ((sp_g[psl], sp_cons[psl], mg, mu, xw_cons)
                      if dc < 3 else (sp_g[psl], sp_cons[psl]))
                sp_last = None
                for h8 in range(HC // 8):
                    sp_last = self.dma(
                        "q", pslab[:, h8 * 8:(h8 + 1) * 8, :],
                        bass.AP(self.scr_p.tensor,
                                self.scr_p[h8 * 1024:h8 * 1024 + 128,
                                           dc * 128:(dc + 1) * 128].offset,
                                [[c.D, 128], [128 * c.D, 8], [1, 128]]),
                        spsem, waits=w0 if h8 == 0 else ())
                sp_g[psl] = sp_last
                qz = dc % 4
                psz = self.PS1[:, qz * 512: qz * 512 + TW]
                for hc in range(HC):
                    self.op("t", lambda h, o=psz, l=pslab[:, hc, :], r=hbuf(hc),
                            st=(hc == 0), sp=(hc == HC - 1):
                            h.matmul(o, l, r, start=st, stop=sp,
                                     skip_group_check=True),
                            waits=(sp_last, h_last, self.ps1q[qz]) if hc == 0 else (),
                            inc=self.ct if hc == HC - 1 else None)
                mz = self.tok("ct")
                sp_cons[psl] = mz
                zsb = self.ar7[:, 1024 + (dc % 2) * 512:1024 + (dc % 2) * 512 + TW]
                ez = self.op("v", lambda h, o=zsb, i=psz: h.tensor_copy(o, i),
                             waits=(mz, zsb_g[dc % 2]), inc=self.cv)
                self.ps1q[qz] = ez
                trk = None
                for tcw in range(NTC):
                    r = self._treg = 1 - self._treg
                    pst = self.PS2[:, r * 512:r * 512 + 128]
                    trk = self.op("t", lambda h, o=pst,
                                  i=zsb[:, tcw * 128:(tcw + 1) * 128]:
                                  h.transpose(o, i, self.identf[:]),
                                  waits=(ez, self.trg[r]), inc=self.ct)
                    ya = self.op("s", lambda h, o=ytm[tcw][:, dc * 128:(dc + 1) * 128],
                                 i=pst, sc=self.colB[:, tcw:tcw + 1]:
                                 h.activation(o, i, AF.Identity, scale=sc),
                                 waits=(trk, rc_t[tcw], ytm_g[tcw]), inc=self.cs)
                    self.trg[r] = ya
                    if dc == c.DC - 1:
                        ytm_g[tcw] = ya
                zsb_g[dc % 2] = trk
            for tcw in range(NTC):
                gq = (tok0 // 128) + tcw
                seq = gq // (c.T // 128)
                tloc = gq % (c.T // 128)
                x1r = self.ar5[:, 2048:2048 + c.D]
                xl = self.dma("q", x1r, self.scr_x1tm[gq * 128:(gq + 1) * 128, :],
                              self.d_xr, waits=(getattr(self, "_xr_g", None),))
                s_ = self.ar6[:, 0:c.D]
                t1 = self.op("v", lambda h, o=s_, a=ytm[tcw], b=x1r:
                             h.tensor_tensor(o, a, b, ALU.add),
                             waits=(xl, ytm_g[tcw], self.st_tok[0], self.st_tok[1]),
                             inc=self.cv)
                self.st_tok = [t1, t1]
                nbk = (c.D + 511) // 512
                t2 = None
                for bi in range(nbk):
                    w = min(512, c.D - bi * 512)
                    t2 = self.op("v", lambda h, o=self.bn6[:, bi * 6:bi * 6 + 6],
                                 i=s_[:, bi * 512:bi * 512 + w]: h.bn_stats(o, i),
                                 waits=(t1,), inc=self.cv)
                t3 = self.op("v", lambda h: h.bn_aggr(self.colA[:, 0:2],
                                                      self.bn6[:, 0:6 * nbk]),
                             waits=(t2,), inc=self.cv)
                t4 = self.op("v", lambda h: h.tensor_scalar(
                    self.colA[:, 2:3], self.colA[:, 1:2], LN_EPS, None, ALU.add),
                    waits=(t3,), inc=self.cv)
                t5 = self.op("v", lambda h: h.reciprocal(
                    self.colA[:, 3:4], self.colA[:, 2:3]), waits=(t4,), inc=self.cv)
                t6 = self.op("s", lambda h: h.activation(
                    self.colA[:, 4:5], self.colA[:, 3:4], AF.Ln),
                    waits=(t5,), inc=self.cs)
                t7 = self.op("s", lambda h: h.activation(
                    self.colA[:, 5:6], self.colA[:, 4:5], AF.Exp, scale=0.5),
                    waits=(t6,), inc=self.cs)
                t8 = self.op("v", lambda h: h.tensor_scalar(
                    self.colA[:, 6:7], self.colA[:, 0:1], self.colA[:, 5:6], -1.0,
                    ALU.mult, ALU.mult), waits=(t7,), inc=self.cv)
                t9 = self.op("s", lambda h, o=ytm[tcw], i=s_: h.activation(
                    o, i, AF.Identity, bias=self.colA[:, 6:7],
                    scale=self.colA[:, 5:6]), waits=(t8,), inc=self.cs)
                ytm_g[tcw] = self.dma(
                    "q", self.out[seq, tloc * 128:(tloc + 1) * 128, :],
                    ytm[tcw], self.d_out, waits=(t9,))
                self._xr_g = t1
            x1w_g = [self.tok("ct"), sp_g[2]]

    def _finish(self):
        waits = tuple(self.tok(nm) for nm in self.cnt if nm.startswith("d_"))
        self.op("q", lambda h: h.nop(), waits=waits)


_BUILT = {}


def _get_nc(cfg_key="full"):
    if cfg_key not in _BUILT:
        c = Cfg()
        b = Builder(c)
        nc = b.build()
        _BUILT[cfg_key] = (c, nc)
    return _BUILT[cfg_key]


def kernel(**inputs):
    c, nc = _get_nc()
    consts = host_consts(c)
    x = np.asarray(inputs["L0_attn_x"])
    base = {
        "Wq": np.ascontiguousarray(inputs["L0_attn_Wq"], np.float32),
        "bq": np.ascontiguousarray(inputs["L0_attn_bq"], np.float32),
        "Wk": np.ascontiguousarray(inputs["L0_attn_Wk"], np.float32),
        "bk": np.ascontiguousarray(inputs["L0_attn_bk"], np.float32),
        "Wv": np.ascontiguousarray(inputs["L0_attn_Wv"], np.float32),
        "bv": np.ascontiguousarray(inputs["L0_attn_bv"], np.float32),
        "Wo": np.ascontiguousarray(inputs["L0_attn_Wo"], np.float32),
        "bo": np.ascontiguousarray(inputs["L0_attn_bo"], np.float32),
        "Wgu": np.ascontiguousarray(inputs["L0_ffn_w_gate_up"], np.float32),
        "Wp": np.ascontiguousarray(inputs["L0_ffn_w_proj"], np.float32),
    }
    base.update({"identb_in": consts["ident"], "identf_in": consts["identf"],
                 "onescol_in": consts["ones_col"], "gmat_in": consts["gmat"],
                 "cos_tab": consts["cos_tab"], "sins_tab": consts["sins_tab"]})
    in_maps = []
    for core in range(NCORES):
        m = dict(base)
        m["x"] = np.ascontiguousarray(
            x[core * c.SEQS:(core + 1) * c.SEQS], np.float32)
        in_maps.append(m)
    res = run_bass_kernel_spmd(nc, in_maps, core_ids=list(range(NCORES)))
    return np.concatenate([r["out"] for r in res.results], axis=0)
